# version 04
# baseline (speedup 1.0000x reference)
import contextlib

import numpy as np
import ml_dtypes
import concourse.bass as bass
import concourse.mybir as mybir
from concourse.bass_utils import run_bass_kernel_spmd

F32 = mybir.dt.float32
BF16 = mybir.dt.bfloat16
AF = mybir.ActivationFunctionType
ALU = mybir.AluOpType
AX = mybir.AxisListType

QT = 512
NCHUNK = 8
NQUARTER = 4
DEBUG_B = False
DEBUG_A = False
NEXPERT = 32
SKIP_WR = False
SKIP_MOE = False
SKIP_DOWN = False
SKIP_SHARED = False
SKIP_ROUTER = False
SKIP_TRANS = False
SKIP_WSCR = False
ENGINES = ("sync", "gpsimd", "tensor", "vector", "scalar")


class Rec:

    def __init__(self, prog, name):
        self._prog = prog
        self._name = name

    def __getattr__(self, meth):
        prog = self._prog[self._name]

        def call(*a, **k):
            entry = [meth, a, k, []]
            prog.append(entry)

            class _C:
                def then_inc(self, sem, v):
                    entry[3].append((sem, v))
                    return self

            return _C()

        return call


def _replay(eng, entries):
    for meth, a, k, incs in entries:
        r = getattr(eng, meth)(*a, **k)
        for sem, v in incs:
            r = r.then_inc(sem, v)


def _emit_blocks(nc, prog):
    with nc.Block() as block:
        @block.sync
        def _(eng):
            _replay(eng, prog["sync"])

        @block.gpsimd
        def _(eng):
            _replay(eng, prog["gpsimd"])

        @block.tensor
        def _(eng):
            _replay(eng, prog["tensor"])

        @block.vector
        def _(eng):
            _replay(eng, prog["vector"])

        @block.scalar
        def _(eng):
            _replay(eng, prog["scalar"])



def build_A(shared=None):
    if shared is None:
        nc = bass.Bass("TRN2", target_bir_lowering=False,
                       detect_race_conditions=False)
    else:
        nc = shared["nc"]

    x_fm = nc.dram_tensor("x_fm", [1024, 4096], F32, kind="ExternalInput").ap()
    w_in_t = nc.dram_tensor("w_in_t", [1024, 3072], F32, kind="ExternalInput").ap()
    conv_w = nc.dram_tensor("conv_w", [128, 64], F32, kind="ExternalInput").ap()
    conv_b = nc.dram_tensor("conv_b", [128, 16], F32, kind="ExternalInput").ap()
    xproj_t = nc.dram_tensor("xproj_t", [128, 1536], F32, kind="ExternalInput").ap()
    dtw_t = nc.dram_tensor("dtw_t", [64, 1024], F32, kind="ExternalInput").ap()
    dt_b = nc.dram_tensor("dt_b", [128, 8], F32, kind="ExternalInput").ap()
    a_log = nc.dram_tensor("a_log", [128, 128], F32, kind="ExternalInput").ap()
    d_h = nc.dram_tensor("d_h", [128, 8], F32, kind="ExternalInput").ap()
    outw_t = nc.dram_tensor("outw_t", [1024, 1024], F32, kind="ExternalInput").ap()

    if shared is None:
        partial = nc.dram_tensor("partial", [4096, 1024], F32,
                                 kind="ExternalOutput").ap()
        cast_jobs = []
    else:
        partial = shared["cc_in"]
        cast_jobs = shared["cast_jobs"]
    dbg = {}
    if DEBUG_A:
        for nm, shp in (("dbg_u0", [128, 512]), ("dbg_sz0", [128, 512]),
                        ("dbg_dtr", [64, 512]), ("dbg_bcr", [32, 512]),
                        ("dbg_delta", [128, 512]), ("dbg_du", [128, 512]),
                        ("dbg_brep", [128, 512]), ("dbg_dA", [128, 512]),
                        ("dbg_hb", [128, 4096]), ("dbg_y", [128, 512]),
                        ("dbg_gated", [128, 512])):
            dbg[nm] = nc.dram_tensor(nm, shp, F32, kind="ExternalOutput").ap()
    bct_h = nc.dram_tensor("bct", [32, QT], BF16)
    bct = bct_h.ap()

    if shared is None:
        prog = {e: [] for e in ENGINES}
    else:
        prog = shared["prog"]
    SY, GP, T, V, S = (Rec(prog, e) for e in ENGINES)

    es = contextlib.ExitStack()
    sem_es = shared["es"] if shared is not None else es
    with es:
        def sb(name, shape, dt):
            return es.enter_context(nc.sbuf_tensor(name, shape, dt))

        cw_all = sb("cw_all", [128, 64], F32)
        cb_all = sb("cb_all", [128, 16], F32)
        dtb_all = sb("dtb_all", [128, 8], F32)
        dh_all = sb("dh_all", [128, 8], F32)
        aneg = sb("aneg", [128, 128], F32)
        hstate = sb("hstate", [128, 128], F32)
        halo = sb("halo", [128, 48], BF16)
        winbf = [sb(f"winbf{k}", [128, 3072], BF16) for k in range(8)]
        outwbf = [sb(f"outwbf{k}", [128, 1024], BF16) for k in range(8)]
        xprojbf = sb("xprojbf", [128, 1536], BF16)
        dtwbf = sb("dtwbf", [64, 1024], BF16)

        xf32 = [sb(f"xf32_{i}", [128, QT], F32) for i in range(2)]
        xbf = [sb(f"xbf{k}", [128, QT], BF16) for k in range(8)]
        xitmp = [sb(f"xitmp{i}", [128, QT + 3], BF16) for i in range(2)]
        cacc = [sb(f"cacc{i}", [128, QT], F32) for i in range(2)]
        u = [sb(f"u{i}", [128, QT], BF16) for i in range(8)]
        utmp = [sb(f"utmp{i}", [128, QT], BF16) for i in range(2)]
        sz = [sb(f"sz{i}", [128, QT], BF16) for i in range(8)]
        dtr_bf = sb("dtr_bf", [64, QT], BF16)
        bcr_bf = sb("bcr_bf", [32, QT], BF16)
        brep = sb("brep", [128, 16 * QT], BF16)
        crep = sb("crep", [128, 16 * QT], BF16)
        spe = sb("spe", [128, QT], F32)
        delta = sb("delta", [128, QT], F32)
        du_bf = sb("du_bf", [128, QT], BF16)
        ebuf = [sb(f"ebuf{i}", [128, 2 * QT], F32) for i in range(2)]
        dAq = [sb(f"dAq{i}", [128, 2 * QT], F32) for i in range(2)]
        p_bf = sb("p_bf", [128, 4 * QT], BF16)
        hb = sb("hb", [128, 8 * QT], BF16)
        t2 = sb("t2", [128, 8 * QT], BF16)
        ytmp = sb("ytmp", [128, QT], BF16)
        y_f32 = sb("y_f32", [128, QT], F32)
        yg = sb("yg", [128, QT], F32)
        gated = [sb(f"gated{i}", [128, QT], BF16) for i in range(8)]
        oevac = [sb(f"oevac{i}", [128, QT], F32) for i in range(2)]

        pin = [es.enter_context(nc.psum_tensor(f"pin{i}", [128, QT], F32))
               for i in range(2)]
        px = es.enter_context(nc.psum_tensor("px", [96, QT], F32))
        pdt = es.enter_context(nc.psum_tensor("pdt", [128, QT], F32))
        pout = [es.enter_context(nc.psum_tensor(f"pout{i}", [128, QT], F32))
                for i in range(2)]

        sem = {}
        c = {}
        for name in ("LX0", "LX1", "LW", "LB", "LC", "STB", "STP0", "STP1",
                     "DBG"):
            sem[name] = sem_es.enter_context(nc.semaphore("sA" + name))
            c[name] = 0
        sMM = sem_es.enter_context(nc.semaphore("sAMM"))
        sVV = sem_es.enter_context(nc.semaphore("sAVV"))
        sSS = sem_es.enter_context(nc.semaphore("sASS"))
        c.update(mm=0, vv=0, ss=0)

        def dma(eng, key, out, in_):
            eng.dma_start(out=out, in_=in_).then_inc(sem[key], 16)
            c[key] += 16
            return (sem[key], c[key])

        def incM():
            prog["tensor"][-1][3].append((sMM, 1))
            c["mm"] += 1

        def incV():
            prog["vector"][-1][3].append((sVV, 1))
            c["vv"] += 1

        def incS():
            prog["scalar"][-1][3].append((sSS, 1))
            c["ss"] += 1

        def gV():
            if not prog["vector"][-1][3]:
                incV()
            V.wait_ge(sVV, c["vv"])

        def dV():
            V.drain()
            V.sem_inc(sVV, 1)
            c["vv"] += 1

        def dS():
            S.drain()
            S.sem_inc(sSS, 1)
            c["ss"] += 1

        def gS():
            if not prog["scalar"][-1][3]:
                incS()
            S.wait_ge(sSS, c["ss"])

        def wtok(Rr, tok):
            if tok is not None:
                Rr.wait_ge(tok[0], tok[1])

        dma(SY, "LW", cw_all[:, :], conv_w)
        dma(SY, "LW", cb_all[:, :], conv_b)
        dma(SY, "LW", dtb_all[:, :], dt_b)
        dma(SY, "LW", dh_all[:, :], d_h)
        tok = dma(SY, "LW", cacc[0][:, 0:128], a_log)
        S.wait_ge(*tok)
        S.activation(out=aneg[:, :], in_=cacc[0][:, 0:128], func=AF.Exp)
        incS()
        dS()
        V.wait_ge(sSS, c["ss"])
        V.memset(hstate[:, :], 0.0)
        V.memset(halo[:, :], 0.0)
        V.tensor_scalar_mul(aneg[:, :], aneg[:, :], -1.0)
        incV()
        aneg_done = c["vv"]

        dma(GP, "STB", xprojbf[:, :], xproj_t)
        dma(GP, "STB", dtwbf[:, :], dtw_t)
        for k in range(8):
            dma(GP, "STB", winbf[k][:, :], w_in_t[k * 128:(k + 1) * 128, :])
        for k in range(8):
            dma(GP, "STB", outwbf[k][:, :], outw_t[k * 128:(k + 1) * 128, :])
        wtok_a = (sem["STB"], c["STB"])

        pin_free = [None, None]
        pout_free = [None, None]
        oevac_free = [None, None]
        utmp_free = [None, None]
        pdt_free = None
        prev_gated_vv = [None] * 8
        prev_scan_all_vv = None
        prev_inproj_mm = None
        prev_dtproj_mm = None
        cast_done = [0] * 8

        for ch in range(NCHUNK):
            col0 = ch * QT
            for k in range(8):
                if k >= 2 or ch > 0:
                    SY.wait_ge(sVV, cast_done[(k - 2) % 8])
                tok = dma(SY, "LX%d" % (k % 2), xf32[k % 2][:, :],
                          x_fm[k * 128:(k + 1) * 128, col0:col0 + QT])
                V.wait_ge(*tok)
                if ch > 0 and k == 0:
                    V.wait_ge(sMM, prev_inproj_mm)
                V.tensor_copy(xbf[k][:, :], xf32[k % 2][:, :])
                incV()
                cast_done[k] = c["vv"]
            all_cast = c["vv"]

            for mt in range(24):
                pi = mt % 2
                wtok(T, pin_free[pi])
                if mt == 0:
                    T.wait_ge(sVV, all_cast)
                    if ch == 0:
                        T.wait_ge(*wtok_a)
                for k in range(8):
                    T.matmul(pin[pi][:, :],
                             winbf[k][:, mt * 128:(mt + 1) * 128],
                             xbf[k][:, :], start=(k == 0), stop=(k == 7))
                incM()
                in_mm = c["mm"]
                if mt < 16:
                    V.wait_ge(sMM, in_mm)
                    V.tensor_copy(xitmp[pi][:, 0:3],
                                  halo[:, mt * 3:mt * 3 + 3])
                    V.tensor_copy(xitmp[pi][:, 3:3 + QT], pin[pi][:, :])
                    gV()
                    V.tensor_copy(halo[:, mt * 3:mt * 3 + 3],
                                  xitmp[pi][:, QT:QT + 3])
                    V.tensor_scalar_mul(cacc[0][:, :], xitmp[pi][:, 0:QT],
                                        cw_all[:, mt * 4:mt * 4 + 1])
                    gV()
                    V.scalar_tensor_tensor(
                        out=cacc[1][:, :], in0=xitmp[pi][:, 1:1 + QT],
                        scalar=cw_all[:, mt * 4 + 1:mt * 4 + 2],
                        in1=cacc[0][:, :], op0=ALU.mult, op1=ALU.add)
                    gV()
                    V.scalar_tensor_tensor(
                        out=cacc[0][:, :], in0=xitmp[pi][:, 2:2 + QT],
                        scalar=cw_all[:, mt * 4 + 2:mt * 4 + 3],
                        in1=cacc[1][:, :], op0=ALU.mult, op1=ALU.add)
                    gV()
                    V.scalar_tensor_tensor(
                        out=cacc[1][:, :], in0=xitmp[pi][:, 3:3 + QT],
                        scalar=cw_all[:, mt * 4 + 3:mt * 4 + 4],
                        in1=cacc[0][:, :], op0=ALU.mult, op1=ALU.add)
                    incV()
                    pin_free[pi] = (sVV, c["vv"])
                    conv_vv = c["vv"]
                    if mt < 8:
                        udst = u[mt]
                        if ch > 0:
                            S.wait_ge(sVV, prev_gated_vv[mt])
                    else:
                        udst = utmp[mt % 2]
                        wtok(S, utmp_free[mt % 2])
                    S.wait_ge(sVV, conv_vv)
                    S.activation(out=udst[:, :], in_=cacc[1][:, :],
                                 func=AF.Silu, bias=cb_all[:, mt:mt + 1])
                    incS()
                    dS()
                    T.wait_ge(sSS, c["ss"])
                    T.matmul(px[:, :], xprojbf[:, mt * 96:(mt + 1) * 96],
                             udst[:, :], start=(mt == 0), stop=(mt == 15))
                    incM()
                    if mt >= 8:
                        utmp_free[mt % 2] = (sMM, c["mm"])
                else:
                    dt = mt - 16
                    if ch > 0:
                        S.wait_ge(sVV, prev_gated_vv[dt])
                    S.wait_ge(sMM, in_mm)
                    S.activation(out=sz[dt][:, :], in_=pin[pi][:, :],
                                 func=AF.Silu)
                    incS()
                    dS()
                    pin_free[pi] = (sSS, c["ss"])
            xproj_mm = c["mm"]

            V.wait_ge(sMM, xproj_mm)
            if ch > 0:
                V.wait_ge(sMM, prev_dtproj_mm)
            V.tensor_copy(dtr_bf[:, :], px[0:64, :])
            V.tensor_copy(bcr_bf[:, :], px[64:96, :])
            incV()
            dV()
            bc_vv = c["vv"]
            GP.wait_ge(sVV, bc_vv)
            if ch > 0:
                GP.wait_ge(sem["LB"], c["LB"])
                GP.wait_ge(sem["LC"], c["LC"])
            bct_tok = dma(GP, "STB", bct[:, :], bcr_bf[:, :])
            SY.wait_ge(*bct_tok)
            if prev_scan_all_vv is not None:
                SY.wait_ge(sVV, prev_scan_all_vv)
                GP.wait_ge(sVV, prev_scan_all_vv)
            for n in range(16):
                dma(SY, "LB", brep[:, n * QT:(n + 1) * QT],
                    bct[n:n + 1, :].broadcast_to([128, QT]))
                dma(GP, "LC", crep[:, n * QT:(n + 1) * QT],
                    bct[16 + n:17 + n, :].broadcast_to([128, QT]))
            rep_b = (sem["LB"], c["LB"])
            rep_c = (sem["LC"], c["LC"])

            if DEBUG_A and ch == 0:
                V.wait_ge(sSS, c["ss"])
                V.wait_ge(*rep_b)
                V.wait_ge(*rep_c)
                incV()
                dV()
                dS()
                GP.wait_ge(sVV, c["vv"])
                GP.wait_ge(sSS, c["ss"])
                dma(GP, "DBG", dbg["dbg_u0"], u[0][:, :])
                dma(GP, "DBG", dbg["dbg_sz0"], sz[0][:, :])
                dma(GP, "DBG", dbg["dbg_dtr"], dtr_bf[:, :])
                dma(GP, "DBG", dbg["dbg_bcr"], bcr_bf[:, :])
                dma(GP, "DBG", dbg["dbg_brep"], brep[:, 0:512])
                dbg_tok = (sem["DBG"], c["DBG"])
                V.wait_ge(*dbg_tok)
                S.wait_ge(*dbg_tok)
            for dt in range(8):
                wtok(T, pdt_free)
                if dt == 0:
                    T.wait_ge(sVV, bc_vv)
                T.matmul(pdt[:, :], dtwbf[:, dt * 128:(dt + 1) * 128],
                         dtr_bf[:, :], start=True, stop=True)
                incM()
                dtmm = c["mm"]
                S.wait_ge(sMM, dtmm)
                S.activation(out=spe[:, :], in_=pdt[:, :], func=AF.Exp,
                             bias=dtb_all[:, dt:dt + 1])
                gS()
                S.activation(out=delta[:, :], in_=spe[:, :], func=AF.Ln,
                             bias=1.0)
                if ch == 0 and dt == 0:
                    S.wait_ge(sVV, aneg_done)
                incS()
                dS()
                pdt_free = (sSS, c["ss"])
                sp_ss = c["ss"]
                V.wait_ge(sSS, sp_ss)
                V.tensor_mul(du_bf[:, :], delta[:, :], u[dt][:, :])
                gV()
                if DEBUG_A and ch == 0 and dt == 0:
                    dV()
                    dS()
                    GP.wait_ge(sVV, c["vv"])
                    GP.wait_ge(sSS, c["ss"])
                    dma(GP, "DBG", dbg["dbg_delta"], delta[:, :])
                    dma(GP, "DBG", dbg["dbg_du"], du_bf[:, :])
                    V.wait_ge(sem["DBG"], c["DBG"])
                    S.wait_ge(sem["DBG"], c["DBG"])
                if dt == 0:
                    V.wait_ge(*rep_b)
                    V.wait_ge(*rep_c)
                scan_vv = {}
                e_vv = {}
                exp_ss = {}
                for n in range(16):
                    nb, half = n % 8, n // 8
                    nq4 = n % 4
                    g, jj = n // 2, n % 2
                    if nq4 == 0:
                        for n2 in range(n, n + 4):
                            V.tensor_mul(p_bf[:, (n2 % 4) * QT:
                                               (n2 % 4 + 1) * QT],
                                         du_bf[:, :],
                                         brep[:, n2 * QT:(n2 + 1) * QT])
                        incV()
                    if jj == 0:
                        if g >= 2:
                            V.wait_ge(sSS, exp_ss[g - 2])
                        for j2 in range(2):
                            V.tensor_scalar_mul(
                                ebuf[g % 2][:, j2 * QT:(j2 + 1) * QT],
                                delta[:, :],
                                aneg[:, dt * 16 + n + j2:
                                     dt * 16 + n + j2 + 1])
                        incV()
                        e_vv[g] = c["vv"]
                        if g >= 2:
                            S.wait_ge(sVV, scan_vv[2 * g - 3])
                        S.wait_ge(sVV, e_vv[g])
                        S.activation(out=dAq[g % 2][:, :],
                                     in_=ebuf[g % 2][:, :], func=AF.Exp)
                        incS()
                        dS()
                        exp_ss[g] = c["ss"]
                    V.wait_ge(sSS, exp_ss[g])
                    V.tensor_tensor_scan(
                        out=hb[:, nb * QT:(nb + 1) * QT],
                        data0=dAq[g % 2][:, jj * QT:(jj + 1) * QT],
                        data1=p_bf[:, nq4 * QT:(nq4 + 1) * QT],
                        initial=(0.0 if ch == 0 else
                                 hstate[:, dt * 16 + n:dt * 16 + n + 1]),
                        op0=ALU.mult, op1=ALU.add)
                    incV()
                    scan_vv[n] = c["vv"]
                    if nb == 7:
                        gV()
                        hb3 = hb[:, :].rearrange("p (n t) -> p n t", n=8)
                        V.tensor_copy(
                            hstate[:, dt * 16 + half * 8:
                                   dt * 16 + half * 8 + 8],
                            hb3[:, :, QT - 1])
                        V.tensor_mul(t2[:, :], hb[:, :],
                                     crep[:, half * 8 * QT:(half + 1) * 8 * QT])
                        gV()
                        V.tensor_add(t2[:, 0:4 * QT], t2[:, 0:4 * QT],
                                     t2[:, 4 * QT:8 * QT])
                        gV()
                        V.tensor_add(t2[:, 0:2 * QT], t2[:, 0:2 * QT],
                                     t2[:, 2 * QT:4 * QT])
                        gV()
                        if half == 0:
                            V.tensor_add(y_f32[:, :], t2[:, 0:QT],
                                         t2[:, QT:2 * QT])
                        else:
                            V.tensor_add(ytmp[:, :], t2[:, 0:QT],
                                         t2[:, QT:2 * QT])
                            gV()
                            V.tensor_add(y_f32[:, :], y_f32[:, :], ytmp[:, :])
                        incV()
                prev_scan_all_vv = c["vv"]
                V.wait_ge(sVV, c["vv"])
                if DEBUG_A and ch == 0 and dt == 0:
                    dV()
                    GP.wait_ge(sVV, c["vv"])
                    dma(GP, "DBG", dbg["dbg_hb"], hb[:, :])
                    dma(GP, "DBG", dbg["dbg_y"], y_f32[:, :])
                    V.wait_ge(sem["DBG"], c["DBG"])
                V.scalar_tensor_tensor(
                    out=yg[:, :], in0=u[dt][:, :],
                    scalar=dh_all[:, dt:dt + 1], in1=y_f32[:, :],
                    op0=ALU.mult, op1=ALU.add)
                gV()
                V.tensor_mul(gated[dt][:, :], yg[:, :], sz[dt][:, :])
                incV()
                prev_gated_vv[dt] = c["vv"]
                if DEBUG_A and ch == 0 and dt == 0:
                    dV()
                    GP.wait_ge(sVV, c["vv"])
                    dma(GP, "DBG", dbg["dbg_gated"], gated[dt][:, :])
                    V.wait_ge(sem["DBG"], c["DBG"])
            prev_dtproj_mm = c["mm"]
            prev_inproj_mm = xproj_mm
            all_gated = c["vv"]
            for _ in range(6):
                if cast_jobs:
                    dst_, src_ = cast_jobs.pop(0)
                    shared["scr_tok"] = dma(GP, "STB", dst_, src_)

            for tt in range(4):
                for fh in range(2):
                    i = (tt * 2 + fh) % 2
                    wtok(T, pout_free[i])
                    if tt == 0 and fh == 0:
                        T.wait_ge(sVV, all_gated)
                    for k in range(8):
                        T.matmul(pout[i][:, :],
                                 gated[k][:, tt * 128:(tt + 1) * 128],
                                 outwbf[k][:, fh * QT:(fh + 1) * QT],
                                 start=(k == 0), stop=(k == 7))
                    incM()
                    omm = c["mm"]
                    wtok(S, oevac_free[i])
                    S.wait_ge(sMM, omm)
                    S.copy(oevac[i][:, :], pout[i][:, :])
                    incS()
                    pout_free[i] = (sSS, c["ss"])
                    dS()
                    GP.wait_ge(sSS, c["ss"])
                    oevac_free[i] = dma(
                        GP, "STP%d" % i,
                        partial[col0 + tt * 128:col0 + (tt + 1) * 128,
                                fh * QT:(fh + 1) * QT],
                        oevac[i][:, :])

        GP.wait_ge(sem["STP0"], c["STP0"])
        GP.wait_ge(sem["STP1"], c["STP1"])
        if shared is None:
            _emit_blocks(nc, prog)
    return nc



def build_B(shared=None):
    if shared is None:
        nc = bass.Bass("TRN2", target_bir_lowering=False,
                       detect_race_conditions=False)
        pa = nc.dram_tensor("pa", [2048, 1024], F32,
                            kind="ExternalInput").ap()
        pb = nc.dram_tensor("pb", [2048, 1024], F32,
                            kind="ExternalInput").ap()
    else:
        nc = shared["nc"]
        pa = shared["cc_out"]
        pb = None
    x_res = nc.dram_tensor("x_res", [2048, 1024], F32, kind="ExternalInput").ap()
    router_t = nc.dram_tensor("router_t", [128, 256], F32,
                              kind="ExternalInput").ap()
    rbias = nc.dram_tensor("rbias", [1, 32], F32, kind="ExternalInput").ap()
    if shared is not None:
        fd = shared["ffn_dram"]
        gate_t, up_t, down_f = fd["gate_t"], fd["up_t"], fd["down_f"]
    else:
        gate_t = nc.dram_tensor("gate_t", [1024, 4096], F32,
                                kind="ExternalInput").ap()
        up_t = nc.dram_tensor("up_t", [1024, 4096], F32,
                              kind="ExternalInput").ap()
        down_f = nc.dram_tensor("down_f", [4096, 1024], F32,
                                kind="ExternalInput").ap()
    sgate_t = nc.dram_tensor("sgate_t", [128, 4096], F32,
                             kind="ExternalInput").ap()
    sup_t = nc.dram_tensor("sup_t", [128, 4096], F32, kind="ExternalInput").ap()
    sdown_t = nc.dram_tensor("sdown_t", [128, 4096], F32,
                             kind="ExternalInput").ap()
    ident = nc.dram_tensor("ident", [128, 128], BF16, kind="ExternalInput").ap()

    out = nc.dram_tensor("out", [2048, 1024], F32, kind="ExternalOutput").ap()

    dbg = {}
    if DEBUG_B:
        for nm, shp in (("dbg_slw", [32, 128]), ("dbg_wbf", [128, 32]),
                        ("dbg_wt", [32, 512]), ("dbg_wr", [128, 512]),
                        ("dbg_ge", [128, 1024]), ("dbg_ub", [128, 512]),
                        ("dbg_xn", [512, 1024]), ("dbg_wsel", [512, 32]),
                        ("dbg_xnT", [128, 512]), ("dbg_h1", [128, 512]),
                        ("dbg_hs", [128, 512]), ("dbg_sg", [128, 512]),
                        ("dbg_res2", [128, 1024])):
            dbg[nm] = nc.dram_tensor(nm, shp, F32, kind="ExternalOutput").ap()
    if shared is not None:
        gbs, ubs, dbs = fd["gbs"], fd["ubs"], fd["dbs"]
    else:
        gbs = nc.dram_tensor("gbs", [1024, 4096], BF16).ap()
        ubs = nc.dram_tensor("ubs", [1024, 4096], BF16).ap()
        dbs = nc.dram_tensor("dbs", [4096, 1024], BF16).ap()
    wscr = nc.dram_tensor("wscr", [32, 512], BF16).ap()

    if shared is None:
        prog = {e: [] for e in ENGINES}
    else:
        prog = shared["prog"]
    SY, GP, T, V, S = (Rec(prog, e) for e in ENGINES)

    es = contextlib.ExitStack()
    sem_es = shared["es"] if shared is not None else es
    with es:
        def sb(name, shape, dt):
            return es.enter_context(nc.sbuf_tensor(name, shape, dt))

        identb = sb("identb", [128, 128], BF16)
        routerbf = sb("routerbf", [128, 256], BF16)
        rbrep = sb("rbrep", [128, 32], F32)
        sgatebf = sb("sgatebf", [128, 4096], BF16)
        supbf = sb("supbf", [128, 4096], BF16)
        sdownbf = sb("sdownbf", [128, 4096], BF16)

        s_pa = sb("s_pa", [128, 1024], F32)
        s_pb = sb("s_pb", [128, 1024], F32)
        s_xr = sb("s_xr", [128, 1024], F32)
        res = sb("res", [128, 1024], F32)
        xc = sb("xc", [128, 1024], F32)
        sqd = sb("sqd", [128, 1024], F32)
        st1 = sb("st1", [128, 1], F32)
        st2 = sb("st2", [128, 1], F32)
        epsb = sb("epsb", [128, 1], F32)
        halfneg = sb("halfneg", [128, 1], F32)
        xn_f32 = [sb(f"xn{i}", [128, 1024], F32) for i in range(4)]
        xn_bf = sb("xn_bf", [128, 1024], BF16)
        xnT = [sb(f"xnT{k}", [128, 512], BF16) for k in range(8)]
        sco = sb("sco", [128, 32], F32)
        scow = sb("scow", [128, 32], F32)
        msk = sb("msk", [128, 32], F32)
        ind = sb("ind", [128, 32], F32)
        wsel_bf = sb("wsel_bf", [128, 128], BF16)
        wselT = sb("wselT", [128, 512], BF16)
        ge = [sb(f"ge{i}", [128, 1024], BF16) for i in range(2)]
        ue = [sb(f"ue{i}", [128, 1024], BF16) for i in range(2)]
        sg2 = [sb(f"sg{i}", [128, 512], BF16) for i in range(2)]
        ub2 = [sb(f"ub{i}", [128, 512], BF16) for i in range(2)]
        wr = [sb(f"wr{i}", [128, 512], BF16) for i in range(2)]
        h1 = [sb(f"h1_{k}", [128, 512], BF16) for k in range(32)]
        dbf = [sb(f"dbf{i}", [128, 512], BF16) for i in range(2)]
        hs = [sb(f"hs{i}", [128, 512], BF16) for i in range(4)]
        hsg = sb("hsg", [128, 512], BF16)
        res2 = [sb(f"res2_{i}", [128, 1024], F32) for i in range(4)]
        if DEBUG_B:
            dbsb = sb("dbsb", [128, 1024], F32)
            dbw = sb("dbw", [128, 128], F32)
            dbw2 = sb("dbw2", [128, 512], F32)
            dbu = sb("dbu", [128, 512], F32)
        outb = [sb(f"outb{i}", [128, 1024], F32) for i in range(2)]

        ppAs = [es.enter_context(nc.psum_tensor(f"ppA{i}", [128, 512], F32))
                for i in range(2)]
        ppBs = [es.enter_context(nc.psum_tensor(f"ppB{i}", [128, 512], F32))
                for i in range(2)]
        ppA, ppB = ppAs[0], ppBs[0]
        ppd = [es.enter_context(nc.psum_tensor(f"ppd{i}", [128, 512], F32))
               for i in range(4)]

        sem = {}
        c = {}
        for name in ("LP", "LW", "LG0", "LG1", "LF0", "LF1",
                     "STW", "STS", "STO0", "STO1", "LT", "DBG"):
            sem[name] = sem_es.enter_context(nc.semaphore("sB" + name))
            c[name] = 0
        sMM = sem_es.enter_context(nc.semaphore("sBMM"))
        sVV = sem_es.enter_context(nc.semaphore("sBVV"))
        sSS = sem_es.enter_context(nc.semaphore("sBSS"))
        c.update(mm=0, vv=0, ss=0)

        def dma(eng, key, out_, in_):
            eng.dma_start(out=out_, in_=in_).then_inc(sem[key], 16)
            c[key] += 16
            return (sem[key], c[key])

        def dma_t(eng, key, out_, in_):
            eng.dma_start_transpose(out=out_, in_=in_).then_inc(sem[key], 16)
            c[key] += 16
            return (sem[key], c[key])

        def dma_t(eng, key, out_, in_):
            eng.dma_start_transpose(out=out_, in_=in_).then_inc(sem[key], 16)
            c[key] += 16
            return (sem[key], c[key])

        def incM():
            prog["tensor"][-1][3].append((sMM, 1))
            c["mm"] += 1

        def incV():
            prog["vector"][-1][3].append((sVV, 1))
            c["vv"] += 1

        def incS():
            prog["scalar"][-1][3].append((sSS, 1))
            c["ss"] += 1

        def gV():
            if not prog["vector"][-1][3]:
                incV()
            V.wait_ge(sVV, c["vv"])

        def dV():
            V.drain()
            V.sem_inc(sVV, 1)
            c["vv"] += 1

        def dS():
            S.drain()
            S.sem_inc(sSS, 1)
            c["ss"] += 1

        def gS():
            if not prog["scalar"][-1][3]:
                incS()
            S.wait_ge(sSS, c["ss"])

        def wtok(Rr, tok):
            if tok is not None:
                Rr.wait_ge(tok[0], tok[1])

        V.memset(epsb[:, :], 1e-5)
        V.memset(halfneg[:, :], -0.5)
        V.memset(wsel_bf[:, :], 0.0)
        incV()
        dma(SY, "LW", identb[:, :], ident)
        dma(SY, "LW", rbrep[:, :], rbias[0:1, :].broadcast_to([128, 32]))
        dma(GP, "STW", routerbf[:, :], router_t)
        dma(GP, "STW", sgatebf[:, :], sgate_t)
        dma(GP, "STW", supbf[:, :], sup_t)
        wcast_tok = dma(GP, "STW", sdownbf[:, :], sdown_t)
        if shared is not None:
            wcast_tok = shared["scr_tok"]
        else:
            for (src, dsc, ntile) in ((gate_t, gbs, 8), (up_t, ubs, 8),
                                      (down_f, dbs, 32)):
                for it in range(ntile):
                    wcast_tok = dma(GP, "STW",
                                    dsc[it * 128:(it + 1) * 128, :],
                                    src[it * 128:(it + 1) * 128, :])

        if shared is not None:
            ccs = shared["ccsem"]
            for Rr in (SY, GP, T, V, S):
                Rr.wait_ge(ccs, 1)
        ppA_frees = [None, None]
        ppB_frees = [None, None]
        xnt_read_tok = None
        ppd_free = [None] * 4
        ge_free = [None, None]
        ue_free = [None, None]
        wr_free = [None, None]
        dbf_free = [None, None]
        outb_free = [None, None]
        xnT_prev_mm = None
        prev_h1_vv = None

        spa_free = None
        wselT_read_tok = None
        for q in range(NQUARTER):
            row0 = q * 512
            wsel_tt_tok = None
            for tt in range(4):
                r0 = row0 + tt * 128
                wtok(SY, spa_free)
                dma(SY, "LP", s_pa[:, :], pa[r0:r0 + 128, :])
                if pb is not None:
                    dma(SY, "LP", s_pb[:, :], pb[r0:r0 + 128, :])
                tok = dma(SY, "LP", s_xr[:, :], x_res[r0:r0 + 128, :])
                V.wait_ge(*tok)
                if pb is not None:
                    V.tensor_add(res[:, :], s_pa[:, :], s_pb[:, :])
                    gV()
                    V.tensor_add(res[:, :], res[:, :], s_xr[:, :])
                else:
                    V.tensor_add(res[:, :], s_pa[:, :], s_xr[:, :])
                gV()
                V.reduce_sum(st1[:, :], res[:, :], axis=AX.X)
                gV()
                V.tensor_scalar_mul(st1[:, :], st1[:, :], 1.0 / 1024.0)
                gV()
                V.tensor_scalar(out=xc[:, :], in0=res[:, :],
                                scalar1=st1[:, 0:1], scalar2=None,
                                op0=ALU.subtract)
                incV()
                spa_free = (sVV, c["vv"])
                S.wait_ge(sVV, c["vv"])
                S.activation(out=sqd[:, :], in_=xc[:, :], func=AF.Square,
                             accum_out=st2[:, 0:1])
                gS()
                S.mul(st2[:, 0:1], st2[:, 0:1], 1.0 / 1024.0)
                gS()
                S.activation(out=st2[:, 0:1], in_=st2[:, 0:1], func=AF.Ln,
                             bias=epsb[:, 0:1])
                gS()
                S.activation(out=st2[:, 0:1], in_=st2[:, 0:1], func=AF.Exp,
                             scale=halfneg[:, 0:1])
                incS()
                V.wait_ge(sSS, c["ss"])
                if q > 0 and tt == 0:
                    V.wait_ge(sMM, xnT_prev_mm)
                V.tensor_scalar_mul(xn_f32[tt][:, :], xc[:, :], st2[:, 0:1])
                gV()
                V.tensor_copy(xn_bf[:, :], xn_f32[tt][:, :])
                incV()
                xn_vv = c["vv"]
                dV()
                SY.wait_ge(sVV, c["vv"])
                if xnt_read_tok is not None:
                    SY.wait_ge(*xnt_read_tok)
                for k in range(8):
                    tok = dma_t(SY, "LT",
                                xnT[k][:, tt * 128:(tt + 1) * 128],
                                xn_bf[:, k * 128:(k + 1) * 128])
                xnT_tt_tok = tok
                if SKIP_ROUTER:
                    V.memset(wselT[0:32, tt * 128:(tt + 1) * 128], 0.125)
                    incV()
                    wsel_tt_tok = (sVV, c["vv"])
                    continue
                wtok(T, ppA_frees[0])
                if q == 0 and tt == 0:
                    T.wait_ge(sem["STW"], c["STW"])
                T.wait_ge(*xnT_tt_tok)
                for k in range(8):
                    T.matmul(ppA[:, 0:32],
                             xnT[k][:, tt * 128:(tt + 1) * 128],
                             routerbf[:, k * 32:(k + 1) * 32],
                             start=(k == 0), stop=(k == 7))
                incM()
                V.wait_ge(sMM, c["mm"])
                V.tensor_add(sco[:, :], ppA[:, 0:32], rbrep[:, :])
                gV()
                ppA_frees[0] = (sVV, c["vv"])
                V.reduce_max(st1[:, :], sco[:, :], axis=AX.X)
                gV()
                V.tensor_scalar_mul(st1[:, :], st1[:, :], -1.0)
                incV()
                S.wait_ge(sVV, c["vv"])
                S.activation(out=sco[:, :], in_=sco[:, :], func=AF.Exp,
                             bias=st1[:, 0:1])
                incS()
                V.wait_ge(sSS, c["ss"])
                V.reduce_sum(st2[:, :], sco[:, :], axis=AX.X)
                gV()
                V.reciprocal(st1[:, :], st2[:, :])
                gV()
                V.tensor_scalar_mul(sco[:, :], sco[:, :], st1[:, 0:1])
                V.tensor_copy(scow[:, :], sco[:, :])
                V.memset(msk[:, :], 0.0)
                for _ in range(4):
                    gV()
                    V.reduce_max(st1[:, :], scow[:, :], axis=AX.X)
                    gV()
                    V.tensor_single_scalar(out=ind[:, :], in_=scow[:, :],
                                           scalar=st1[:, 0:1], op=ALU.is_ge)
                    gV()
                    V.tensor_add(msk[:, :], msk[:, :], ind[:, :])
                    V.scalar_tensor_tensor(
                        out=scow[:, :], in0=ind[:, :], scalar=-1e30,
                        in1=scow[:, :], op0=ALU.mult, op1=ALU.add)
                V.tensor_mul(sco[:, :], sco[:, :], msk[:, :])
                gV()
                V.reduce_sum(st1[:, :], sco[:, :], axis=AX.X)
                gV()
                V.tensor_scalar_max(st1[:, :], st1[:, :], 1e-12)
                gV()
                V.reciprocal(st2[:, :], st1[:, :])
                gV()
                V.tensor_scalar_mul(sco[:, :], sco[:, :], st2[:, 0:1])
                gV()
                V.tensor_copy(wsel_bf[:, 0:32], sco[:, :])
                incV()
                if DEBUG_B and q == 0:
                    V.tensor_copy(dbw[:, tt * 32:(tt + 1) * 32], sco[:, :])
                    incV()
                    GP.wait_ge(sVV, c["vv"])
                    dma(GP, "DBG", dbg["dbg_xn"][tt * 128:(tt + 1) * 128, :],
                        xn_f32[tt][:, :])
                    if tt == 3:
                        dma(GP, "DBG",
                            dbg["dbg_wsel"].rearrange("(a p) e -> p a e", a=4),
                            dbw[:, :].rearrange("p (a e) -> p a e", a=4))
                dV()
                SY.wait_ge(sVV, c["vv"])
                wtok(SY, wselT_read_tok)
                wsv_tok = dma_t(SY, "LT",
                                wselT[:, tt * 128:(tt + 1) * 128],
                                wsel_bf[:, :])
                if DEBUG_B and q == 0 and tt == 0:
                    V.tensor_copy(dbu[0:32, 0:128], sl)
                    V.tensor_copy(dbu[0:128, 128:160], wsel_bf[:, :])
                    incV()
                    GP.wait_ge(sVV, c["vv"])
                    dma(GP, "DBG", dbg["dbg_slw"], dbu[0:32, 0:128])
                    dma(GP, "DBG", dbg["dbg_wbf"], dbu[0:128, 128:160])
                wsel_tt_tok = wsv_tok
            all_xnT_tok = (sem["LT"], c["LT"])

            if DEBUG_B and q == 0:
                V.wait_ge(*wsel_tt_tok)
                V.tensor_copy(dbw2[0:32, :], wselT[0:32, :])
                incV()
                GP.wait_ge(sVV, c["vv"])
                dma(GP, "DBG", dbg["dbg_wt"], dbw2[0:32, :])
            GP.wait_ge(*wsel_tt_tok)
            if prev_h1_vv is not None:
                GP.wait_ge(sVV, prev_h1_vv)
            if not SKIP_WSCR:
                wscr_tok = dma(GP, "STS", wscr[:, :], wselT[0:32, :])
                SY.wait_ge(*wscr_tok)

            h1_vv = [None] * 32
            for e in range(0 if SKIP_MOE else NEXPERT):
                i = e % 2
                key = "LG%d" % i
                if e == 0 and q == 0:
                    SY.wait_ge(*wcast_tok)
                wtok(SY, ge_free[i])
                wtok(SY, ue_free[i])
                wtok(SY, wr_free[i])
                dma(SY, key, ge[i][:, :].rearrange("p (k j) -> p k j", k=8),
                    gbs[:, e * 128:(e + 1) * 128].rearrange(
                        "(k p) j -> p k j", k=8))
                dma(SY, key, ue[i][:, :].rearrange("p (k j) -> p k j", k=8),
                    ubs[:, e * 128:(e + 1) * 128].rearrange(
                        "(k p) j -> p k j", k=8))
                if SKIP_WR:
                    V.memset(wr[i][:, :], 0.5)
                    incV()
                    grp_tok = (sem[key], c[key])
                else:
                    grp_tok = dma(SY, key, wr[i][:, :],
                                  wscr[e:e + 1, :].broadcast_to([128, 512]))
                if e == 0 and q == 0:
                    T.wait_ge(*wcast_tok)
                pa_ = ppAs[i]
                pb_ = ppBs[i]
                wtok(T, ppA_frees[i])
                T.wait_ge(*grp_tok)
                if e == 0:
                    T.wait_ge(*all_xnT_tok)
                for k in range(8):
                    T.matmul(pa_[:, :], ge[i][:, k * 128:(k + 1) * 128],
                             xnT[k][:, :], start=(k == 0), stop=(k == 7))
                incM()
                ge_free[i] = (sMM, c["mm"])
                wtok(T, ppB_frees[i])
                for k in range(8):
                    T.matmul(pb_[:, :], ue[i][:, k * 128:(k + 1) * 128],
                             xnT[k][:, :], start=(k == 0), stop=(k == 7))
                incM()
                ue_free[i] = (sMM, c["mm"])
                S.wait_ge(sMM, c["mm"] - 1)
                S.activation(out=sg2[i][:, :], in_=pa_[:, :], func=AF.Silu)
                incS()
                ppA_frees[i] = (sSS, c["ss"])
                S.wait_ge(sMM, c["mm"])
                S.copy(ub2[i][:, :], pb_[:, :])
                incS()
                dS()
                ppB_frees[i] = (sSS, c["ss"])
                ub_ss = c["ss"]
                V.wait_ge(sSS, ub_ss)
                V.wait_ge(*grp_tok)
                V.tensor_mul(h1[e][:, :], sg2[i][:, :], ub2[i][:, :])
                gV()
                V.tensor_mul(h1[e][:, :], h1[e][:, :], wr[i][:, :])
                incV()
                h1_vv[e] = c["vv"]
                wr_free[i] = (sVV, c["vv"])
                if DEBUG_B and q == 0 and e == 0:
                    V.tensor_copy(dbsb[:, :], ge[0][:, :])
                    V.tensor_copy(dbw2[:, :], wr[0][:, :])
                    V.tensor_copy(dbu[:, :], ub[:, :])
                    incV()
                    GP.wait_ge(sVV, c["vv"])
                    dma(GP, "DBG", dbg["dbg_h1"], h1[0][:, :])
                    dma(GP, "DBG", dbg["dbg_sg"], sg[:, :])
                    dma(GP, "DBG", dbg["dbg_xnT"], xnT[0][:, :])
                    dma(GP, "DBG", dbg["dbg_ge"], dbsb[:, :])
                    dma(GP, "DBG", dbg["dbg_wr"], dbw2[:, :])
                    dma(GP, "DBG", dbg["dbg_ub"], dbu[:, :])
                    dbg_tok = (sem["DBG"], c["DBG"])
                else:
                    dbg_tok = None
            prev_h1_vv = c["vv"] if not SKIP_MOE else None

            for st_ in ([] if SKIP_SHARED else range(4)):
                wtok(T, ppA_frees[0])
                if q == 0 and st_ == 0:
                    T.wait_ge(sem["STW"], c["STW"])
                for k in range(8):
                    T.matmul(ppA[:, :],
                             sgatebf[:, k * 512 + st_ * 128:
                                     k * 512 + (st_ + 1) * 128],
                             xnT[k][:, :], start=(k == 0), stop=(k == 7))
                incM()
                S.wait_ge(sMM, c["mm"])
                S.activation(out=hsg[:, :], in_=ppA[:, :], func=AF.Silu)
                incS()
                ppA_frees[0] = (sSS, c["ss"])
                wtok(T, ppB_frees[0])
                for k in range(8):
                    T.matmul(ppB[:, :],
                             supbf[:, k * 512 + st_ * 128:
                                   k * 512 + (st_ + 1) * 128],
                             xnT[k][:, :], start=(k == 0), stop=(k == 7))
                incM()
                S.wait_ge(sMM, c["mm"])
                S.copy(ub2[0][:, :], ppB[:, :])
                incS()
                dS()
                ppB_frees[0] = (sSS, c["ss"])
                V.wait_ge(sSS, c["ss"])
                V.tensor_mul(hs[st_][:, :], hsg[:, :], ub2[0][:, :])
                incV()
            hs_vv = c["vv"]
            if SKIP_SHARED:
                for st_ in range(4):
                    V.memset(hs[st_][:, :], 0.01)
                incV()
                hs_vv = c["vv"]
            if DEBUG_B and q == 0:
                GP.wait_ge(sVV, hs_vv)
                dma(GP, "DBG", dbg["dbg_hs"], hs[0][:, :])

            if SKIP_DOWN:
                for tt in range(4):
                    V.memset(res2[tt][:, :], 0.1)
                    incV()
                    ppd_free[tt] = (sVV, c["vv"])
                down_mm = c["mm"]
            for fh in ([] if SKIP_DOWN else range(2)):
                for k in list(range(NEXPERT)) + list(range(32, 36)):
                    if k < 32:
                        i = k % 2
                        if q == 0 and fh == 0 and k == 0:
                            SY.wait_ge(*wcast_tok)
                        wtok(SY, dbf_free[i])
                        ftok = dma(SY, "LF%d" % i, dbf[i][:, :],
                                   dbs[k * 128:(k + 1) * 128,
                                       fh * 512:(fh + 1) * 512])
                        T.wait_ge(*ftok)
                        if fh == 0 and k == 0:
                            T.wait_ge(sVV, hs_vv)
                        lhs_full = h1[k]
                        rhs = dbf[i][:, :]
                    else:
                        lhs_full = hs[k - 32]
                        rhs = sdownbf[:, (k - 32) * 1024 + fh * 512:
                                      (k - 32) * 1024 + (fh + 1) * 512]
                    for tt in range(4):
                        if k == 0:
                            wtok(T, ppd_free[tt])
                        T.matmul(ppd[tt][:, :],
                                 lhs_full[:, tt * 128:(tt + 1) * 128],
                                 rhs, start=(k == 0), stop=(k == 35))
                    if k < 32:
                        incM()
                        dbf_free[i] = (sMM, c["mm"])
                incM()
                down_mm = c["mm"]
                for tt in range(4):
                    V.wait_ge(sMM, down_mm)
                    V.scalar_tensor_tensor(
                        out=res2[tt][:, fh * 512:(fh + 1) * 512],
                        in0=ppd[tt][:, :], scalar=1.0,
                        in1=xn_f32[tt][:, fh * 512:(fh + 1) * 512],
                        op0=ALU.mult, op1=ALU.add)
                    incV()
                    ppd_free[tt] = (sVV, c["vv"])
            xnT_prev_mm = down_mm
            xnt_read_tok = (sMM, down_mm)
            wselT_read_tok = wscr_tok if not SKIP_WSCR else wsel_tt_tok
            if DEBUG_B and q == 0:
                GP.wait_ge(sVV, c["vv"])
                dma(GP, "DBG", dbg["dbg_res2"], res2[0][:, :])

            for tt in range(4):
                o = tt % 2
                V.reduce_sum(st1[:, :], res2[tt][:, :], axis=AX.X)
                gV()
                V.tensor_scalar_mul(st1[:, :], st1[:, :], 1.0 / 1024.0)
                gV()
                V.tensor_scalar(out=xc[:, :], in0=res2[tt][:, :],
                                scalar1=st1[:, 0:1], scalar2=None,
                                op0=ALU.subtract)
                incV()
                S.wait_ge(sVV, c["vv"])
                S.activation(out=sqd[:, :], in_=xc[:, :], func=AF.Square,
                             accum_out=st2[:, 0:1])
                gS()
                S.mul(st2[:, 0:1], st2[:, 0:1], 1.0 / 1024.0)
                gS()
                S.activation(out=st2[:, 0:1], in_=st2[:, 0:1], func=AF.Ln,
                             bias=epsb[:, 0:1])
                gS()
                S.activation(out=st2[:, 0:1], in_=st2[:, 0:1], func=AF.Exp,
                             scale=halfneg[:, 0:1])
                incS()
                V.wait_ge(sSS, c["ss"])
                wtok(V, outb_free[o])
                V.tensor_scalar_mul(outb[o][:, :], xc[:, :], st2[:, 0:1])
                incV()
                dV()
                GP.wait_ge(sVV, c["vv"])
                outb_free[o] = dma(
                    GP, "STO%d" % o,
                    out[row0 + tt * 128:row0 + (tt + 1) * 128, :],
                    outb[o][:, :])

        GP.wait_ge(sem["STO0"], c["STO0"])
        GP.wait_ge(sem["STO1"], c["STO1"])
        if shared is None:
            _emit_blocks(nc, prog)
    return nc


def build_AB():
    nc = bass.Bass("TRN2", target_bir_lowering=False,
                   detect_race_conditions=False)
    cc_in = nc.dram_tensor("cc_in", [4096, 1024], F32).ap()
    cc_out = nc.dram_tensor("cc_out", [2048, 1024], F32).ap()
    gate_t = nc.dram_tensor("gate_t", [1024, 4096], F32,
                            kind="ExternalInput").ap()
    up_t = nc.dram_tensor("up_t", [1024, 4096], F32,
                          kind="ExternalInput").ap()
    down_f = nc.dram_tensor("down_f", [4096, 1024], F32,
                            kind="ExternalInput").ap()
    gbs = nc.dram_tensor("gbs", [1024, 4096], BF16).ap()
    ubs = nc.dram_tensor("ubs", [1024, 4096], BF16).ap()
    dbs = nc.dram_tensor("dbs", [4096, 1024], BF16).ap()
    cast_jobs = []
    for (src, dsc, ntile) in ((gate_t, gbs, 8), (up_t, ubs, 8),
                              (down_f, dbs, 32)):
        for it in range(ntile):
            cast_jobs.append((dsc[it * 128:(it + 1) * 128, :],
                              src[it * 128:(it + 1) * 128, :]))
    prog = {e: [] for e in ENGINES}
    es = contextlib.ExitStack()
    with es:
        ccsem = es.enter_context(nc.semaphore("sCC"))
        shared = dict(nc=nc, prog=prog, es=es, cc_in=cc_in, cc_out=cc_out,
                      ccsem=ccsem, cast_jobs=cast_jobs, scr_tok=None,
                      ffn_dram=dict(gate_t=gate_t, up_t=up_t, down_f=down_f,
                                    gbs=gbs, ubs=ubs, dbs=dbs))
        build_A(shared)
        assert not cast_jobs
        GP = Rec(prog, "gpsimd")
        GP.collective_compute(
            "ReduceScatter", ALU.add,
            replica_groups=[[0, 1], [2, 3], [4, 5], [6, 7]],
            ins=[cc_in.opt()], outs=[cc_out.opt()]).then_inc(ccsem, 1)
        build_B(shared)
        _emit_blocks(nc, prog)
    return nc



def _tile_rows(a, k):
    a = np.asarray(a)
    return np.ascontiguousarray(
        a.reshape(k, 128, -1).transpose(1, 0, 2).reshape(128, -1))


def kernel(L0_attn_x, L0_attn_in_proj_w, L0_attn_conv_w, L0_attn_conv_b,
           L0_attn_x_proj_w, L0_attn_dt_proj_w, L0_attn_dt_proj_b,
           L0_attn_A_log, L0_attn_D, L0_attn_out_proj_w, L0_ffn_router_w,
           L0_ffn_router_bias, L0_ffn_gate_w, L0_ffn_up_w, L0_ffn_down_w,
           L0_ffn_sgate_w, L0_ffn_sup_w, L0_ffn_sdown_w):
    f = lambda a: np.ascontiguousarray(np.asarray(a, dtype=np.float32))
    x = f(L0_attn_x)
    w_in = f(L0_attn_in_proj_w)
    conv_w = f(L0_attn_conv_w)
    conv_b = f(L0_attn_conv_b)
    x_proj = f(L0_attn_x_proj_w)
    dt_w = f(L0_attn_dt_proj_w)
    dt_b = f(L0_attn_dt_proj_b)
    a_log = f(L0_attn_A_log)
    d_vec = f(L0_attn_D)
    out_w = f(L0_attn_out_proj_w)

    fused = True
    ncA = build_AB() if fused else build_A()
    mapsA = []
    for core in range(8):
        b, h = divmod(core, 2)
        perm = np.concatenate([np.arange(h * 1024, (h + 1) * 1024),
                               np.arange((1 - h) * 1024, (1 - h) * 1024 + 1024)])
        m = {
            "x_fm": np.ascontiguousarray(x[b].T),
            "w_in_t": np.ascontiguousarray(
                np.concatenate([w_in[0:2048][perm],
                                w_in[2048 + h * 1024:2048 + (h + 1) * 1024]],
                               axis=0).T),
            "conv_w": _tile_rows(conv_w[perm], 16),
            "conv_b": _tile_rows(conv_b[perm][:, None], 16),
            "xproj_t": _tile_rows(x_proj.T[perm], 16),
            "dtw_t": np.ascontiguousarray(
                dt_w[h * 1024:(h + 1) * 1024].T),
            "dt_b": _tile_rows(dt_b[h * 1024:(h + 1) * 1024][:, None], 8),
            "a_log": _tile_rows(a_log[h * 1024:(h + 1) * 1024], 8),
            "d_h": _tile_rows(d_vec[h * 1024:(h + 1) * 1024][:, None], 8),
            "outw_t": np.ascontiguousarray(
                out_w[:, h * 1024:(h + 1) * 1024].T),
        }
        mapsA.append(m)

    router_w = f(L0_ffn_router_w)
    router_b = f(L0_ffn_router_bias)
    gate_w = f(L0_ffn_gate_w)
    up_w = f(L0_ffn_up_w)
    down_w = f(L0_ffn_down_w)
    sgate_w = f(L0_ffn_sgate_w)
    sup_w = f(L0_ffn_sup_w)
    sdown_w = f(L0_ffn_sdown_w)

    gate_t = np.ascontiguousarray(gate_w.reshape(4096, 1024).T)
    up_t = np.ascontiguousarray(up_w.reshape(4096, 1024).T)
    down_flat = np.ascontiguousarray(
        down_w.transpose(0, 2, 1).reshape(4096, 1024))
    shared_w = {
        "router_t": _tile_rows(router_w.T, 8),
        "rbias": np.ascontiguousarray(router_b[None, :]),
        "gate_t": gate_t,
        "up_t": up_t,
        "down_f": down_flat,
        "sgate_t": _tile_rows(sgate_w.T, 8),
        "sup_t": _tile_rows(sup_w.T, 8),
        "sdown_t": _tile_rows(sdown_w.T, 4),
        "ident": np.eye(128, dtype=np.float32).astype(ml_dtypes.bfloat16),
    }
    maps = []
    for core in range(8):
        b, h = divmod(core, 2)
        m = dict(shared_w)
        m.update(mapsA[core])
        m["x_res"] = np.ascontiguousarray(x[b, h * 2048:(h + 1) * 2048])
        maps.append(m)
    res = run_bass_kernel_spmd(ncA, maps, core_ids=list(range(8)))

    out = np.empty((4, 4096, 1024), dtype=np.float32)
    for core in range(8):
        b, h = divmod(core, 2)
        out[b, h * 2048:(h + 1) * 2048] = np.asarray(res.results[core]["out"])
    return out


# version 05
# speedup vs baseline: 1.1134x; 1.1134x over previous
import contextlib

import numpy as np
import ml_dtypes
import concourse.bass as bass
import concourse.mybir as mybir
from concourse.bass_utils import run_bass_kernel_spmd

F32 = mybir.dt.float32
BF16 = mybir.dt.bfloat16
AF = mybir.ActivationFunctionType
ALU = mybir.AluOpType
AX = mybir.AxisListType

QT = 512
NCHUNK = 8
NQUARTER = 4
DEBUG_B = False
DEBUG_A = False
NEXPERT = 32
SKIP_WR = False
SKIP_MOE = False
SKIP_DOWN = False
SKIP_SHARED = False
SKIP_ROUTER = False
SKIP_TRANS = False
SKIP_WSCR = False
ENGINES = ("sync", "gpsimd", "tensor", "vector", "scalar")


class Rec:

    def __init__(self, prog, name):
        self._prog = prog
        self._name = name

    def __getattr__(self, meth):
        prog = self._prog[self._name]

        def call(*a, **k):
            entry = [meth, a, k, []]
            prog.append(entry)

            class _C:
                def then_inc(self, sem, v):
                    entry[3].append((sem, v))
                    return self

            return _C()

        return call


def _replay(eng, entries):
    for meth, a, k, incs in entries:
        r = getattr(eng, meth)(*a, **k)
        for sem, v in incs:
            r = r.then_inc(sem, v)


def _emit_blocks(nc, prog):
    with nc.Block() as block:
        @block.sync
        def _(eng):
            _replay(eng, prog["sync"])

        @block.gpsimd
        def _(eng):
            _replay(eng, prog["gpsimd"])

        @block.tensor
        def _(eng):
            _replay(eng, prog["tensor"])

        @block.vector
        def _(eng):
            _replay(eng, prog["vector"])

        @block.scalar
        def _(eng):
            _replay(eng, prog["scalar"])



def build_A(shared=None):
    if shared is None:
        nc = bass.Bass("TRN2", target_bir_lowering=False,
                       detect_race_conditions=False)
    else:
        nc = shared["nc"]

    x_fm = nc.dram_tensor("x_fm", [1024, 4096], F32, kind="ExternalInput").ap()
    w_in_t = nc.dram_tensor("w_in_t", [1024, 3072], F32, kind="ExternalInput").ap()
    conv_w = nc.dram_tensor("conv_w", [128, 64], F32, kind="ExternalInput").ap()
    conv_b = nc.dram_tensor("conv_b", [128, 16], F32, kind="ExternalInput").ap()
    xproj_t = nc.dram_tensor("xproj_t", [128, 1536], F32, kind="ExternalInput").ap()
    dtw_t = nc.dram_tensor("dtw_t", [64, 1024], F32, kind="ExternalInput").ap()
    dt_b = nc.dram_tensor("dt_b", [128, 8], F32, kind="ExternalInput").ap()
    a_log = nc.dram_tensor("a_log", [128, 128], F32, kind="ExternalInput").ap()
    d_h = nc.dram_tensor("d_h", [128, 8], F32, kind="ExternalInput").ap()
    outw_t = nc.dram_tensor("outw_t", [1024, 1024], F32, kind="ExternalInput").ap()

    if shared is None:
        partial = nc.dram_tensor("partial", [4096, 1024], F32,
                                 kind="ExternalOutput").ap()
        cast_jobs = []
    else:
        partial = shared["cc_in"]
        cast_jobs = shared["cast_jobs"]
    dbg = {}
    if DEBUG_A:
        for nm, shp in (("dbg_u0", [128, 512]), ("dbg_sz0", [128, 512]),
                        ("dbg_dtr", [64, 512]), ("dbg_bcr", [32, 512]),
                        ("dbg_delta", [128, 512]), ("dbg_du", [128, 512]),
                        ("dbg_brep", [128, 512]), ("dbg_dA", [128, 512]),
                        ("dbg_hb", [128, 4096]), ("dbg_y", [128, 512]),
                        ("dbg_gated", [128, 512])):
            dbg[nm] = nc.dram_tensor(nm, shp, F32, kind="ExternalOutput").ap()
    bct_h = nc.dram_tensor("bct", [32, QT], BF16)
    bct = bct_h.ap()

    if shared is None:
        prog = {e: [] for e in ENGINES}
    else:
        prog = shared["prog"]
    SY, GP, T, V, S = (Rec(prog, e) for e in ENGINES)

    es = contextlib.ExitStack()
    sem_es = shared["es"] if shared is not None else es
    with es:
        def sb(name, shape, dt):
            return es.enter_context(nc.sbuf_tensor(name, shape, dt))

        cw_all = sb("cw_all", [128, 64], F32)
        cb_all = sb("cb_all", [128, 16], F32)
        dtb_all = sb("dtb_all", [128, 8], F32)
        dh_all = sb("dh_all", [128, 8], F32)
        aneg = sb("aneg", [128, 128], F32)
        hstate = sb("hstate", [128, 128], F32)
        halo = sb("halo", [128, 48], BF16)
        winbf = [sb(f"winbf{k}", [128, 3072], BF16) for k in range(8)]
        outwbf = [sb(f"outwbf{k}", [128, 1024], BF16) for k in range(8)]
        xprojbf = sb("xprojbf", [128, 1536], BF16)
        dtwbf = sb("dtwbf", [64, 1024], BF16)

        xf32 = [sb(f"xf32_{i}", [128, QT], F32) for i in range(2)]
        xbf = [sb(f"xbf{k}", [128, QT], BF16) for k in range(8)]
        xitmp = [sb(f"xitmp{i}", [128, QT + 3], BF16) for i in range(2)]
        cacc = [sb(f"cacc{i}", [128, QT], F32) for i in range(2)]
        u = [sb(f"u{i}", [128, QT], BF16) for i in range(8)]
        utmp = [sb(f"utmp{i}", [128, QT], BF16) for i in range(2)]
        sz = [sb(f"sz{i}", [128, QT], BF16) for i in range(8)]
        dtr_bf = sb("dtr_bf", [64, QT], BF16)
        bcr_bf = sb("bcr_bf", [32, QT], BF16)
        brep = sb("brep", [128, 16 * QT], BF16)
        crep = sb("crep", [128, 16 * QT], BF16)
        spe = sb("spe", [128, QT], F32)
        delta = sb("delta", [128, QT], F32)
        du_bf = sb("du_bf", [128, QT], BF16)
        ebuf = [sb(f"ebuf{i}", [128, 2 * QT], F32) for i in range(2)]
        dAq = [sb(f"dAq{i}", [128, 2 * QT], F32) for i in range(3)]
        p_bf = sb("p_bf", [128, 4 * QT], BF16)
        hb = sb("hb", [128, 8 * QT], BF16)
        t2 = sb("t2", [128, 8 * QT], BF16)
        ytmp = sb("ytmp", [128, QT], BF16)
        y_f32 = sb("y_f32", [128, QT], F32)
        yg = sb("yg", [128, QT], F32)
        gated = [sb(f"gated{i}", [128, QT], BF16) for i in range(8)]
        oevac = [sb(f"oevac{i}", [128, QT], F32) for i in range(2)]

        pin = [es.enter_context(nc.psum_tensor(f"pin{i}", [128, QT], F32))
               for i in range(2)]
        px = es.enter_context(nc.psum_tensor("px", [96, QT], F32))
        pdt = es.enter_context(nc.psum_tensor("pdt", [128, QT], F32))
        pout = [es.enter_context(nc.psum_tensor(f"pout{i}", [128, QT], F32))
                for i in range(2)]

        sem = {}
        c = {}
        for name in ("LX0", "LX1", "LW", "LB", "LC", "STB", "STP0", "STP1",
                     "DBG"):
            sem[name] = sem_es.enter_context(nc.semaphore("sA" + name))
            c[name] = 0
        sMM = sem_es.enter_context(nc.semaphore("sAMM"))
        sVV = sem_es.enter_context(nc.semaphore("sAVV"))
        sSS = sem_es.enter_context(nc.semaphore("sASS"))
        c.update(mm=0, vv=0, ss=0)

        def dma(eng, key, out, in_):
            eng.dma_start(out=out, in_=in_).then_inc(sem[key], 16)
            c[key] += 16
            return (sem[key], c[key])

        def incM():
            prog["tensor"][-1][3].append((sMM, 1))
            c["mm"] += 1

        def incV():
            prog["vector"][-1][3].append((sVV, 1))
            c["vv"] += 1

        def incS():
            prog["scalar"][-1][3].append((sSS, 1))
            c["ss"] += 1

        def gV():
            if not prog["vector"][-1][3]:
                incV()
            V.wait_ge(sVV, c["vv"])

        def dV():
            V.drain()
            V.sem_inc(sVV, 1)
            c["vv"] += 1

        def dS():
            S.drain()
            S.sem_inc(sSS, 1)
            c["ss"] += 1

        def gS():
            if not prog["scalar"][-1][3]:
                incS()
            S.wait_ge(sSS, c["ss"])

        def wtok(Rr, tok):
            if tok is not None:
                Rr.wait_ge(tok[0], tok[1])

        dma(SY, "LW", cw_all[:, :], conv_w)
        dma(SY, "LW", cb_all[:, :], conv_b)
        dma(SY, "LW", dtb_all[:, :], dt_b)
        dma(SY, "LW", dh_all[:, :], d_h)
        tok = dma(SY, "LW", cacc[0][:, 0:128], a_log)
        S.wait_ge(*tok)
        S.activation(out=aneg[:, :], in_=cacc[0][:, 0:128], func=AF.Exp)
        incS()
        dS()
        V.wait_ge(sSS, c["ss"])
        V.memset(hstate[:, :], 0.0)
        V.memset(halo[:, :], 0.0)
        V.tensor_scalar_mul(aneg[:, :], aneg[:, :], -1.0)
        incV()
        aneg_done = c["vv"]

        dma(GP, "STB", xprojbf[:, :], xproj_t)
        dma(GP, "STB", dtwbf[:, :], dtw_t)
        for k in range(8):
            dma(GP, "STB", winbf[k][:, :], w_in_t[k * 128:(k + 1) * 128, :])
        for k in range(8):
            dma(GP, "STB", outwbf[k][:, :], outw_t[k * 128:(k + 1) * 128, :])
        wtok_a = (sem["STB"], c["STB"])

        pin_free = [None, None]
        pout_free = [None, None]
        oevac_free = [None, None]
        utmp_free = [None, None]
        pdt_free = None
        prev_gated_vv = [None] * 8
        prev_scan_all_vv = None
        prev_inproj_mm = None
        prev_dtproj_mm = None
        cast_done = [0] * 8

        for ch in range(NCHUNK):
            col0 = ch * QT
            for k in range(8):
                if k >= 2 or ch > 0:
                    SY.wait_ge(sVV, cast_done[(k - 2) % 8])
                tok = dma(SY, "LX%d" % (k % 2), xf32[k % 2][:, :],
                          x_fm[k * 128:(k + 1) * 128, col0:col0 + QT])
                V.wait_ge(*tok)
                if ch > 0 and k == 0:
                    V.wait_ge(sMM, prev_inproj_mm)
                V.tensor_copy(xbf[k][:, :], xf32[k % 2][:, :])
                incV()
                cast_done[k] = c["vv"]
            all_cast = c["vv"]

            for mt in range(24):
                pi = mt % 2
                wtok(T, pin_free[pi])
                if mt == 0:
                    T.wait_ge(sVV, all_cast)
                    if ch == 0:
                        T.wait_ge(*wtok_a)
                for k in range(8):
                    T.matmul(pin[pi][:, :],
                             winbf[k][:, mt * 128:(mt + 1) * 128],
                             xbf[k][:, :], start=(k == 0), stop=(k == 7))
                incM()
                in_mm = c["mm"]
                if mt < 16:
                    V.wait_ge(sMM, in_mm)
                    V.tensor_copy(xitmp[pi][:, 0:3],
                                  halo[:, mt * 3:mt * 3 + 3])
                    V.tensor_copy(xitmp[pi][:, 3:3 + QT], pin[pi][:, :])
                    gV()
                    V.tensor_copy(halo[:, mt * 3:mt * 3 + 3],
                                  xitmp[pi][:, QT:QT + 3])
                    V.tensor_scalar_mul(cacc[0][:, :], xitmp[pi][:, 0:QT],
                                        cw_all[:, mt * 4:mt * 4 + 1])
                    gV()
                    V.scalar_tensor_tensor(
                        out=cacc[1][:, :], in0=xitmp[pi][:, 1:1 + QT],
                        scalar=cw_all[:, mt * 4 + 1:mt * 4 + 2],
                        in1=cacc[0][:, :], op0=ALU.mult, op1=ALU.add)
                    gV()
                    V.scalar_tensor_tensor(
                        out=cacc[0][:, :], in0=xitmp[pi][:, 2:2 + QT],
                        scalar=cw_all[:, mt * 4 + 2:mt * 4 + 3],
                        in1=cacc[1][:, :], op0=ALU.mult, op1=ALU.add)
                    gV()
                    V.scalar_tensor_tensor(
                        out=cacc[1][:, :], in0=xitmp[pi][:, 3:3 + QT],
                        scalar=cw_all[:, mt * 4 + 3:mt * 4 + 4],
                        in1=cacc[0][:, :], op0=ALU.mult, op1=ALU.add)
                    incV()
                    pin_free[pi] = (sVV, c["vv"])
                    conv_vv = c["vv"]
                    if mt < 8:
                        udst = u[mt]
                        if ch > 0:
                            S.wait_ge(sVV, prev_gated_vv[mt])
                    else:
                        udst = utmp[mt % 2]
                        wtok(S, utmp_free[mt % 2])
                    S.wait_ge(sVV, conv_vv)
                    S.activation(out=udst[:, :], in_=cacc[1][:, :],
                                 func=AF.Silu, bias=cb_all[:, mt:mt + 1])
                    incS()
                    dS()
                    T.wait_ge(sSS, c["ss"])
                    T.matmul(px[:, :], xprojbf[:, mt * 96:(mt + 1) * 96],
                             udst[:, :], start=(mt == 0), stop=(mt == 15))
                    incM()
                    if mt >= 8:
                        utmp_free[mt % 2] = (sMM, c["mm"])
                else:
                    dt = mt - 16
                    if ch > 0:
                        S.wait_ge(sVV, prev_gated_vv[dt])
                    S.wait_ge(sMM, in_mm)
                    S.activation(out=sz[dt][:, :], in_=pin[pi][:, :],
                                 func=AF.Silu)
                    incS()
                    dS()
                    pin_free[pi] = (sSS, c["ss"])
            xproj_mm = c["mm"]

            V.wait_ge(sMM, xproj_mm)
            if ch > 0:
                V.wait_ge(sMM, prev_dtproj_mm)
            V.tensor_copy(dtr_bf[:, :], px[0:64, :])
            V.tensor_copy(bcr_bf[:, :], px[64:96, :])
            incV()
            dV()
            bc_vv = c["vv"]
            GP.wait_ge(sVV, bc_vv)
            if ch > 0:
                GP.wait_ge(sem["LB"], c["LB"])
                GP.wait_ge(sem["LC"], c["LC"])
            bct_tok = dma(GP, "STB", bct[:, :], bcr_bf[:, :])
            SY.wait_ge(*bct_tok)
            if prev_scan_all_vv is not None:
                SY.wait_ge(sVV, prev_scan_all_vv)
                GP.wait_ge(sVV, prev_scan_all_vv)
            for n in range(16):
                dma(SY, "LB", brep[:, n * QT:(n + 1) * QT],
                    bct[n:n + 1, :].broadcast_to([128, QT]))
                dma(GP, "LC", crep[:, n * QT:(n + 1) * QT],
                    bct[16 + n:17 + n, :].broadcast_to([128, QT]))
            rep_b = (sem["LB"], c["LB"])
            rep_c = (sem["LC"], c["LC"])

            if DEBUG_A and ch == 0:
                V.wait_ge(sSS, c["ss"])
                V.wait_ge(*rep_b)
                V.wait_ge(*rep_c)
                incV()
                dV()
                dS()
                GP.wait_ge(sVV, c["vv"])
                GP.wait_ge(sSS, c["ss"])
                dma(GP, "DBG", dbg["dbg_u0"], u[0][:, :])
                dma(GP, "DBG", dbg["dbg_sz0"], sz[0][:, :])
                dma(GP, "DBG", dbg["dbg_dtr"], dtr_bf[:, :])
                dma(GP, "DBG", dbg["dbg_bcr"], bcr_bf[:, :])
                dma(GP, "DBG", dbg["dbg_brep"], brep[:, 0:512])
                dbg_tok = (sem["DBG"], c["DBG"])
                V.wait_ge(*dbg_tok)
                S.wait_ge(*dbg_tok)
            for dt in range(8):
                wtok(T, pdt_free)
                if dt == 0:
                    T.wait_ge(sVV, bc_vv)
                T.matmul(pdt[:, :], dtwbf[:, dt * 128:(dt + 1) * 128],
                         dtr_bf[:, :], start=True, stop=True)
                incM()
                dtmm = c["mm"]
                S.wait_ge(sMM, dtmm)
                S.activation(out=spe[:, :], in_=pdt[:, :], func=AF.Exp,
                             bias=dtb_all[:, dt:dt + 1])
                gS()
                S.activation(out=delta[:, :], in_=spe[:, :], func=AF.Ln,
                             bias=1.0)
                if ch == 0 and dt == 0:
                    S.wait_ge(sVV, aneg_done)
                incS()
                dS()
                pdt_free = (sSS, c["ss"])
                sp_ss = c["ss"]
                V.wait_ge(sSS, sp_ss)
                V.tensor_mul(du_bf[:, :], delta[:, :], u[dt][:, :])
                gV()
                if DEBUG_A and ch == 0 and dt == 0:
                    dV()
                    dS()
                    GP.wait_ge(sVV, c["vv"])
                    GP.wait_ge(sSS, c["ss"])
                    dma(GP, "DBG", dbg["dbg_delta"], delta[:, :])
                    dma(GP, "DBG", dbg["dbg_du"], du_bf[:, :])
                    V.wait_ge(sem["DBG"], c["DBG"])
                    S.wait_ge(sem["DBG"], c["DBG"])
                if dt == 0:
                    V.wait_ge(*rep_b)
                    V.wait_ge(*rep_c)
                scan_vv = {}
                e_vv = {}
                exp_ss = {}

                def emit_E(g2):
                    if g2 >= 2:
                        V.wait_ge(sSS, exp_ss[g2 - 2])
                    for j2 in range(2):
                        V.tensor_scalar_mul(
                            ebuf[g2 % 2][:, j2 * QT:(j2 + 1) * QT],
                            delta[:, :],
                            aneg[:, dt * 16 + 2 * g2 + j2:
                                 dt * 16 + 2 * g2 + j2 + 1])
                    incV()
                    e_vv[g2] = c["vv"]

                def emit_exp(g2):
                    if g2 >= 3:
                        S.wait_ge(sVV, scan_vv[2 * g2 - 5])
                    S.wait_ge(sVV, e_vv[g2])
                    S.activation(out=dAq[g2 % 3][:, :],
                                 in_=ebuf[g2 % 2][:, :], func=AF.Exp)
                    incS()
                    dS()
                    exp_ss[g2] = c["ss"]

                emit_E(0)
                emit_exp(0)
                emit_E(1)
                emit_exp(1)
                for n in range(16):
                    nb, half = n % 8, n // 8
                    nq4 = n % 4
                    g, jj = n // 2, n % 2
                    if nq4 == 0:
                        for n2 in range(n, n + 4):
                            V.tensor_mul(p_bf[:, (n2 % 4) * QT:
                                               (n2 % 4 + 1) * QT],
                                         du_bf[:, :],
                                         brep[:, n2 * QT:(n2 + 1) * QT])
                        incV()
                    if jj == 0 and g + 2 <= 7:
                        emit_E(g + 2)
                        emit_exp(g + 2)
                    V.wait_ge(sSS, exp_ss[g])
                    V.tensor_tensor_scan(
                        out=hb[:, nb * QT:(nb + 1) * QT],
                        data0=dAq[g % 3][:, jj * QT:(jj + 1) * QT],
                        data1=p_bf[:, nq4 * QT:(nq4 + 1) * QT],
                        initial=(0.0 if ch == 0 else
                                 hstate[:, dt * 16 + n:dt * 16 + n + 1]),
                        op0=ALU.mult, op1=ALU.add)
                    incV()
                    scan_vv[n] = c["vv"]
                    if nb == 7:
                        gV()
                        hb3 = hb[:, :].rearrange("p (n t) -> p n t", n=8)
                        V.tensor_copy(
                            hstate[:, dt * 16 + half * 8:
                                   dt * 16 + half * 8 + 8],
                            hb3[:, :, QT - 1])
                        V.tensor_mul(t2[:, :], hb[:, :],
                                     crep[:, half * 8 * QT:(half + 1) * 8 * QT])
                        gV()
                        V.tensor_add(t2[:, 0:4 * QT], t2[:, 0:4 * QT],
                                     t2[:, 4 * QT:8 * QT])
                        gV()
                        V.tensor_add(t2[:, 0:2 * QT], t2[:, 0:2 * QT],
                                     t2[:, 2 * QT:4 * QT])
                        gV()
                        if half == 0:
                            V.tensor_add(y_f32[:, :], t2[:, 0:QT],
                                         t2[:, QT:2 * QT])
                        else:
                            V.tensor_add(ytmp[:, :], t2[:, 0:QT],
                                         t2[:, QT:2 * QT])
                            gV()
                            V.tensor_add(y_f32[:, :], y_f32[:, :], ytmp[:, :])
                        incV()
                prev_scan_all_vv = c["vv"]
                V.wait_ge(sVV, c["vv"])
                if DEBUG_A and ch == 0 and dt == 0:
                    dV()
                    GP.wait_ge(sVV, c["vv"])
                    dma(GP, "DBG", dbg["dbg_hb"], hb[:, :])
                    dma(GP, "DBG", dbg["dbg_y"], y_f32[:, :])
                    V.wait_ge(sem["DBG"], c["DBG"])
                V.scalar_tensor_tensor(
                    out=yg[:, :], in0=u[dt][:, :],
                    scalar=dh_all[:, dt:dt + 1], in1=y_f32[:, :],
                    op0=ALU.mult, op1=ALU.add)
                gV()
                V.tensor_mul(gated[dt][:, :], yg[:, :], sz[dt][:, :])
                incV()
                prev_gated_vv[dt] = c["vv"]
                if DEBUG_A and ch == 0 and dt == 0:
                    dV()
                    GP.wait_ge(sVV, c["vv"])
                    dma(GP, "DBG", dbg["dbg_gated"], gated[dt][:, :])
                    V.wait_ge(sem["DBG"], c["DBG"])
            prev_dtproj_mm = c["mm"]
            prev_inproj_mm = xproj_mm
            all_gated = c["vv"]
            for _ in range(6):
                if cast_jobs:
                    dst_, src_ = cast_jobs.pop(0)
                    shared["scr_tok"] = dma(GP, "STB", dst_, src_)

            for tt in range(4):
                for fh in range(2):
                    i = (tt * 2 + fh) % 2
                    wtok(T, pout_free[i])
                    if tt == 0 and fh == 0:
                        T.wait_ge(sVV, all_gated)
                    for k in range(8):
                        T.matmul(pout[i][:, :],
                                 gated[k][:, tt * 128:(tt + 1) * 128],
                                 outwbf[k][:, fh * QT:(fh + 1) * QT],
                                 start=(k == 0), stop=(k == 7))
                    incM()
                    omm = c["mm"]
                    wtok(S, oevac_free[i])
                    S.wait_ge(sMM, omm)
                    S.copy(oevac[i][:, :], pout[i][:, :])
                    incS()
                    pout_free[i] = (sSS, c["ss"])
                    dS()
                    GP.wait_ge(sSS, c["ss"])
                    oevac_free[i] = dma(
                        GP, "STP%d" % i,
                        partial[col0 + tt * 128:col0 + (tt + 1) * 128,
                                fh * QT:(fh + 1) * QT],
                        oevac[i][:, :])

        GP.wait_ge(sem["STP0"], c["STP0"])
        GP.wait_ge(sem["STP1"], c["STP1"])
        if shared is None:
            _emit_blocks(nc, prog)
    return nc



def build_B(shared=None):
    if shared is None:
        nc = bass.Bass("TRN2", target_bir_lowering=False,
                       detect_race_conditions=False)
        pa = nc.dram_tensor("pa", [2048, 1024], F32,
                            kind="ExternalInput").ap()
        pb = nc.dram_tensor("pb", [2048, 1024], F32,
                            kind="ExternalInput").ap()
    else:
        nc = shared["nc"]
        pa = shared["cc_out"]
        pb = None
    x_res = nc.dram_tensor("x_res", [2048, 1024], F32, kind="ExternalInput").ap()
    router_t = nc.dram_tensor("router_t", [128, 256], F32,
                              kind="ExternalInput").ap()
    rbias = nc.dram_tensor("rbias", [1, 32], F32, kind="ExternalInput").ap()
    if shared is not None:
        fd = shared["ffn_dram"]
        gate_t, up_t, down_f = fd["gate_t"], fd["up_t"], fd["down_f"]
    else:
        gate_t = nc.dram_tensor("gate_t", [1024, 4096], F32,
                                kind="ExternalInput").ap()
        up_t = nc.dram_tensor("up_t", [1024, 4096], F32,
                              kind="ExternalInput").ap()
        down_f = nc.dram_tensor("down_f", [4096, 1024], F32,
                                kind="ExternalInput").ap()
    sgate_t = nc.dram_tensor("sgate_t", [128, 4096], F32,
                             kind="ExternalInput").ap()
    sup_t = nc.dram_tensor("sup_t", [128, 4096], F32, kind="ExternalInput").ap()
    sdown_t = nc.dram_tensor("sdown_t", [128, 4096], F32,
                             kind="ExternalInput").ap()
    ident = nc.dram_tensor("ident", [128, 128], BF16, kind="ExternalInput").ap()

    out = nc.dram_tensor("out", [2048, 1024], F32, kind="ExternalOutput").ap()

    dbg = {}
    if DEBUG_B:
        for nm, shp in (("dbg_slw", [32, 128]), ("dbg_wbf", [128, 32]),
                        ("dbg_wt", [32, 512]), ("dbg_wr", [128, 512]),
                        ("dbg_ge", [128, 1024]), ("dbg_ub", [128, 512]),
                        ("dbg_xn", [512, 1024]), ("dbg_wsel", [512, 32]),
                        ("dbg_xnT", [128, 512]), ("dbg_h1", [128, 512]),
                        ("dbg_hs", [128, 512]), ("dbg_sg", [128, 512]),
                        ("dbg_res2", [128, 1024])):
            dbg[nm] = nc.dram_tensor(nm, shp, F32, kind="ExternalOutput").ap()
    if shared is not None:
        gbs, ubs, dbs = fd["gbs"], fd["ubs"], fd["dbs"]
    else:
        gbs = nc.dram_tensor("gbs", [1024, 4096], BF16).ap()
        ubs = nc.dram_tensor("ubs", [1024, 4096], BF16).ap()
        dbs = nc.dram_tensor("dbs", [4096, 1024], BF16).ap()
    wscr = nc.dram_tensor("wscr", [32, 512], BF16).ap()

    if shared is None:
        prog = {e: [] for e in ENGINES}
    else:
        prog = shared["prog"]
    SY, GP, T, V, S = (Rec(prog, e) for e in ENGINES)

    es = contextlib.ExitStack()
    sem_es = shared["es"] if shared is not None else es
    with es:
        def sb(name, shape, dt):
            return es.enter_context(nc.sbuf_tensor(name, shape, dt))

        identb = sb("identb", [128, 128], BF16)
        routerbf = sb("routerbf", [128, 256], BF16)
        rbrep = sb("rbrep", [128, 32], F32)
        sgatebf = sb("sgatebf", [128, 4096], BF16)
        supbf = sb("supbf", [128, 4096], BF16)
        sdownbf = sb("sdownbf", [128, 4096], BF16)

        s_pa = sb("s_pa", [128, 1024], F32)
        s_pb = sb("s_pb", [128, 1024], F32)
        s_xr = sb("s_xr", [128, 1024], F32)
        res = sb("res", [128, 1024], F32)
        xc = sb("xc", [128, 1024], F32)
        sqd = sb("sqd", [128, 1024], F32)
        st1 = sb("st1", [128, 1], F32)
        st2 = sb("st2", [128, 1], F32)
        epsb = sb("epsb", [128, 1], F32)
        halfneg = sb("halfneg", [128, 1], F32)
        xn_f32 = [sb(f"xn{i}", [128, 1024], F32) for i in range(4)]
        xn_bf = sb("xn_bf", [128, 1024], BF16)
        xnT = [sb(f"xnT{k}", [128, 512], BF16) for k in range(8)]
        sco = sb("sco", [128, 32], F32)
        scow = sb("scow", [128, 32], F32)
        msk = sb("msk", [128, 32], F32)
        ind = sb("ind", [128, 32], F32)
        wsel_bf = sb("wsel_bf", [128, 128], BF16)
        wselT = sb("wselT", [128, 512], BF16)
        ge = [sb(f"ge{i}", [128, 1024], BF16) for i in range(2)]
        ue = [sb(f"ue{i}", [128, 1024], BF16) for i in range(2)]
        sg2 = [sb(f"sg{i}", [128, 512], BF16) for i in range(2)]
        ub2 = [sb(f"ub{i}", [128, 512], BF16) for i in range(2)]
        wr = [sb(f"wr{i}", [128, 512], BF16) for i in range(2)]
        h1 = [sb(f"h1_{k}", [128, 512], BF16) for k in range(32)]
        dbf = [sb(f"dbf{i}", [128, 512], BF16) for i in range(2)]
        hs = [sb(f"hs{i}", [128, 512], BF16) for i in range(4)]
        hsg = sb("hsg", [128, 512], BF16)
        res2 = [sb(f"res2_{i}", [128, 1024], F32) for i in range(4)]
        if DEBUG_B:
            dbsb = sb("dbsb", [128, 1024], F32)
            dbw = sb("dbw", [128, 128], F32)
            dbw2 = sb("dbw2", [128, 512], F32)
            dbu = sb("dbu", [128, 512], F32)
        outb = [sb(f"outb{i}", [128, 1024], F32) for i in range(2)]

        ppAs = [es.enter_context(nc.psum_tensor(f"ppA{i}", [128, 512], F32))
                for i in range(2)]
        ppBs = [es.enter_context(nc.psum_tensor(f"ppB{i}", [128, 512], F32))
                for i in range(2)]
        ppA, ppB = ppAs[0], ppBs[0]
        ppd = [es.enter_context(nc.psum_tensor(f"ppd{i}", [128, 512], F32))
               for i in range(4)]

        sem = {}
        c = {}
        for name in ("LP", "LW", "LG0", "LG1", "LF0", "LF1",
                     "STW", "STS", "STO0", "STO1", "LT", "DBG"):
            sem[name] = sem_es.enter_context(nc.semaphore("sB" + name))
            c[name] = 0
        sMM = sem_es.enter_context(nc.semaphore("sBMM"))
        sVV = sem_es.enter_context(nc.semaphore("sBVV"))
        sSS = sem_es.enter_context(nc.semaphore("sBSS"))
        c.update(mm=0, vv=0, ss=0)

        def dma(eng, key, out_, in_):
            eng.dma_start(out=out_, in_=in_).then_inc(sem[key], 16)
            c[key] += 16
            return (sem[key], c[key])

        def dma_t(eng, key, out_, in_):
            eng.dma_start_transpose(out=out_, in_=in_).then_inc(sem[key], 16)
            c[key] += 16
            return (sem[key], c[key])

        def dma_t(eng, key, out_, in_):
            eng.dma_start_transpose(out=out_, in_=in_).then_inc(sem[key], 16)
            c[key] += 16
            return (sem[key], c[key])

        def incM():
            prog["tensor"][-1][3].append((sMM, 1))
            c["mm"] += 1

        def incV():
            prog["vector"][-1][3].append((sVV, 1))
            c["vv"] += 1

        def incS():
            prog["scalar"][-1][3].append((sSS, 1))
            c["ss"] += 1

        def gV():
            if not prog["vector"][-1][3]:
                incV()
            V.wait_ge(sVV, c["vv"])

        def dV():
            V.drain()
            V.sem_inc(sVV, 1)
            c["vv"] += 1

        def dS():
            S.drain()
            S.sem_inc(sSS, 1)
            c["ss"] += 1

        def gS():
            if not prog["scalar"][-1][3]:
                incS()
            S.wait_ge(sSS, c["ss"])

        def wtok(Rr, tok):
            if tok is not None:
                Rr.wait_ge(tok[0], tok[1])

        V.memset(epsb[:, :], 1e-5)
        V.memset(halfneg[:, :], -0.5)
        V.memset(wsel_bf[:, :], 0.0)
        incV()
        dma(SY, "LW", identb[:, :], ident)
        dma(SY, "LW", rbrep[:, :], rbias[0:1, :].broadcast_to([128, 32]))
        dma(GP, "STW", routerbf[:, :], router_t)
        dma(GP, "STW", sgatebf[:, :], sgate_t)
        dma(GP, "STW", supbf[:, :], sup_t)
        wcast_tok = dma(GP, "STW", sdownbf[:, :], sdown_t)
        if shared is not None:
            wcast_tok = shared["scr_tok"]
        else:
            for (src, dsc, ntile) in ((gate_t, gbs, 8), (up_t, ubs, 8),
                                      (down_f, dbs, 32)):
                for it in range(ntile):
                    wcast_tok = dma(GP, "STW",
                                    dsc[it * 128:(it + 1) * 128, :],
                                    src[it * 128:(it + 1) * 128, :])

        if shared is not None:
            ccs = shared["ccsem"]
            for Rr in (SY, GP, T, V, S):
                Rr.wait_ge(ccs, 1)
        ppA_frees = [None, None]
        ppB_frees = [None, None]
        xnt_read_tok = None
        ppd_free = [None] * 4
        ge_free = [None, None]
        ue_free = [None, None]
        wr_free = [None, None]
        dbf_free = [None, None]
        outb_free = [None, None]
        xnT_prev_mm = None
        prev_h1_vv = None

        spa_free = None
        wselT_read_tok = None
        for q in range(NQUARTER):
            row0 = q * 512
            wsel_tt_tok = None
            for tt in range(4):
                r0 = row0 + tt * 128
                wtok(SY, spa_free)
                dma(SY, "LP", s_pa[:, :], pa[r0:r0 + 128, :])
                if pb is not None:
                    dma(SY, "LP", s_pb[:, :], pb[r0:r0 + 128, :])
                tok = dma(SY, "LP", s_xr[:, :], x_res[r0:r0 + 128, :])
                V.wait_ge(*tok)
                if pb is not None:
                    V.tensor_add(res[:, :], s_pa[:, :], s_pb[:, :])
                    gV()
                    V.tensor_add(res[:, :], res[:, :], s_xr[:, :])
                else:
                    V.tensor_add(res[:, :], s_pa[:, :], s_xr[:, :])
                gV()
                V.reduce_sum(st1[:, :], res[:, :], axis=AX.X)
                gV()
                V.tensor_scalar_mul(st1[:, :], st1[:, :], 1.0 / 1024.0)
                gV()
                V.tensor_scalar(out=xc[:, :], in0=res[:, :],
                                scalar1=st1[:, 0:1], scalar2=None,
                                op0=ALU.subtract)
                incV()
                spa_free = (sVV, c["vv"])
                S.wait_ge(sVV, c["vv"])
                S.activation(out=sqd[:, :], in_=xc[:, :], func=AF.Square,
                             accum_out=st2[:, 0:1])
                gS()
                S.mul(st2[:, 0:1], st2[:, 0:1], 1.0 / 1024.0)
                gS()
                S.activation(out=st2[:, 0:1], in_=st2[:, 0:1], func=AF.Ln,
                             bias=epsb[:, 0:1])
                gS()
                S.activation(out=st2[:, 0:1], in_=st2[:, 0:1], func=AF.Exp,
                             scale=halfneg[:, 0:1])
                incS()
                V.wait_ge(sSS, c["ss"])
                if q > 0 and tt == 0:
                    V.wait_ge(sMM, xnT_prev_mm)
                V.tensor_scalar_mul(xn_f32[tt][:, :], xc[:, :], st2[:, 0:1])
                gV()
                V.tensor_copy(xn_bf[:, :], xn_f32[tt][:, :])
                incV()
                xn_vv = c["vv"]
                dV()
                SY.wait_ge(sVV, c["vv"])
                if xnt_read_tok is not None:
                    SY.wait_ge(*xnt_read_tok)
                for k in range(8):
                    tok = dma_t(SY, "LT",
                                xnT[k][:, tt * 128:(tt + 1) * 128],
                                xn_bf[:, k * 128:(k + 1) * 128])
                xnT_tt_tok = tok
                if SKIP_ROUTER:
                    V.memset(wselT[0:32, tt * 128:(tt + 1) * 128], 0.125)
                    incV()
                    wsel_tt_tok = (sVV, c["vv"])
                    continue
                wtok(T, ppA_frees[0])
                if q == 0 and tt == 0:
                    T.wait_ge(sem["STW"], c["STW"])
                T.wait_ge(*xnT_tt_tok)
                for k in range(8):
                    T.matmul(ppA[:, 0:32],
                             xnT[k][:, tt * 128:(tt + 1) * 128],
                             routerbf[:, k * 32:(k + 1) * 32],
                             start=(k == 0), stop=(k == 7))
                incM()
                V.wait_ge(sMM, c["mm"])
                V.tensor_add(sco[:, :], ppA[:, 0:32], rbrep[:, :])
                gV()
                ppA_frees[0] = (sVV, c["vv"])
                V.reduce_max(st1[:, :], sco[:, :], axis=AX.X)
                gV()
                V.tensor_scalar_mul(st1[:, :], st1[:, :], -1.0)
                incV()
                S.wait_ge(sVV, c["vv"])
                S.activation(out=sco[:, :], in_=sco[:, :], func=AF.Exp,
                             bias=st1[:, 0:1])
                incS()
                V.wait_ge(sSS, c["ss"])
                V.reduce_sum(st2[:, :], sco[:, :], axis=AX.X)
                gV()
                V.reciprocal(st1[:, :], st2[:, :])
                gV()
                V.tensor_scalar_mul(sco[:, :], sco[:, :], st1[:, 0:1])
                V.tensor_copy(scow[:, :], sco[:, :])
                V.memset(msk[:, :], 0.0)
                for _ in range(4):
                    gV()
                    V.reduce_max(st1[:, :], scow[:, :], axis=AX.X)
                    gV()
                    V.tensor_single_scalar(out=ind[:, :], in_=scow[:, :],
                                           scalar=st1[:, 0:1], op=ALU.is_ge)
                    gV()
                    V.tensor_add(msk[:, :], msk[:, :], ind[:, :])
                    V.scalar_tensor_tensor(
                        out=scow[:, :], in0=ind[:, :], scalar=-1e30,
                        in1=scow[:, :], op0=ALU.mult, op1=ALU.add)
                V.tensor_mul(sco[:, :], sco[:, :], msk[:, :])
                gV()
                V.reduce_sum(st1[:, :], sco[:, :], axis=AX.X)
                gV()
                V.tensor_scalar_max(st1[:, :], st1[:, :], 1e-12)
                gV()
                V.reciprocal(st2[:, :], st1[:, :])
                gV()
                V.tensor_scalar_mul(sco[:, :], sco[:, :], st2[:, 0:1])
                gV()
                V.tensor_copy(wsel_bf[:, 0:32], sco[:, :])
                incV()
                if DEBUG_B and q == 0:
                    V.tensor_copy(dbw[:, tt * 32:(tt + 1) * 32], sco[:, :])
                    incV()
                    GP.wait_ge(sVV, c["vv"])
                    dma(GP, "DBG", dbg["dbg_xn"][tt * 128:(tt + 1) * 128, :],
                        xn_f32[tt][:, :])
                    if tt == 3:
                        dma(GP, "DBG",
                            dbg["dbg_wsel"].rearrange("(a p) e -> p a e", a=4),
                            dbw[:, :].rearrange("p (a e) -> p a e", a=4))
                dV()
                SY.wait_ge(sVV, c["vv"])
                wtok(SY, wselT_read_tok)
                wsv_tok = dma_t(SY, "LT",
                                wselT[:, tt * 128:(tt + 1) * 128],
                                wsel_bf[:, :])
                if DEBUG_B and q == 0 and tt == 0:
                    V.tensor_copy(dbu[0:32, 0:128], sl)
                    V.tensor_copy(dbu[0:128, 128:160], wsel_bf[:, :])
                    incV()
                    GP.wait_ge(sVV, c["vv"])
                    dma(GP, "DBG", dbg["dbg_slw"], dbu[0:32, 0:128])
                    dma(GP, "DBG", dbg["dbg_wbf"], dbu[0:128, 128:160])
                wsel_tt_tok = wsv_tok
            all_xnT_tok = (sem["LT"], c["LT"])

            if DEBUG_B and q == 0:
                V.wait_ge(*wsel_tt_tok)
                V.tensor_copy(dbw2[0:32, :], wselT[0:32, :])
                incV()
                GP.wait_ge(sVV, c["vv"])
                dma(GP, "DBG", dbg["dbg_wt"], dbw2[0:32, :])
            GP.wait_ge(*wsel_tt_tok)
            if prev_h1_vv is not None:
                GP.wait_ge(sVV, prev_h1_vv)
            if not SKIP_WSCR:
                wscr_tok = dma(GP, "STS", wscr[:, :], wselT[0:32, :])
                SY.wait_ge(*wscr_tok)

            h1_vv = [None] * 32
            for e in range(0 if SKIP_MOE else NEXPERT):
                i = e % 2
                key = "LG%d" % i
                if e == 0 and q == 0:
                    SY.wait_ge(*wcast_tok)
                wtok(SY, ge_free[i])
                wtok(SY, ue_free[i])
                wtok(SY, wr_free[i])
                dma(SY, key, ge[i][:, :].rearrange("p (k j) -> p k j", k=8),
                    gbs[:, e * 128:(e + 1) * 128].rearrange(
                        "(k p) j -> p k j", k=8))
                dma(SY, key, ue[i][:, :].rearrange("p (k j) -> p k j", k=8),
                    ubs[:, e * 128:(e + 1) * 128].rearrange(
                        "(k p) j -> p k j", k=8))
                if SKIP_WR:
                    V.memset(wr[i][:, :], 0.5)
                    incV()
                    grp_tok = (sem[key], c[key])
                else:
                    grp_tok = dma(SY, key, wr[i][:, :],
                                  wscr[e:e + 1, :].broadcast_to([128, 512]))
                if e == 0 and q == 0:
                    T.wait_ge(*wcast_tok)
                pa_ = ppAs[i]
                pb_ = ppBs[i]
                wtok(T, ppA_frees[i])
                T.wait_ge(*grp_tok)
                if e == 0:
                    T.wait_ge(*all_xnT_tok)
                for k in range(8):
                    T.matmul(pa_[:, :], ge[i][:, k * 128:(k + 1) * 128],
                             xnT[k][:, :], start=(k == 0), stop=(k == 7))
                incM()
                ge_free[i] = (sMM, c["mm"])
                wtok(T, ppB_frees[i])
                for k in range(8):
                    T.matmul(pb_[:, :], ue[i][:, k * 128:(k + 1) * 128],
                             xnT[k][:, :], start=(k == 0), stop=(k == 7))
                incM()
                ue_free[i] = (sMM, c["mm"])
                S.wait_ge(sMM, c["mm"] - 1)
                S.activation(out=sg2[i][:, :], in_=pa_[:, :], func=AF.Silu)
                incS()
                ppA_frees[i] = (sSS, c["ss"])
                S.wait_ge(sMM, c["mm"])
                S.copy(ub2[i][:, :], pb_[:, :])
                incS()
                dS()
                ppB_frees[i] = (sSS, c["ss"])
                ub_ss = c["ss"]
                V.wait_ge(sSS, ub_ss)
                V.wait_ge(*grp_tok)
                V.tensor_mul(h1[e][:, :], sg2[i][:, :], ub2[i][:, :])
                gV()
                V.tensor_mul(h1[e][:, :], h1[e][:, :], wr[i][:, :])
                incV()
                h1_vv[e] = c["vv"]
                wr_free[i] = (sVV, c["vv"])
                if DEBUG_B and q == 0 and e == 0:
                    V.tensor_copy(dbsb[:, :], ge[0][:, :])
                    V.tensor_copy(dbw2[:, :], wr[0][:, :])
                    V.tensor_copy(dbu[:, :], ub[:, :])
                    incV()
                    GP.wait_ge(sVV, c["vv"])
                    dma(GP, "DBG", dbg["dbg_h1"], h1[0][:, :])
                    dma(GP, "DBG", dbg["dbg_sg"], sg[:, :])
                    dma(GP, "DBG", dbg["dbg_xnT"], xnT[0][:, :])
                    dma(GP, "DBG", dbg["dbg_ge"], dbsb[:, :])
                    dma(GP, "DBG", dbg["dbg_wr"], dbw2[:, :])
                    dma(GP, "DBG", dbg["dbg_ub"], dbu[:, :])
                    dbg_tok = (sem["DBG"], c["DBG"])
                else:
                    dbg_tok = None
            prev_h1_vv = c["vv"] if not SKIP_MOE else None

            for st_ in ([] if SKIP_SHARED else range(4)):
                wtok(T, ppA_frees[0])
                if q == 0 and st_ == 0:
                    T.wait_ge(sem["STW"], c["STW"])
                for k in range(8):
                    T.matmul(ppA[:, :],
                             sgatebf[:, k * 512 + st_ * 128:
                                     k * 512 + (st_ + 1) * 128],
                             xnT[k][:, :], start=(k == 0), stop=(k == 7))
                incM()
                S.wait_ge(sMM, c["mm"])
                S.activation(out=hsg[:, :], in_=ppA[:, :], func=AF.Silu)
                incS()
                ppA_frees[0] = (sSS, c["ss"])
                wtok(T, ppB_frees[0])
                for k in range(8):
                    T.matmul(ppB[:, :],
                             supbf[:, k * 512 + st_ * 128:
                                   k * 512 + (st_ + 1) * 128],
                             xnT[k][:, :], start=(k == 0), stop=(k == 7))
                incM()
                S.wait_ge(sMM, c["mm"])
                S.copy(ub2[0][:, :], ppB[:, :])
                incS()
                dS()
                ppB_frees[0] = (sSS, c["ss"])
                V.wait_ge(sSS, c["ss"])
                V.tensor_mul(hs[st_][:, :], hsg[:, :], ub2[0][:, :])
                incV()
            hs_vv = c["vv"]
            if SKIP_SHARED:
                for st_ in range(4):
                    V.memset(hs[st_][:, :], 0.01)
                incV()
                hs_vv = c["vv"]
            if DEBUG_B and q == 0:
                GP.wait_ge(sVV, hs_vv)
                dma(GP, "DBG", dbg["dbg_hs"], hs[0][:, :])

            if SKIP_DOWN:
                for tt in range(4):
                    V.memset(res2[tt][:, :], 0.1)
                    incV()
                    ppd_free[tt] = (sVV, c["vv"])
                down_mm = c["mm"]
            for fh in ([] if SKIP_DOWN else range(2)):
                for k in list(range(NEXPERT)) + list(range(32, 36)):
                    if k < 32:
                        i = k % 2
                        if q == 0 and fh == 0 and k == 0:
                            SY.wait_ge(*wcast_tok)
                        wtok(SY, dbf_free[i])
                        ftok = dma(SY, "LF%d" % i, dbf[i][:, :],
                                   dbs[k * 128:(k + 1) * 128,
                                       fh * 512:(fh + 1) * 512])
                        T.wait_ge(*ftok)
                        if fh == 0 and k == 0:
                            T.wait_ge(sVV, hs_vv)
                        lhs_full = h1[k]
                        rhs = dbf[i][:, :]
                    else:
                        lhs_full = hs[k - 32]
                        rhs = sdownbf[:, (k - 32) * 1024 + fh * 512:
                                      (k - 32) * 1024 + (fh + 1) * 512]
                    for tt in range(4):
                        if k == 0:
                            wtok(T, ppd_free[tt])
                        T.matmul(ppd[tt][:, :],
                                 lhs_full[:, tt * 128:(tt + 1) * 128],
                                 rhs, start=(k == 0), stop=(k == 35))
                    if k < 32:
                        incM()
                        dbf_free[i] = (sMM, c["mm"])
                incM()
                down_mm = c["mm"]
                for tt in range(4):
                    V.wait_ge(sMM, down_mm)
                    V.scalar_tensor_tensor(
                        out=res2[tt][:, fh * 512:(fh + 1) * 512],
                        in0=ppd[tt][:, :], scalar=1.0,
                        in1=xn_f32[tt][:, fh * 512:(fh + 1) * 512],
                        op0=ALU.mult, op1=ALU.add)
                    incV()
                    ppd_free[tt] = (sVV, c["vv"])
            xnT_prev_mm = down_mm
            xnt_read_tok = (sMM, down_mm)
            wselT_read_tok = wscr_tok if not SKIP_WSCR else wsel_tt_tok
            if DEBUG_B and q == 0:
                GP.wait_ge(sVV, c["vv"])
                dma(GP, "DBG", dbg["dbg_res2"], res2[0][:, :])

            for tt in range(4):
                o = tt % 2
                V.reduce_sum(st1[:, :], res2[tt][:, :], axis=AX.X)
                gV()
                V.tensor_scalar_mul(st1[:, :], st1[:, :], 1.0 / 1024.0)
                gV()
                V.tensor_scalar(out=xc[:, :], in0=res2[tt][:, :],
                                scalar1=st1[:, 0:1], scalar2=None,
                                op0=ALU.subtract)
                incV()
                S.wait_ge(sVV, c["vv"])
                S.activation(out=sqd[:, :], in_=xc[:, :], func=AF.Square,
                             accum_out=st2[:, 0:1])
                gS()
                S.mul(st2[:, 0:1], st2[:, 0:1], 1.0 / 1024.0)
                gS()
                S.activation(out=st2[:, 0:1], in_=st2[:, 0:1], func=AF.Ln,
                             bias=epsb[:, 0:1])
                gS()
                S.activation(out=st2[:, 0:1], in_=st2[:, 0:1], func=AF.Exp,
                             scale=halfneg[:, 0:1])
                incS()
                V.wait_ge(sSS, c["ss"])
                wtok(V, outb_free[o])
                V.tensor_scalar_mul(outb[o][:, :], xc[:, :], st2[:, 0:1])
                incV()
                dV()
                GP.wait_ge(sVV, c["vv"])
                outb_free[o] = dma(
                    GP, "STO%d" % o,
                    out[row0 + tt * 128:row0 + (tt + 1) * 128, :],
                    outb[o][:, :])

        GP.wait_ge(sem["STO0"], c["STO0"])
        GP.wait_ge(sem["STO1"], c["STO1"])
        if shared is None:
            _emit_blocks(nc, prog)
    return nc


def build_AB():
    nc = bass.Bass("TRN2", target_bir_lowering=False,
                   detect_race_conditions=False)
    cc_in = nc.dram_tensor("cc_in", [4096, 1024], F32).ap()
    cc_out = nc.dram_tensor("cc_out", [2048, 1024], F32).ap()
    gate_t = nc.dram_tensor("gate_t", [1024, 4096], F32,
                            kind="ExternalInput").ap()
    up_t = nc.dram_tensor("up_t", [1024, 4096], F32,
                          kind="ExternalInput").ap()
    down_f = nc.dram_tensor("down_f", [4096, 1024], F32,
                            kind="ExternalInput").ap()
    gbs = nc.dram_tensor("gbs", [1024, 4096], BF16).ap()
    ubs = nc.dram_tensor("ubs", [1024, 4096], BF16).ap()
    dbs = nc.dram_tensor("dbs", [4096, 1024], BF16).ap()
    cast_jobs = []
    for (src, dsc, ntile) in ((gate_t, gbs, 8), (up_t, ubs, 8),
                              (down_f, dbs, 32)):
        for it in range(ntile):
            cast_jobs.append((dsc[it * 128:(it + 1) * 128, :],
                              src[it * 128:(it + 1) * 128, :]))
    prog = {e: [] for e in ENGINES}
    es = contextlib.ExitStack()
    with es:
        ccsem = es.enter_context(nc.semaphore("sCC"))
        shared = dict(nc=nc, prog=prog, es=es, cc_in=cc_in, cc_out=cc_out,
                      ccsem=ccsem, cast_jobs=cast_jobs, scr_tok=None,
                      ffn_dram=dict(gate_t=gate_t, up_t=up_t, down_f=down_f,
                                    gbs=gbs, ubs=ubs, dbs=dbs))
        build_A(shared)
        assert not cast_jobs
        GP = Rec(prog, "gpsimd")
        GP.collective_compute(
            "ReduceScatter", ALU.add,
            replica_groups=[[0, 1], [2, 3], [4, 5], [6, 7]],
            ins=[cc_in.opt()], outs=[cc_out.opt()]).then_inc(ccsem, 1)
        build_B(shared)
        _emit_blocks(nc, prog)
    return nc



def _tile_rows(a, k):
    a = np.asarray(a)
    return np.ascontiguousarray(
        a.reshape(k, 128, -1).transpose(1, 0, 2).reshape(128, -1))


def kernel(L0_attn_x, L0_attn_in_proj_w, L0_attn_conv_w, L0_attn_conv_b,
           L0_attn_x_proj_w, L0_attn_dt_proj_w, L0_attn_dt_proj_b,
           L0_attn_A_log, L0_attn_D, L0_attn_out_proj_w, L0_ffn_router_w,
           L0_ffn_router_bias, L0_ffn_gate_w, L0_ffn_up_w, L0_ffn_down_w,
           L0_ffn_sgate_w, L0_ffn_sup_w, L0_ffn_sdown_w):
    f = lambda a: np.ascontiguousarray(np.asarray(a, dtype=np.float32))
    x = f(L0_attn_x)
    w_in = f(L0_attn_in_proj_w)
    conv_w = f(L0_attn_conv_w)
    conv_b = f(L0_attn_conv_b)
    x_proj = f(L0_attn_x_proj_w)
    dt_w = f(L0_attn_dt_proj_w)
    dt_b = f(L0_attn_dt_proj_b)
    a_log = f(L0_attn_A_log)
    d_vec = f(L0_attn_D)
    out_w = f(L0_attn_out_proj_w)

    fused = True
    ncA = build_AB() if fused else build_A()
    mapsA = []
    for core in range(8):
        b, h = divmod(core, 2)
        perm = np.concatenate([np.arange(h * 1024, (h + 1) * 1024),
                               np.arange((1 - h) * 1024, (1 - h) * 1024 + 1024)])
        m = {
            "x_fm": np.ascontiguousarray(x[b].T),
            "w_in_t": np.ascontiguousarray(
                np.concatenate([w_in[0:2048][perm],
                                w_in[2048 + h * 1024:2048 + (h + 1) * 1024]],
                               axis=0).T),
            "conv_w": _tile_rows(conv_w[perm], 16),
            "conv_b": _tile_rows(conv_b[perm][:, None], 16),
            "xproj_t": _tile_rows(x_proj.T[perm], 16),
            "dtw_t": np.ascontiguousarray(
                dt_w[h * 1024:(h + 1) * 1024].T),
            "dt_b": _tile_rows(dt_b[h * 1024:(h + 1) * 1024][:, None], 8),
            "a_log": _tile_rows(a_log[h * 1024:(h + 1) * 1024], 8),
            "d_h": _tile_rows(d_vec[h * 1024:(h + 1) * 1024][:, None], 8),
            "outw_t": np.ascontiguousarray(
                out_w[:, h * 1024:(h + 1) * 1024].T),
        }
        mapsA.append(m)

    router_w = f(L0_ffn_router_w)
    router_b = f(L0_ffn_router_bias)
    gate_w = f(L0_ffn_gate_w)
    up_w = f(L0_ffn_up_w)
    down_w = f(L0_ffn_down_w)
    sgate_w = f(L0_ffn_sgate_w)
    sup_w = f(L0_ffn_sup_w)
    sdown_w = f(L0_ffn_sdown_w)

    gate_t = np.ascontiguousarray(gate_w.reshape(4096, 1024).T)
    up_t = np.ascontiguousarray(up_w.reshape(4096, 1024).T)
    down_flat = np.ascontiguousarray(
        down_w.transpose(0, 2, 1).reshape(4096, 1024))
    shared_w = {
        "router_t": _tile_rows(router_w.T, 8),
        "rbias": np.ascontiguousarray(router_b[None, :]),
        "gate_t": gate_t,
        "up_t": up_t,
        "down_f": down_flat,
        "sgate_t": _tile_rows(sgate_w.T, 8),
        "sup_t": _tile_rows(sup_w.T, 8),
        "sdown_t": _tile_rows(sdown_w.T, 4),
        "ident": np.eye(128, dtype=np.float32).astype(ml_dtypes.bfloat16),
    }
    maps = []
    for core in range(8):
        b, h = divmod(core, 2)
        m = dict(shared_w)
        m.update(mapsA[core])
        m["x_res"] = np.ascontiguousarray(x[b, h * 2048:(h + 1) * 2048])
        maps.append(m)
    res = run_bass_kernel_spmd(ncA, maps, core_ids=list(range(8)))

    out = np.empty((4, 4096, 1024), dtype=np.float32)
    for core in range(8):
        b, h = divmod(core, 2)
        out[b, h * 2048:(h + 1) * 2048] = np.asarray(res.results[core]["out"])
    return out


# version 06
# speedup vs baseline: 1.1615x; 1.0432x over previous
import contextlib

import numpy as np
import ml_dtypes
import concourse.bass as bass
import concourse.mybir as mybir
from concourse.bass_utils import run_bass_kernel_spmd

F32 = mybir.dt.float32
BF16 = mybir.dt.bfloat16
AF = mybir.ActivationFunctionType
ALU = mybir.AluOpType
AX = mybir.AxisListType

QT = 512
NCHUNK = 8
NQUARTER = 4
DEBUG_B = False
DEBUG_A = False
NEXPERT = 32
SKIP_WR = False
SKIP_MOE = False
SKIP_DOWN = False
SKIP_SHARED = False
SKIP_ROUTER = False
SKIP_TRANS = False
SKIP_WSCR = False
ENGINES = ("sync", "gpsimd", "tensor", "vector", "scalar")


class Rec:

    def __init__(self, prog, name):
        self._prog = prog
        self._name = name

    def __getattr__(self, meth):
        prog = self._prog[self._name]

        def call(*a, **k):
            entry = [meth, a, k, []]
            prog.append(entry)

            class _C:
                def then_inc(self, sem, v):
                    entry[3].append((sem, v))
                    return self

            return _C()

        return call


def _replay(eng, entries):
    for meth, a, k, incs in entries:
        r = getattr(eng, meth)(*a, **k)
        for sem, v in incs:
            r = r.then_inc(sem, v)


def _emit_blocks(nc, prog):
    with nc.Block() as block:
        @block.sync
        def _(eng):
            _replay(eng, prog["sync"])

        @block.gpsimd
        def _(eng):
            _replay(eng, prog["gpsimd"])

        @block.tensor
        def _(eng):
            _replay(eng, prog["tensor"])

        @block.vector
        def _(eng):
            _replay(eng, prog["vector"])

        @block.scalar
        def _(eng):
            _replay(eng, prog["scalar"])



def build_A(shared=None):
    if shared is None:
        nc = bass.Bass("TRN2", target_bir_lowering=False,
                       detect_race_conditions=False)
    else:
        nc = shared["nc"]

    x_fm = nc.dram_tensor("x_fm", [1024, 4096], F32, kind="ExternalInput").ap()
    w_in_t = nc.dram_tensor("w_in_t", [1024, 3072], F32, kind="ExternalInput").ap()
    conv_w = nc.dram_tensor("conv_w", [128, 64], F32, kind="ExternalInput").ap()
    conv_b = nc.dram_tensor("conv_b", [128, 16], F32, kind="ExternalInput").ap()
    xproj_t = nc.dram_tensor("xproj_t", [128, 1536], F32, kind="ExternalInput").ap()
    dtw_t = nc.dram_tensor("dtw_t", [64, 1024], F32, kind="ExternalInput").ap()
    dt_b = nc.dram_tensor("dt_b", [128, 8], F32, kind="ExternalInput").ap()
    a_log = nc.dram_tensor("a_log", [128, 128], F32, kind="ExternalInput").ap()
    d_h = nc.dram_tensor("d_h", [128, 8], F32, kind="ExternalInput").ap()
    outw_t = nc.dram_tensor("outw_t", [1024, 1024], F32, kind="ExternalInput").ap()

    if shared is None:
        partial = nc.dram_tensor("partial", [4096, 1024], F32,
                                 kind="ExternalOutput").ap()
        cast_jobs = []
    else:
        partial = shared["cc_in"]
        cast_jobs = shared["cast_jobs"]
    dbg = {}
    if DEBUG_A:
        for nm, shp in (("dbg_u0", [128, 512]), ("dbg_sz0", [128, 512]),
                        ("dbg_dtr", [64, 512]), ("dbg_bcr", [32, 512]),
                        ("dbg_delta", [128, 512]), ("dbg_du", [128, 512]),
                        ("dbg_brep", [128, 512]), ("dbg_dA", [128, 512]),
                        ("dbg_hb", [128, 4096]), ("dbg_y", [128, 512]),
                        ("dbg_gated", [128, 512])):
            dbg[nm] = nc.dram_tensor(nm, shp, F32, kind="ExternalOutput").ap()
    bct_h = nc.dram_tensor("bct", [32, QT], BF16)
    bct = bct_h.ap()

    if shared is None:
        prog = {e: [] for e in ENGINES}
    else:
        prog = shared["prog"]
    SY, GP, T, V, S = (Rec(prog, e) for e in ENGINES)

    es = contextlib.ExitStack()
    sem_es = shared["es"] if shared is not None else es
    with es:
        def sb(name, shape, dt):
            return es.enter_context(nc.sbuf_tensor(name, shape, dt))

        cw_all = sb("cw_all", [128, 64], F32)
        cb_all = sb("cb_all", [128, 16], F32)
        dtb_all = sb("dtb_all", [128, 8], F32)
        dh_all = sb("dh_all", [128, 8], F32)
        aneg = sb("aneg", [128, 128], F32)
        hstate = sb("hstate", [128, 128], F32)
        halo = sb("halo", [128, 48], BF16)
        winbf = [sb(f"winbf{k}", [128, 3072], BF16) for k in range(8)]
        outwbf = [sb(f"outwbf{k}", [128, 1024], BF16) for k in range(8)]
        xprojbf = sb("xprojbf", [128, 1536], BF16)
        dtwbf = sb("dtwbf", [64, 1024], BF16)

        xf32 = [sb(f"xf32_{i}", [128, QT], F32) for i in range(2)]
        xbf = [sb(f"xbf{k}", [128, QT], BF16) for k in range(8)]
        xitmp = [sb(f"xitmp{i}", [128, QT + 3], BF16) for i in range(2)]
        cacc = [sb(f"cacc{i}", [128, QT], F32) for i in range(2)]
        u = [sb(f"u{i}", [128, QT], BF16) for i in range(8)]
        utmp = [sb(f"utmp{i}", [128, QT], BF16) for i in range(2)]
        sz = [sb(f"sz{i}", [128, QT], BF16) for i in range(8)]
        dtr_bf = sb("dtr_bf", [64, QT], BF16)
        bcr_bf = sb("bcr_bf", [32, QT], BF16)
        brep = sb("brep", [128, 16 * QT], BF16)
        crep = sb("crep", [128, 16 * QT], BF16)
        spe = sb("spe", [128, QT], F32)
        delta = sb("delta", [128, QT], F32)
        du_bf = sb("du_bf", [128, QT], BF16)
        dA = [sb(f"dA{i}", [128, QT], F32) for i in range(4)]
        p_bf = sb("p_bf", [128, 4 * QT], BF16)
        hb = sb("hb", [128, 8 * QT], BF16)
        t2 = sb("t2", [128, 8 * QT], BF16)
        ytmp = sb("ytmp", [128, QT], BF16)
        y_f32 = sb("y_f32", [128, QT], F32)
        yg = sb("yg", [128, QT], F32)
        gated = [sb(f"gated{i}", [128, QT], BF16) for i in range(8)]
        oevac = [sb(f"oevac{i}", [128, QT], F32) for i in range(2)]

        pin = [es.enter_context(nc.psum_tensor(f"pin{i}", [128, QT], F32))
               for i in range(2)]
        px = es.enter_context(nc.psum_tensor("px", [96, QT], F32))
        pdt = es.enter_context(nc.psum_tensor("pdt", [128, QT], F32))
        pout = [es.enter_context(nc.psum_tensor(f"pout{i}", [128, QT], F32))
                for i in range(2)]

        sem = {}
        c = {}
        for name in ("LX0", "LX1", "LW", "LB", "LC", "STB", "STP0", "STP1",
                     "DBG"):
            sem[name] = sem_es.enter_context(nc.semaphore("sA" + name))
            c[name] = 0
        sMM = sem_es.enter_context(nc.semaphore("sAMM"))
        sVV = sem_es.enter_context(nc.semaphore("sAVV"))
        sSS = sem_es.enter_context(nc.semaphore("sASS"))
        c.update(mm=0, vv=0, ss=0)

        def dma(eng, key, out, in_):
            eng.dma_start(out=out, in_=in_).then_inc(sem[key], 16)
            c[key] += 16
            return (sem[key], c[key])

        def incM():
            prog["tensor"][-1][3].append((sMM, 1))
            c["mm"] += 1

        def incV():
            prog["vector"][-1][3].append((sVV, 1))
            c["vv"] += 1

        def incS():
            prog["scalar"][-1][3].append((sSS, 1))
            c["ss"] += 1

        def gV():
            if not prog["vector"][-1][3]:
                incV()
            V.wait_ge(sVV, c["vv"])

        def dV():
            V.drain()
            V.sem_inc(sVV, 1)
            c["vv"] += 1

        def dS():
            S.drain()
            S.sem_inc(sSS, 1)
            c["ss"] += 1

        def gS():
            if not prog["scalar"][-1][3]:
                incS()
            S.wait_ge(sSS, c["ss"])

        def wtok(Rr, tok):
            if tok is not None:
                Rr.wait_ge(tok[0], tok[1])

        dma(SY, "LW", cw_all[:, :], conv_w)
        dma(SY, "LW", cb_all[:, :], conv_b)
        dma(SY, "LW", dtb_all[:, :], dt_b)
        dma(SY, "LW", dh_all[:, :], d_h)
        tok = dma(SY, "LW", cacc[0][:, 0:128], a_log)
        S.wait_ge(*tok)
        S.activation(out=aneg[:, :], in_=cacc[0][:, 0:128], func=AF.Exp)
        incS()
        dS()
        V.wait_ge(sSS, c["ss"])
        V.memset(hstate[:, :], 0.0)
        V.memset(halo[:, :], 0.0)
        V.tensor_scalar_mul(aneg[:, :], aneg[:, :], -1.0)
        incV()
        aneg_done = c["vv"]

        dma(GP, "STB", xprojbf[:, :], xproj_t)
        dma(GP, "STB", dtwbf[:, :], dtw_t)
        for k in range(8):
            dma(GP, "STB", winbf[k][:, :], w_in_t[k * 128:(k + 1) * 128, :])
        for k in range(8):
            dma(GP, "STB", outwbf[k][:, :], outw_t[k * 128:(k + 1) * 128, :])
        wtok_a = (sem["STB"], c["STB"])

        pin_free = [None, None]
        pout_free = [None, None]
        oevac_free = [None, None]
        utmp_free = [None, None]
        prev_gated_vv = [None] * 8
        prev_scan_all_vv = None
        cast_done = [0] * 8
        st = {"seq": 0, "pdt_free": None}

        def emit_xload_cast(ch):
            col0 = ch * QT
            for k in range(8):
                if k >= 2 or ch > 0:
                    SY.wait_ge(sVV, cast_done[(k - 2) % 8])
                tok = dma(SY, "LX%d" % (k % 2), xf32[k % 2][:, :],
                          x_fm[k * 128:(k + 1) * 128, col0:col0 + QT])
                V.wait_ge(*tok)
                if ch > 0 and k == 0:
                    V.wait_ge(sMM, st["p1mm%d" % (ch - 1)])
                V.tensor_copy(xbf[k][:, :], xf32[k % 2][:, :])
                incV()
                cast_done[k] = c["vv"]
            st["cast%d" % ch] = c["vv"]

        def emit_piece(ch, mt):
            pi = st["seq"] % 2
            st["seq"] += 1
            wtok(T, pin_free[pi])
            if mt == 0:
                T.wait_ge(sVV, st["cast%d" % ch])
                if ch == 0:
                    T.wait_ge(*wtok_a)
            for k in range(8):
                T.matmul(pin[pi][:, :],
                         winbf[k][:, mt * 128:(mt + 1) * 128],
                         xbf[k][:, :], start=(k == 0), stop=(k == 7))
            incM()
            in_mm = c["mm"]
            if mt == 23:
                st["p1mm%d" % ch] = in_mm
            if mt < 16:
                V.wait_ge(sMM, in_mm)
                V.tensor_copy(xitmp[pi][:, 0:3],
                              halo[:, mt * 3:mt * 3 + 3])
                V.tensor_copy(xitmp[pi][:, 3:3 + QT], pin[pi][:, :])
                gV()
                V.tensor_copy(halo[:, mt * 3:mt * 3 + 3],
                              xitmp[pi][:, QT:QT + 3])
                V.tensor_scalar_mul(cacc[0][:, :], xitmp[pi][:, 0:QT],
                                    cw_all[:, mt * 4:mt * 4 + 1])
                gV()
                V.scalar_tensor_tensor(
                    out=cacc[1][:, :], in0=xitmp[pi][:, 1:1 + QT],
                    scalar=cw_all[:, mt * 4 + 1:mt * 4 + 2],
                    in1=cacc[0][:, :], op0=ALU.mult, op1=ALU.add)
                gV()
                V.scalar_tensor_tensor(
                    out=cacc[0][:, :], in0=xitmp[pi][:, 2:2 + QT],
                    scalar=cw_all[:, mt * 4 + 2:mt * 4 + 3],
                    in1=cacc[1][:, :], op0=ALU.mult, op1=ALU.add)
                gV()
                V.scalar_tensor_tensor(
                    out=cacc[1][:, :], in0=xitmp[pi][:, 3:3 + QT],
                    scalar=cw_all[:, mt * 4 + 3:mt * 4 + 4],
                    in1=cacc[0][:, :], op0=ALU.mult, op1=ALU.add)
                incV()
                pin_free[pi] = (sVV, c["vv"])
                conv_vv = c["vv"]
                if mt < 8:
                    udst = u[mt]
                    if ch > 0:
                        S.wait_ge(sVV, prev_gated_vv[mt])
                else:
                    udst = utmp[mt % 2]
                    wtok(S, utmp_free[mt % 2])
                S.wait_ge(sVV, conv_vv)
                S.activation(out=udst[:, :], in_=cacc[1][:, :],
                             func=AF.Silu, bias=cb_all[:, mt:mt + 1])
                incS()
                dS()
                T.wait_ge(sSS, c["ss"])
                T.matmul(px[:, :], xprojbf[:, mt * 96:(mt + 1) * 96],
                         udst[:, :], start=(mt == 0), stop=(mt == 15))
                incM()
                if mt >= 8:
                    utmp_free[mt % 2] = (sMM, c["mm"])
            else:
                dt = mt - 16
                if ch > 0:
                    S.wait_ge(sVV, prev_gated_vv[dt])
                S.wait_ge(sMM, in_mm)
                S.activation(out=sz[dt][:, :], in_=pin[pi][:, :],
                             func=AF.Silu)
                incS()
                dS()
                pin_free[pi] = (sSS, c["ss"])

        def emit_dtsp(dt, bc_vv, first):
            wtok(T, st["pdt_free"])
            if dt == 0:
                T.wait_ge(sVV, bc_vv)
            T.matmul(pdt[:, :], dtwbf[:, dt * 128:(dt + 1) * 128],
                     dtr_bf[:, :], start=True, stop=True)
            incM()
            S.wait_ge(sMM, c["mm"])
            S.activation(out=spe[:, :], in_=pdt[:, :], func=AF.Exp,
                         bias=dtb_all[:, dt:dt + 1])
            gS()
            S.activation(out=delta[:, :], in_=spe[:, :], func=AF.Ln,
                         bias=1.0)
            if first:
                S.wait_ge(sVV, aneg_done)
            incS()
            dS()
            st["pdt_free"] = (sSS, c["ss"])
            st["spss%d" % dt] = c["ss"]

        for ch in range(NCHUNK):
            col0 = ch * QT
            if ch == 0:
                emit_xload_cast(0)
                for mt in range(24):
                    emit_piece(0, mt)

            V.wait_ge(sMM, st["p1mm%d" % ch])
            if ch > 0:
                V.wait_ge(sMM, st["dtmm%d" % (ch - 1)])
            V.tensor_copy(dtr_bf[:, :], px[0:64, :])
            V.tensor_copy(bcr_bf[:, :], px[64:96, :])
            incV()
            dV()
            bc_vv = c["vv"]
            GP.wait_ge(sVV, bc_vv)
            if ch > 0:
                GP.wait_ge(sem["LB"], c["LB"])
                GP.wait_ge(sem["LC"], c["LC"])
            bct_tok = dma(GP, "STB", bct[:, :], bcr_bf[:, :])
            SY.wait_ge(*bct_tok)
            if prev_scan_all_vv is not None:
                SY.wait_ge(sVV, prev_scan_all_vv)
                GP.wait_ge(sVV, prev_scan_all_vv)
            for n in range(16):
                dma(SY, "LB", brep[:, n * QT:(n + 1) * QT],
                    bct[n:n + 1, :].broadcast_to([128, QT]))
                dma(GP, "LC", crep[:, n * QT:(n + 1) * QT],
                    bct[16 + n:17 + n, :].broadcast_to([128, QT]))
            rep_b = (sem["LB"], c["LB"])
            rep_c = (sem["LC"], c["LC"])

            emit_dtsp(0, bc_vv, ch == 0)
            for dt in range(8):
                V.wait_ge(sSS, st["spss%d" % dt])
                V.tensor_mul(du_bf[:, :], delta[:, :], u[dt][:, :])
                gV()
                if dt == 0:
                    V.wait_ge(*rep_b)
                    V.wait_ge(*rep_c)
                scan_vv = {}
                for n in range(16):
                    nb, half = n % 8, n // 8
                    nq4 = n % 4
                    if nq4 == 0:
                        for n2 in range(n, n + 4):
                            V.tensor_mul(p_bf[:, (n2 % 4) * QT:
                                               (n2 % 4 + 1) * QT],
                                         du_bf[:, :],
                                         brep[:, n2 * QT:(n2 + 1) * QT])
                        incV()
                    if n >= 4:
                        S.wait_ge(sVV, scan_vv[n - 4])
                    S.activation(out=dA[n % 4][:, :], in_=delta[:, :],
                                 func=AF.Exp,
                                 scale=aneg[:, dt * 16 + n:dt * 16 + n + 1])
                    incS()
                    dS()
                    V.wait_ge(sSS, c["ss"])
                    V.tensor_tensor_scan(
                        out=hb[:, nb * QT:(nb + 1) * QT],
                        data0=dA[n % 4][:, :],
                        data1=p_bf[:, nq4 * QT:(nq4 + 1) * QT],
                        initial=(0.0 if ch == 0 else
                                 hstate[:, dt * 16 + n:dt * 16 + n + 1]),
                        op0=ALU.mult, op1=ALU.add)
                    incV()
                    scan_vv[n] = c["vv"]
                    if nb == 7:
                        gV()
                        hb3 = hb[:, :].rearrange("p (n t) -> p n t", n=8)
                        V.tensor_copy(
                            hstate[:, dt * 16 + half * 8:
                                   dt * 16 + half * 8 + 8],
                            hb3[:, :, QT - 1])
                        V.tensor_mul(t2[:, :], hb[:, :],
                                     crep[:, half * 8 * QT:(half + 1) * 8 * QT])
                        gV()
                        V.tensor_add(t2[:, 0:4 * QT], t2[:, 0:4 * QT],
                                     t2[:, 4 * QT:8 * QT])
                        gV()
                        V.tensor_add(t2[:, 0:2 * QT], t2[:, 0:2 * QT],
                                     t2[:, 2 * QT:4 * QT])
                        gV()
                        if half == 0:
                            V.tensor_add(y_f32[:, :], t2[:, 0:QT],
                                         t2[:, QT:2 * QT])
                        else:
                            V.tensor_add(ytmp[:, :], t2[:, 0:QT],
                                         t2[:, QT:2 * QT])
                            gV()
                            V.tensor_add(y_f32[:, :], y_f32[:, :], ytmp[:, :])
                        incV()
                prev_scan_all_vv = c["vv"]
                V.wait_ge(sVV, c["vv"])
                V.scalar_tensor_tensor(
                    out=yg[:, :], in0=u[dt][:, :],
                    scalar=dh_all[:, dt:dt + 1], in1=y_f32[:, :],
                    op0=ALU.mult, op1=ALU.add)
                gV()
                V.tensor_mul(gated[dt][:, :], yg[:, :], sz[dt][:, :])
                incV()
                prev_gated_vv[dt] = c["vv"]
                if dt == 0 and ch < NCHUNK - 1:
                    emit_xload_cast(ch + 1)
                if dt < 7:
                    emit_dtsp(dt + 1, bc_vv, False)
                if ch < NCHUNK - 1:
                    for mt2 in (dt, 8 + dt, 16 + dt):
                        emit_piece(ch + 1, mt2)
            st["dtmm%d" % ch] = c["mm"]
            all_gated = prev_gated_vv[7]
            for _ in range(6):
                if cast_jobs:
                    dst_, src_ = cast_jobs.pop(0)
                    shared["scr_tok"] = dma(GP, "STB", dst_, src_)

            for tt in range(4):
                for fh in range(2):
                    i = (tt * 2 + fh) % 2
                    wtok(T, pout_free[i])
                    if tt == 0 and fh == 0:
                        T.wait_ge(sVV, all_gated)
                    for k in range(8):
                        T.matmul(pout[i][:, :],
                                 gated[k][:, tt * 128:(tt + 1) * 128],
                                 outwbf[k][:, fh * QT:(fh + 1) * QT],
                                 start=(k == 0), stop=(k == 7))
                    incM()
                    omm = c["mm"]
                    wtok(S, oevac_free[i])
                    S.wait_ge(sMM, omm)
                    S.copy(oevac[i][:, :], pout[i][:, :])
                    incS()
                    pout_free[i] = (sSS, c["ss"])
                    dS()
                    GP.wait_ge(sSS, c["ss"])
                    oevac_free[i] = dma(
                        GP, "STP%d" % i,
                        partial[col0 + tt * 128:col0 + (tt + 1) * 128,
                                fh * QT:(fh + 1) * QT],
                        oevac[i][:, :])

        GP.wait_ge(sem["STP0"], c["STP0"])
        GP.wait_ge(sem["STP1"], c["STP1"])
        if shared is None:
            _emit_blocks(nc, prog)
    return nc



def build_B(shared=None):
    if shared is None:
        nc = bass.Bass("TRN2", target_bir_lowering=False,
                       detect_race_conditions=False)
        pa = nc.dram_tensor("pa", [2048, 1024], F32,
                            kind="ExternalInput").ap()
        pb = nc.dram_tensor("pb", [2048, 1024], F32,
                            kind="ExternalInput").ap()
    else:
        nc = shared["nc"]
        pa = shared["cc_out"]
        pb = None
    x_res = nc.dram_tensor("x_res", [2048, 1024], F32, kind="ExternalInput").ap()
    router_t = nc.dram_tensor("router_t", [128, 256], F32,
                              kind="ExternalInput").ap()
    rbias = nc.dram_tensor("rbias", [1, 32], F32, kind="ExternalInput").ap()
    if shared is not None:
        fd = shared["ffn_dram"]
        gate_t, up_t, down_f = fd["gate_t"], fd["up_t"], fd["down_f"]
    else:
        gate_t = nc.dram_tensor("gate_t", [1024, 4096], F32,
                                kind="ExternalInput").ap()
        up_t = nc.dram_tensor("up_t", [1024, 4096], F32,
                              kind="ExternalInput").ap()
        down_f = nc.dram_tensor("down_f", [4096, 1024], F32,
                                kind="ExternalInput").ap()
    sgate_t = nc.dram_tensor("sgate_t", [128, 4096], F32,
                             kind="ExternalInput").ap()
    sup_t = nc.dram_tensor("sup_t", [128, 4096], F32, kind="ExternalInput").ap()
    sdown_t = nc.dram_tensor("sdown_t", [128, 4096], F32,
                             kind="ExternalInput").ap()
    ident = nc.dram_tensor("ident", [128, 128], BF16, kind="ExternalInput").ap()

    out = nc.dram_tensor("out", [2048, 1024], F32, kind="ExternalOutput").ap()

    dbg = {}
    if DEBUG_B:
        for nm, shp in (("dbg_slw", [32, 128]), ("dbg_wbf", [128, 32]),
                        ("dbg_wt", [32, 512]), ("dbg_wr", [128, 512]),
                        ("dbg_ge", [128, 1024]), ("dbg_ub", [128, 512]),
                        ("dbg_xn", [512, 1024]), ("dbg_wsel", [512, 32]),
                        ("dbg_xnT", [128, 512]), ("dbg_h1", [128, 512]),
                        ("dbg_hs", [128, 512]), ("dbg_sg", [128, 512]),
                        ("dbg_res2", [128, 1024])):
            dbg[nm] = nc.dram_tensor(nm, shp, F32, kind="ExternalOutput").ap()
    if shared is not None:
        gbs, ubs, dbs = fd["gbs"], fd["ubs"], fd["dbs"]
    else:
        gbs = nc.dram_tensor("gbs", [1024, 4096], BF16).ap()
        ubs = nc.dram_tensor("ubs", [1024, 4096], BF16).ap()
        dbs = nc.dram_tensor("dbs", [4096, 1024], BF16).ap()
    wscr = nc.dram_tensor("wscr", [32, 512], BF16).ap()

    if shared is None:
        prog = {e: [] for e in ENGINES}
    else:
        prog = shared["prog"]
    SY, GP, T, V, S = (Rec(prog, e) for e in ENGINES)

    es = contextlib.ExitStack()
    sem_es = shared["es"] if shared is not None else es
    with es:
        def sb(name, shape, dt):
            return es.enter_context(nc.sbuf_tensor(name, shape, dt))

        identb = sb("identb", [128, 128], BF16)
        routerbf = sb("routerbf", [128, 256], BF16)
        rbrep = sb("rbrep", [128, 32], F32)
        sgatebf = sb("sgatebf", [128, 4096], BF16)
        supbf = sb("supbf", [128, 4096], BF16)
        sdownbf = sb("sdownbf", [128, 4096], BF16)

        s_pa = sb("s_pa", [128, 1024], F32)
        s_pb = sb("s_pb", [128, 1024], F32)
        s_xr = sb("s_xr", [128, 1024], F32)
        res = sb("res", [128, 1024], F32)
        xc = sb("xc", [128, 1024], F32)
        sqd = sb("sqd", [128, 1024], F32)
        st1 = sb("st1", [128, 1], F32)
        st2 = sb("st2", [128, 1], F32)
        epsb = sb("epsb", [128, 1], F32)
        halfneg = sb("halfneg", [128, 1], F32)
        xn_f32 = [sb(f"xn{i}", [128, 1024], F32) for i in range(4)]
        xn_bf = sb("xn_bf", [128, 1024], BF16)
        xnT = [sb(f"xnT{k}", [128, 512], BF16) for k in range(8)]
        sco = sb("sco", [128, 32], F32)
        scow = sb("scow", [128, 32], F32)
        msk = sb("msk", [128, 32], F32)
        ind = sb("ind", [128, 32], F32)
        wsel_bf = sb("wsel_bf", [128, 128], BF16)
        wselT = sb("wselT", [128, 512], BF16)
        ge = [sb(f"ge{i}", [128, 1024], BF16) for i in range(2)]
        ue = [sb(f"ue{i}", [128, 1024], BF16) for i in range(2)]
        sg2 = [sb(f"sg{i}", [128, 512], BF16) for i in range(2)]
        ub2 = [sb(f"ub{i}", [128, 512], BF16) for i in range(2)]
        wr = [sb(f"wr{i}", [128, 512], BF16) for i in range(2)]
        h1 = [sb(f"h1_{k}", [128, 512], BF16) for k in range(32)]
        dbf = [sb(f"dbf{i}", [128, 512], BF16) for i in range(2)]
        hs = [sb(f"hs{i}", [128, 512], BF16) for i in range(4)]
        hsg = sb("hsg", [128, 512], BF16)
        res2 = [sb(f"res2_{i}", [128, 1024], F32) for i in range(4)]
        if DEBUG_B:
            dbsb = sb("dbsb", [128, 1024], F32)
            dbw = sb("dbw", [128, 128], F32)
            dbw2 = sb("dbw2", [128, 512], F32)
            dbu = sb("dbu", [128, 512], F32)
        outb = [sb(f"outb{i}", [128, 1024], F32) for i in range(2)]

        ppAs = [es.enter_context(nc.psum_tensor(f"ppA{i}", [128, 512], F32))
                for i in range(2)]
        ppBs = [es.enter_context(nc.psum_tensor(f"ppB{i}", [128, 512], F32))
                for i in range(2)]
        ppA, ppB = ppAs[0], ppBs[0]
        ppd = [es.enter_context(nc.psum_tensor(f"ppd{i}", [128, 512], F32))
               for i in range(4)]

        sem = {}
        c = {}
        for name in ("LP", "LW", "LG0", "LG1", "LF0", "LF1",
                     "STW", "STS", "STO0", "STO1", "LT", "DBG"):
            sem[name] = sem_es.enter_context(nc.semaphore("sB" + name))
            c[name] = 0
        sMM = sem_es.enter_context(nc.semaphore("sBMM"))
        sVV = sem_es.enter_context(nc.semaphore("sBVV"))
        sSS = sem_es.enter_context(nc.semaphore("sBSS"))
        c.update(mm=0, vv=0, ss=0)

        def dma(eng, key, out_, in_):
            eng.dma_start(out=out_, in_=in_).then_inc(sem[key], 16)
            c[key] += 16
            return (sem[key], c[key])

        def dma_t(eng, key, out_, in_):
            eng.dma_start_transpose(out=out_, in_=in_).then_inc(sem[key], 16)
            c[key] += 16
            return (sem[key], c[key])

        def dma_t(eng, key, out_, in_):
            eng.dma_start_transpose(out=out_, in_=in_).then_inc(sem[key], 16)
            c[key] += 16
            return (sem[key], c[key])

        def incM():
            prog["tensor"][-1][3].append((sMM, 1))
            c["mm"] += 1

        def incV():
            prog["vector"][-1][3].append((sVV, 1))
            c["vv"] += 1

        def incS():
            prog["scalar"][-1][3].append((sSS, 1))
            c["ss"] += 1

        def gV():
            if not prog["vector"][-1][3]:
                incV()
            V.wait_ge(sVV, c["vv"])

        def dV():
            V.drain()
            V.sem_inc(sVV, 1)
            c["vv"] += 1

        def dS():
            S.drain()
            S.sem_inc(sSS, 1)
            c["ss"] += 1

        def gS():
            if not prog["scalar"][-1][3]:
                incS()
            S.wait_ge(sSS, c["ss"])

        def wtok(Rr, tok):
            if tok is not None:
                Rr.wait_ge(tok[0], tok[1])

        V.memset(epsb[:, :], 1e-5)
        V.memset(halfneg[:, :], -0.5)
        V.memset(wsel_bf[:, :], 0.0)
        incV()
        dma(SY, "LW", identb[:, :], ident)
        dma(SY, "LW", rbrep[:, :], rbias[0:1, :].broadcast_to([128, 32]))
        dma(GP, "STW", routerbf[:, :], router_t)
        dma(GP, "STW", sgatebf[:, :], sgate_t)
        dma(GP, "STW", supbf[:, :], sup_t)
        wcast_tok = dma(GP, "STW", sdownbf[:, :], sdown_t)
        if shared is not None:
            wcast_tok = shared["scr_tok"]
        else:
            for (src, dsc, ntile) in ((gate_t, gbs, 8), (up_t, ubs, 8),
                                      (down_f, dbs, 32)):
                for it in range(ntile):
                    wcast_tok = dma(GP, "STW",
                                    dsc[it * 128:(it + 1) * 128, :],
                                    src[it * 128:(it + 1) * 128, :])

        if shared is not None:
            ccs = shared["ccsem"]
            for Rr in (SY, GP, T, V, S):
                Rr.wait_ge(ccs, 1)
        ppA_frees = [None, None]
        ppB_frees = [None, None]
        xnt_read_tok = None
        ppd_free = [None] * 4
        ge_free = [None, None]
        ue_free = [None, None]
        wr_free = [None, None]
        dbf_free = [None, None]
        outb_free = [None, None]
        xnT_prev_mm = None
        prev_h1_vv = None

        spa_free = None
        wselT_read_tok = None
        for q in range(NQUARTER):
            row0 = q * 512
            wsel_tt_tok = None
            for tt in range(4):
                r0 = row0 + tt * 128
                wtok(SY, spa_free)
                dma(SY, "LP", s_pa[:, :], pa[r0:r0 + 128, :])
                if pb is not None:
                    dma(SY, "LP", s_pb[:, :], pb[r0:r0 + 128, :])
                tok = dma(SY, "LP", s_xr[:, :], x_res[r0:r0 + 128, :])
                V.wait_ge(*tok)
                if pb is not None:
                    V.tensor_add(res[:, :], s_pa[:, :], s_pb[:, :])
                    gV()
                    V.tensor_add(res[:, :], res[:, :], s_xr[:, :])
                else:
                    V.tensor_add(res[:, :], s_pa[:, :], s_xr[:, :])
                gV()
                V.reduce_sum(st1[:, :], res[:, :], axis=AX.X)
                gV()
                V.tensor_scalar_mul(st1[:, :], st1[:, :], 1.0 / 1024.0)
                gV()
                V.tensor_scalar(out=xc[:, :], in0=res[:, :],
                                scalar1=st1[:, 0:1], scalar2=None,
                                op0=ALU.subtract)
                incV()
                spa_free = (sVV, c["vv"])
                S.wait_ge(sVV, c["vv"])
                S.activation(out=sqd[:, :], in_=xc[:, :], func=AF.Square,
                             accum_out=st2[:, 0:1])
                gS()
                S.mul(st2[:, 0:1], st2[:, 0:1], 1.0 / 1024.0)
                gS()
                S.activation(out=st2[:, 0:1], in_=st2[:, 0:1], func=AF.Ln,
                             bias=epsb[:, 0:1])
                gS()
                S.activation(out=st2[:, 0:1], in_=st2[:, 0:1], func=AF.Exp,
                             scale=halfneg[:, 0:1])
                incS()
                V.wait_ge(sSS, c["ss"])
                if q > 0 and tt == 0:
                    V.wait_ge(sMM, xnT_prev_mm)
                V.tensor_scalar_mul(xn_f32[tt][:, :], xc[:, :], st2[:, 0:1])
                gV()
                V.tensor_copy(xn_bf[:, :], xn_f32[tt][:, :])
                incV()
                xn_vv = c["vv"]
                dV()
                SY.wait_ge(sVV, c["vv"])
                if xnt_read_tok is not None:
                    SY.wait_ge(*xnt_read_tok)
                for k in range(8):
                    tok = dma_t(SY, "LT",
                                xnT[k][:, tt * 128:(tt + 1) * 128],
                                xn_bf[:, k * 128:(k + 1) * 128])
                xnT_tt_tok = tok
                if SKIP_ROUTER:
                    V.memset(wselT[0:32, tt * 128:(tt + 1) * 128], 0.125)
                    incV()
                    wsel_tt_tok = (sVV, c["vv"])
                    continue
                wtok(T, ppA_frees[0])
                if q == 0 and tt == 0:
                    T.wait_ge(sem["STW"], c["STW"])
                T.wait_ge(*xnT_tt_tok)
                for k in range(8):
                    T.matmul(ppA[:, 0:32],
                             xnT[k][:, tt * 128:(tt + 1) * 128],
                             routerbf[:, k * 32:(k + 1) * 32],
                             start=(k == 0), stop=(k == 7))
                incM()
                V.wait_ge(sMM, c["mm"])
                V.tensor_add(sco[:, :], ppA[:, 0:32], rbrep[:, :])
                gV()
                ppA_frees[0] = (sVV, c["vv"])
                V.reduce_max(st1[:, :], sco[:, :], axis=AX.X)
                gV()
                V.tensor_scalar_mul(st1[:, :], st1[:, :], -1.0)
                incV()
                S.wait_ge(sVV, c["vv"])
                S.activation(out=sco[:, :], in_=sco[:, :], func=AF.Exp,
                             bias=st1[:, 0:1])
                incS()
                V.wait_ge(sSS, c["ss"])
                V.reduce_sum(st2[:, :], sco[:, :], axis=AX.X)
                gV()
                V.reciprocal(st1[:, :], st2[:, :])
                gV()
                V.tensor_scalar_mul(sco[:, :], sco[:, :], st1[:, 0:1])
                V.tensor_copy(scow[:, :], sco[:, :])
                V.memset(msk[:, :], 0.0)
                for _ in range(4):
                    gV()
                    V.reduce_max(st1[:, :], scow[:, :], axis=AX.X)
                    gV()
                    V.tensor_single_scalar(out=ind[:, :], in_=scow[:, :],
                                           scalar=st1[:, 0:1], op=ALU.is_ge)
                    gV()
                    V.tensor_add(msk[:, :], msk[:, :], ind[:, :])
                    V.scalar_tensor_tensor(
                        out=scow[:, :], in0=ind[:, :], scalar=-1e30,
                        in1=scow[:, :], op0=ALU.mult, op1=ALU.add)
                V.tensor_mul(sco[:, :], sco[:, :], msk[:, :])
                gV()
                V.reduce_sum(st1[:, :], sco[:, :], axis=AX.X)
                gV()
                V.tensor_scalar_max(st1[:, :], st1[:, :], 1e-12)
                gV()
                V.reciprocal(st2[:, :], st1[:, :])
                gV()
                V.tensor_scalar_mul(sco[:, :], sco[:, :], st2[:, 0:1])
                gV()
                V.tensor_copy(wsel_bf[:, 0:32], sco[:, :])
                incV()
                if DEBUG_B and q == 0:
                    V.tensor_copy(dbw[:, tt * 32:(tt + 1) * 32], sco[:, :])
                    incV()
                    GP.wait_ge(sVV, c["vv"])
                    dma(GP, "DBG", dbg["dbg_xn"][tt * 128:(tt + 1) * 128, :],
                        xn_f32[tt][:, :])
                    if tt == 3:
                        dma(GP, "DBG",
                            dbg["dbg_wsel"].rearrange("(a p) e -> p a e", a=4),
                            dbw[:, :].rearrange("p (a e) -> p a e", a=4))
                dV()
                SY.wait_ge(sVV, c["vv"])
                wtok(SY, wselT_read_tok)
                wsv_tok = dma_t(SY, "LT",
                                wselT[:, tt * 128:(tt + 1) * 128],
                                wsel_bf[:, :])
                if DEBUG_B and q == 0 and tt == 0:
                    V.tensor_copy(dbu[0:32, 0:128], sl)
                    V.tensor_copy(dbu[0:128, 128:160], wsel_bf[:, :])
                    incV()
                    GP.wait_ge(sVV, c["vv"])
                    dma(GP, "DBG", dbg["dbg_slw"], dbu[0:32, 0:128])
                    dma(GP, "DBG", dbg["dbg_wbf"], dbu[0:128, 128:160])
                wsel_tt_tok = wsv_tok
            all_xnT_tok = (sem["LT"], c["LT"])

            if DEBUG_B and q == 0:
                V.wait_ge(*wsel_tt_tok)
                V.tensor_copy(dbw2[0:32, :], wselT[0:32, :])
                incV()
                GP.wait_ge(sVV, c["vv"])
                dma(GP, "DBG", dbg["dbg_wt"], dbw2[0:32, :])
            GP.wait_ge(*wsel_tt_tok)
            if prev_h1_vv is not None:
                GP.wait_ge(sVV, prev_h1_vv)
            if not SKIP_WSCR:
                wscr_tok = dma(GP, "STS", wscr[:, :], wselT[0:32, :])
                SY.wait_ge(*wscr_tok)

            h1_vv = [None] * 32
            for e in range(0 if SKIP_MOE else NEXPERT):
                i = e % 2
                key = "LG%d" % i
                if e == 0 and q == 0:
                    SY.wait_ge(*wcast_tok)
                wtok(SY, ge_free[i])
                wtok(SY, ue_free[i])
                wtok(SY, wr_free[i])
                dma(SY, key, ge[i][:, :].rearrange("p (k j) -> p k j", k=8),
                    gbs[:, e * 128:(e + 1) * 128].rearrange(
                        "(k p) j -> p k j", k=8))
                dma(SY, key, ue[i][:, :].rearrange("p (k j) -> p k j", k=8),
                    ubs[:, e * 128:(e + 1) * 128].rearrange(
                        "(k p) j -> p k j", k=8))
                if SKIP_WR:
                    V.memset(wr[i][:, :], 0.5)
                    incV()
                    grp_tok = (sem[key], c[key])
                else:
                    grp_tok = dma(SY, key, wr[i][:, :],
                                  wscr[e:e + 1, :].broadcast_to([128, 512]))
                if e == 0 and q == 0:
                    T.wait_ge(*wcast_tok)
                pa_ = ppAs[i]
                pb_ = ppBs[i]
                wtok(T, ppA_frees[i])
                T.wait_ge(*grp_tok)
                if e == 0:
                    T.wait_ge(*all_xnT_tok)
                for k in range(8):
                    T.matmul(pa_[:, :], ge[i][:, k * 128:(k + 1) * 128],
                             xnT[k][:, :], start=(k == 0), stop=(k == 7))
                incM()
                ge_free[i] = (sMM, c["mm"])
                wtok(T, ppB_frees[i])
                for k in range(8):
                    T.matmul(pb_[:, :], ue[i][:, k * 128:(k + 1) * 128],
                             xnT[k][:, :], start=(k == 0), stop=(k == 7))
                incM()
                ue_free[i] = (sMM, c["mm"])
                S.wait_ge(sMM, c["mm"] - 1)
                S.activation(out=sg2[i][:, :], in_=pa_[:, :], func=AF.Silu)
                incS()
                ppA_frees[i] = (sSS, c["ss"])
                S.wait_ge(sMM, c["mm"])
                S.copy(ub2[i][:, :], pb_[:, :])
                incS()
                dS()
                ppB_frees[i] = (sSS, c["ss"])
                ub_ss = c["ss"]
                V.wait_ge(sSS, ub_ss)
                V.wait_ge(*grp_tok)
                V.tensor_mul(h1[e][:, :], sg2[i][:, :], ub2[i][:, :])
                gV()
                V.tensor_mul(h1[e][:, :], h1[e][:, :], wr[i][:, :])
                incV()
                h1_vv[e] = c["vv"]
                wr_free[i] = (sVV, c["vv"])
                if DEBUG_B and q == 0 and e == 0:
                    V.tensor_copy(dbsb[:, :], ge[0][:, :])
                    V.tensor_copy(dbw2[:, :], wr[0][:, :])
                    V.tensor_copy(dbu[:, :], ub[:, :])
                    incV()
                    GP.wait_ge(sVV, c["vv"])
                    dma(GP, "DBG", dbg["dbg_h1"], h1[0][:, :])
                    dma(GP, "DBG", dbg["dbg_sg"], sg[:, :])
                    dma(GP, "DBG", dbg["dbg_xnT"], xnT[0][:, :])
                    dma(GP, "DBG", dbg["dbg_ge"], dbsb[:, :])
                    dma(GP, "DBG", dbg["dbg_wr"], dbw2[:, :])
                    dma(GP, "DBG", dbg["dbg_ub"], dbu[:, :])
                    dbg_tok = (sem["DBG"], c["DBG"])
                else:
                    dbg_tok = None
            prev_h1_vv = c["vv"] if not SKIP_MOE else None

            for st_ in ([] if SKIP_SHARED else range(4)):
                wtok(T, ppA_frees[0])
                if q == 0 and st_ == 0:
                    T.wait_ge(sem["STW"], c["STW"])
                for k in range(8):
                    T.matmul(ppA[:, :],
                             sgatebf[:, k * 512 + st_ * 128:
                                     k * 512 + (st_ + 1) * 128],
                             xnT[k][:, :], start=(k == 0), stop=(k == 7))
                incM()
                S.wait_ge(sMM, c["mm"])
                S.activation(out=hsg[:, :], in_=ppA[:, :], func=AF.Silu)
                incS()
                ppA_frees[0] = (sSS, c["ss"])
                wtok(T, ppB_frees[0])
                for k in range(8):
                    T.matmul(ppB[:, :],
                             supbf[:, k * 512 + st_ * 128:
                                   k * 512 + (st_ + 1) * 128],
                             xnT[k][:, :], start=(k == 0), stop=(k == 7))
                incM()
                S.wait_ge(sMM, c["mm"])
                S.copy(ub2[0][:, :], ppB[:, :])
                incS()
                dS()
                ppB_frees[0] = (sSS, c["ss"])
                V.wait_ge(sSS, c["ss"])
                V.tensor_mul(hs[st_][:, :], hsg[:, :], ub2[0][:, :])
                incV()
            hs_vv = c["vv"]
            if SKIP_SHARED:
                for st_ in range(4):
                    V.memset(hs[st_][:, :], 0.01)
                incV()
                hs_vv = c["vv"]
            if DEBUG_B and q == 0:
                GP.wait_ge(sVV, hs_vv)
                dma(GP, "DBG", dbg["dbg_hs"], hs[0][:, :])

            if SKIP_DOWN:
                for tt in range(4):
                    V.memset(res2[tt][:, :], 0.1)
                    incV()
                    ppd_free[tt] = (sVV, c["vv"])
                down_mm = c["mm"]
            for fh in ([] if SKIP_DOWN else range(2)):
                for k in list(range(NEXPERT)) + list(range(32, 36)):
                    if k < 32:
                        i = k % 2
                        if q == 0 and fh == 0 and k == 0:
                            SY.wait_ge(*wcast_tok)
                        wtok(SY, dbf_free[i])
                        ftok = dma(SY, "LF%d" % i, dbf[i][:, :],
                                   dbs[k * 128:(k + 1) * 128,
                                       fh * 512:(fh + 1) * 512])
                        T.wait_ge(*ftok)
                        if fh == 0 and k == 0:
                            T.wait_ge(sVV, hs_vv)
                        lhs_full = h1[k]
                        rhs = dbf[i][:, :]
                    else:
                        lhs_full = hs[k - 32]
                        rhs = sdownbf[:, (k - 32) * 1024 + fh * 512:
                                      (k - 32) * 1024 + (fh + 1) * 512]
                    for tt in range(4):
                        if k == 0:
                            wtok(T, ppd_free[tt])
                        T.matmul(ppd[tt][:, :],
                                 lhs_full[:, tt * 128:(tt + 1) * 128],
                                 rhs, start=(k == 0), stop=(k == 35))
                    if k < 32:
                        incM()
                        dbf_free[i] = (sMM, c["mm"])
                incM()
                down_mm = c["mm"]
                for tt in range(4):
                    V.wait_ge(sMM, down_mm)
                    V.scalar_tensor_tensor(
                        out=res2[tt][:, fh * 512:(fh + 1) * 512],
                        in0=ppd[tt][:, :], scalar=1.0,
                        in1=xn_f32[tt][:, fh * 512:(fh + 1) * 512],
                        op0=ALU.mult, op1=ALU.add)
                    incV()
                    ppd_free[tt] = (sVV, c["vv"])
            xnT_prev_mm = down_mm
            xnt_read_tok = (sMM, down_mm)
            wselT_read_tok = wscr_tok if not SKIP_WSCR else wsel_tt_tok
            if DEBUG_B and q == 0:
                GP.wait_ge(sVV, c["vv"])
                dma(GP, "DBG", dbg["dbg_res2"], res2[0][:, :])

            for tt in range(4):
                o = tt % 2
                V.reduce_sum(st1[:, :], res2[tt][:, :], axis=AX.X)
                gV()
                V.tensor_scalar_mul(st1[:, :], st1[:, :], 1.0 / 1024.0)
                gV()
                V.tensor_scalar(out=xc[:, :], in0=res2[tt][:, :],
                                scalar1=st1[:, 0:1], scalar2=None,
                                op0=ALU.subtract)
                incV()
                S.wait_ge(sVV, c["vv"])
                S.activation(out=sqd[:, :], in_=xc[:, :], func=AF.Square,
                             accum_out=st2[:, 0:1])
                gS()
                S.mul(st2[:, 0:1], st2[:, 0:1], 1.0 / 1024.0)
                gS()
                S.activation(out=st2[:, 0:1], in_=st2[:, 0:1], func=AF.Ln,
                             bias=epsb[:, 0:1])
                gS()
                S.activation(out=st2[:, 0:1], in_=st2[:, 0:1], func=AF.Exp,
                             scale=halfneg[:, 0:1])
                incS()
                V.wait_ge(sSS, c["ss"])
                wtok(V, outb_free[o])
                V.tensor_scalar_mul(outb[o][:, :], xc[:, :], st2[:, 0:1])
                incV()
                dV()
                GP.wait_ge(sVV, c["vv"])
                outb_free[o] = dma(
                    GP, "STO%d" % o,
                    out[row0 + tt * 128:row0 + (tt + 1) * 128, :],
                    outb[o][:, :])

        GP.wait_ge(sem["STO0"], c["STO0"])
        GP.wait_ge(sem["STO1"], c["STO1"])
        if shared is None:
            _emit_blocks(nc, prog)
    return nc


def build_AB():
    nc = bass.Bass("TRN2", target_bir_lowering=False,
                   detect_race_conditions=False)
    cc_in = nc.dram_tensor("cc_in", [4096, 1024], F32).ap()
    cc_out = nc.dram_tensor("cc_out", [2048, 1024], F32).ap()
    gate_t = nc.dram_tensor("gate_t", [1024, 4096], F32,
                            kind="ExternalInput").ap()
    up_t = nc.dram_tensor("up_t", [1024, 4096], F32,
                          kind="ExternalInput").ap()
    down_f = nc.dram_tensor("down_f", [4096, 1024], F32,
                            kind="ExternalInput").ap()
    gbs = nc.dram_tensor("gbs", [1024, 4096], BF16).ap()
    ubs = nc.dram_tensor("ubs", [1024, 4096], BF16).ap()
    dbs = nc.dram_tensor("dbs", [4096, 1024], BF16).ap()
    cast_jobs = []
    for (src, dsc, ntile) in ((gate_t, gbs, 8), (up_t, ubs, 8),
                              (down_f, dbs, 32)):
        for it in range(ntile):
            cast_jobs.append((dsc[it * 128:(it + 1) * 128, :],
                              src[it * 128:(it + 1) * 128, :]))
    prog = {e: [] for e in ENGINES}
    es = contextlib.ExitStack()
    with es:
        ccsem = es.enter_context(nc.semaphore("sCC"))
        shared = dict(nc=nc, prog=prog, es=es, cc_in=cc_in, cc_out=cc_out,
                      ccsem=ccsem, cast_jobs=cast_jobs, scr_tok=None,
                      ffn_dram=dict(gate_t=gate_t, up_t=up_t, down_f=down_f,
                                    gbs=gbs, ubs=ubs, dbs=dbs))
        build_A(shared)
        assert not cast_jobs
        GP = Rec(prog, "gpsimd")
        GP.collective_compute(
            "ReduceScatter", ALU.add,
            replica_groups=[[0, 1], [2, 3], [4, 5], [6, 7]],
            ins=[cc_in.opt()], outs=[cc_out.opt()]).then_inc(ccsem, 1)
        build_B(shared)
        _emit_blocks(nc, prog)
    return nc



def _tile_rows(a, k):
    a = np.asarray(a)
    return np.ascontiguousarray(
        a.reshape(k, 128, -1).transpose(1, 0, 2).reshape(128, -1))


def kernel(L0_attn_x, L0_attn_in_proj_w, L0_attn_conv_w, L0_attn_conv_b,
           L0_attn_x_proj_w, L0_attn_dt_proj_w, L0_attn_dt_proj_b,
           L0_attn_A_log, L0_attn_D, L0_attn_out_proj_w, L0_ffn_router_w,
           L0_ffn_router_bias, L0_ffn_gate_w, L0_ffn_up_w, L0_ffn_down_w,
           L0_ffn_sgate_w, L0_ffn_sup_w, L0_ffn_sdown_w):
    f = lambda a: np.ascontiguousarray(np.asarray(a, dtype=np.float32))
    x = f(L0_attn_x)
    w_in = f(L0_attn_in_proj_w)
    conv_w = f(L0_attn_conv_w)
    conv_b = f(L0_attn_conv_b)
    x_proj = f(L0_attn_x_proj_w)
    dt_w = f(L0_attn_dt_proj_w)
    dt_b = f(L0_attn_dt_proj_b)
    a_log = f(L0_attn_A_log)
    d_vec = f(L0_attn_D)
    out_w = f(L0_attn_out_proj_w)

    fused = True
    ncA = build_AB() if fused else build_A()
    mapsA = []
    for core in range(8):
        b, h = divmod(core, 2)
        perm = np.concatenate([np.arange(h * 1024, (h + 1) * 1024),
                               np.arange((1 - h) * 1024, (1 - h) * 1024 + 1024)])
        m = {
            "x_fm": np.ascontiguousarray(x[b].T),
            "w_in_t": np.ascontiguousarray(
                np.concatenate([w_in[0:2048][perm],
                                w_in[2048 + h * 1024:2048 + (h + 1) * 1024]],
                               axis=0).T),
            "conv_w": _tile_rows(conv_w[perm], 16),
            "conv_b": _tile_rows(conv_b[perm][:, None], 16),
            "xproj_t": _tile_rows(x_proj.T[perm], 16),
            "dtw_t": np.ascontiguousarray(
                dt_w[h * 1024:(h + 1) * 1024].T),
            "dt_b": _tile_rows(dt_b[h * 1024:(h + 1) * 1024][:, None], 8),
            "a_log": _tile_rows(a_log[h * 1024:(h + 1) * 1024], 8),
            "d_h": _tile_rows(d_vec[h * 1024:(h + 1) * 1024][:, None], 8),
            "outw_t": np.ascontiguousarray(
                out_w[:, h * 1024:(h + 1) * 1024].T),
        }
        mapsA.append(m)

    router_w = f(L0_ffn_router_w)
    router_b = f(L0_ffn_router_bias)
    gate_w = f(L0_ffn_gate_w)
    up_w = f(L0_ffn_up_w)
    down_w = f(L0_ffn_down_w)
    sgate_w = f(L0_ffn_sgate_w)
    sup_w = f(L0_ffn_sup_w)
    sdown_w = f(L0_ffn_sdown_w)

    gate_t = np.ascontiguousarray(gate_w.reshape(4096, 1024).T)
    up_t = np.ascontiguousarray(up_w.reshape(4096, 1024).T)
    down_flat = np.ascontiguousarray(
        down_w.transpose(0, 2, 1).reshape(4096, 1024))
    shared_w = {
        "router_t": _tile_rows(router_w.T, 8),
        "rbias": np.ascontiguousarray(router_b[None, :]),
        "gate_t": gate_t,
        "up_t": up_t,
        "down_f": down_flat,
        "sgate_t": _tile_rows(sgate_w.T, 8),
        "sup_t": _tile_rows(sup_w.T, 8),
        "sdown_t": _tile_rows(sdown_w.T, 4),
        "ident": np.eye(128, dtype=np.float32).astype(ml_dtypes.bfloat16),
    }
    maps = []
    for core in range(8):
        b, h = divmod(core, 2)
        m = dict(shared_w)
        m.update(mapsA[core])
        m["x_res"] = np.ascontiguousarray(x[b, h * 2048:(h + 1) * 2048])
        maps.append(m)
    res = run_bass_kernel_spmd(ncA, maps, core_ids=list(range(8)))

    out = np.empty((4, 4096, 1024), dtype=np.float32)
    for core in range(8):
        b, h = divmod(core, 2)
        out[b, h * 2048:(h + 1) * 2048] = np.asarray(res.results[core]["out"])
    return out


# version 07
# speedup vs baseline: 1.1895x; 1.0241x over previous
import contextlib

import numpy as np
import ml_dtypes
import concourse.bass as bass
import concourse.mybir as mybir
from concourse.bass_utils import run_bass_kernel_spmd

F32 = mybir.dt.float32
BF16 = mybir.dt.bfloat16
AF = mybir.ActivationFunctionType
ALU = mybir.AluOpType
AX = mybir.AxisListType

QT = 512
NCHUNK = 8
NQUARTER = 4
DEBUG_B = False
DEBUG_A = False
NEXPERT = 32
SKIP_WR = False
SKIP_MOE = False
SKIP_DOWN = False
SKIP_SHARED = False
SKIP_ROUTER = False
SKIP_TRANS = False
SKIP_WSCR = False
ENGINES = ("sync", "gpsimd", "tensor", "vector", "scalar")


class Rec:

    def __init__(self, prog, name):
        self._prog = prog
        self._name = name

    def __getattr__(self, meth):
        prog = self._prog[self._name]

        def call(*a, **k):
            entry = [meth, a, k, []]
            prog.append(entry)

            class _C:
                def then_inc(self, sem, v):
                    entry[3].append((sem, v))
                    return self

            return _C()

        return call


def _replay(eng, entries):
    for meth, a, k, incs in entries:
        r = getattr(eng, meth)(*a, **k)
        for sem, v in incs:
            r = r.then_inc(sem, v)


def _emit_blocks(nc, prog):
    with nc.Block() as block:
        @block.sync
        def _(eng):
            _replay(eng, prog["sync"])

        @block.gpsimd
        def _(eng):
            _replay(eng, prog["gpsimd"])

        @block.tensor
        def _(eng):
            _replay(eng, prog["tensor"])

        @block.vector
        def _(eng):
            _replay(eng, prog["vector"])

        @block.scalar
        def _(eng):
            _replay(eng, prog["scalar"])



def build_A(shared=None):
    if shared is None:
        nc = bass.Bass("TRN2", target_bir_lowering=False,
                       detect_race_conditions=False)
    else:
        nc = shared["nc"]

    x_fm = nc.dram_tensor("x_fm", [1024, 4096], F32, kind="ExternalInput").ap()
    w_in_t = nc.dram_tensor("w_in_t", [1024, 3072], F32, kind="ExternalInput").ap()
    conv_w = nc.dram_tensor("conv_w", [128, 64], F32, kind="ExternalInput").ap()
    conv_b = nc.dram_tensor("conv_b", [128, 16], F32, kind="ExternalInput").ap()
    xproj_t = nc.dram_tensor("xproj_t", [128, 1536], F32, kind="ExternalInput").ap()
    dtw_t = nc.dram_tensor("dtw_t", [64, 1024], F32, kind="ExternalInput").ap()
    dt_b = nc.dram_tensor("dt_b", [128, 8], F32, kind="ExternalInput").ap()
    a_log = nc.dram_tensor("a_log", [128, 128], F32, kind="ExternalInput").ap()
    d_h = nc.dram_tensor("d_h", [128, 8], F32, kind="ExternalInput").ap()
    outw_t = nc.dram_tensor("outw_t", [1024, 1024], F32, kind="ExternalInput").ap()

    if shared is None:
        partial = nc.dram_tensor("partial", [4096, 1024], F32,
                                 kind="ExternalOutput").ap()
        cast_jobs = []
    else:
        partial = shared["cc_in"]
        cast_jobs = shared["cast_jobs"]
    dbg = {}
    if DEBUG_A:
        for nm, shp in (("dbg_u0", [128, 512]), ("dbg_sz0", [128, 512]),
                        ("dbg_dtr", [64, 512]), ("dbg_bcr", [32, 512]),
                        ("dbg_delta", [128, 512]), ("dbg_du", [128, 512]),
                        ("dbg_brep", [128, 512]), ("dbg_dA", [128, 512]),
                        ("dbg_hb", [128, 4096]), ("dbg_y", [128, 512]),
                        ("dbg_gated", [128, 512])):
            dbg[nm] = nc.dram_tensor(nm, shp, F32, kind="ExternalOutput").ap()
    bct_h = nc.dram_tensor("bct", [32, QT], BF16)
    bct = bct_h.ap()

    if shared is None:
        prog = {e: [] for e in ENGINES}
    else:
        prog = shared["prog"]
    SY, GP, T, V, S = (Rec(prog, e) for e in ENGINES)

    es = contextlib.ExitStack()
    sem_es = shared["es"] if shared is not None else es
    with es:
        def sb(name, shape, dt):
            return es.enter_context(nc.sbuf_tensor(name, shape, dt))

        cw_all = sb("cw_all", [128, 64], F32)
        cb_all = sb("cb_all", [128, 16], F32)
        dtb_all = sb("dtb_all", [128, 8], F32)
        dh_all = sb("dh_all", [128, 8], F32)
        aneg = sb("aneg", [128, 128], F32)
        hstate = sb("hstate", [128, 128], F32)
        halo = sb("halo", [128, 48], BF16)
        winbf = [sb(f"winbf{k}", [128, 3072], BF16) for k in range(8)]
        outwbf = [sb(f"outwbf{k}", [128, 1024], BF16) for k in range(8)]
        xprojbf = sb("xprojbf", [128, 1536], BF16)
        dtwbf = sb("dtwbf", [64, 1024], BF16)

        xf32 = [sb(f"xf32_{i}", [128, QT], F32) for i in range(2)]
        xbf = [sb(f"xbf{k}", [128, QT], BF16) for k in range(8)]
        xitmp = [sb(f"xitmp{i}", [128, QT + 3], BF16) for i in range(2)]
        cacc = [sb(f"cacc{i}", [128, QT], F32) for i in range(2)]
        u = [sb(f"u{i}", [128, QT], BF16) for i in range(8)]
        utmp = [sb(f"utmp{i}", [128, QT], BF16) for i in range(2)]
        sz = [sb(f"sz{i}", [128, QT], BF16) for i in range(8)]
        dtr_bf = sb("dtr_bf", [64, QT], BF16)
        bcr_bf = sb("bcr_bf", [32, QT], BF16)
        brep = sb("brep", [128, 16 * QT], BF16)
        crep = sb("crep", [128, 16 * QT], BF16)
        spe = sb("spe", [128, QT], F32)
        delta = sb("delta", [128, QT], F32)
        du_bf = sb("du_bf", [128, QT], BF16)
        dA = [sb(f"dA{i}", [128, QT], F32) for i in range(4)]
        p_bf = sb("p_bf", [128, 4 * QT], BF16)
        hb = sb("hb", [128, 8 * QT], BF16)
        t2 = sb("t2", [128, 8 * QT], BF16)
        ytmp = sb("ytmp", [128, QT], BF16)
        y_f32 = sb("y_f32", [128, QT], F32)
        yg = sb("yg", [128, QT], F32)
        gated = [sb(f"gated{i}", [128, QT], BF16) for i in range(8)]
        oevac = [sb(f"oevac{i}", [128, QT], F32) for i in range(2)]

        pin = [es.enter_context(nc.psum_tensor(f"pin{i}", [128, QT], F32))
               for i in range(2)]
        px = es.enter_context(nc.psum_tensor("px", [96, QT], F32))
        pdt = es.enter_context(nc.psum_tensor("pdt", [128, QT], F32))
        pout = [es.enter_context(nc.psum_tensor(f"pout{i}", [128, QT], F32))
                for i in range(2)]

        sem = {}
        c = {}
        for name in ("LX0", "LX1", "LW", "LB", "LC", "STB", "STP0", "STP1",
                     "DBG"):
            sem[name] = sem_es.enter_context(nc.semaphore("sA" + name))
            c[name] = 0
        sMM = sem_es.enter_context(nc.semaphore("sAMM"))
        sVV = sem_es.enter_context(nc.semaphore("sAVV"))
        sSS = sem_es.enter_context(nc.semaphore("sASS"))
        c.update(mm=0, vv=0, ss=0)

        def dma(eng, key, out, in_):
            eng.dma_start(out=out, in_=in_).then_inc(sem[key], 16)
            c[key] += 16
            return (sem[key], c[key])

        def incM():
            prog["tensor"][-1][3].append((sMM, 1))
            c["mm"] += 1

        def incV():
            prog["vector"][-1][3].append((sVV, 1))
            c["vv"] += 1

        def incS():
            prog["scalar"][-1][3].append((sSS, 1))
            c["ss"] += 1

        def gV():
            if not prog["vector"][-1][3]:
                incV()
            V.wait_ge(sVV, c["vv"])

        def dV():
            V.drain()
            V.sem_inc(sVV, 1)
            c["vv"] += 1

        def dS():
            S.drain()
            S.sem_inc(sSS, 1)
            c["ss"] += 1

        def gS():
            if not prog["scalar"][-1][3]:
                incS()
            S.wait_ge(sSS, c["ss"])

        def wtok(Rr, tok):
            if tok is not None:
                Rr.wait_ge(tok[0], tok[1])

        dma(SY, "LW", cw_all[:, :], conv_w)
        dma(SY, "LW", cb_all[:, :], conv_b)
        dma(SY, "LW", dtb_all[:, :], dt_b)
        dma(SY, "LW", dh_all[:, :], d_h)
        tok = dma(SY, "LW", cacc[0][:, 0:128], a_log)
        S.wait_ge(*tok)
        S.activation(out=aneg[:, :], in_=cacc[0][:, 0:128], func=AF.Exp)
        incS()
        dS()
        V.wait_ge(sSS, c["ss"])
        V.memset(hstate[:, :], 0.0)
        V.memset(halo[:, :], 0.0)
        V.tensor_scalar_mul(aneg[:, :], aneg[:, :], -1.0)
        incV()
        aneg_done = c["vv"]

        dma(GP, "STB", xprojbf[:, :], xproj_t)
        dma(GP, "STB", dtwbf[:, :], dtw_t)
        for k in range(8):
            dma(GP, "STB", winbf[k][:, :], w_in_t[k * 128:(k + 1) * 128, :])
        for k in range(8):
            dma(GP, "STB", outwbf[k][:, :], outw_t[k * 128:(k + 1) * 128, :])
        wtok_a = (sem["STB"], c["STB"])

        pin_free = [None, None]
        pout_free = [None, None]
        oevac_free = [None, None]
        utmp_free = [None, None]
        pdt_free = None
        prev_gated_vv = [None] * 8
        prev_scan_all_vv = None
        prev_inproj_mm = None
        prev_dtproj_mm = None
        cast_done = [0] * 8

        for ch in range(NCHUNK):
            col0 = ch * QT
            for k in range(8):
                if k >= 2 or ch > 0:
                    SY.wait_ge(sVV, cast_done[(k - 2) % 8])
                tok = dma(SY, "LX%d" % (k % 2), xf32[k % 2][:, :],
                          x_fm[k * 128:(k + 1) * 128, col0:col0 + QT])
                V.wait_ge(*tok)
                if ch > 0 and k == 0:
                    V.wait_ge(sMM, prev_inproj_mm)
                V.tensor_copy(xbf[k][:, :], xf32[k % 2][:, :])
                incV()
                cast_done[k] = c["vv"]
            all_cast = c["vv"]

            for mt in range(24):
                pi = mt % 2
                wtok(T, pin_free[pi])
                if mt == 0:
                    T.wait_ge(sVV, all_cast)
                    if ch == 0:
                        T.wait_ge(*wtok_a)
                for k in range(8):
                    T.matmul(pin[pi][:, :],
                             winbf[k][:, mt * 128:(mt + 1) * 128],
                             xbf[k][:, :], start=(k == 0), stop=(k == 7))
                incM()
                in_mm = c["mm"]
                if mt < 16:
                    V.wait_ge(sMM, in_mm)
                    V.tensor_copy(xitmp[pi][:, 0:3],
                                  halo[:, mt * 3:mt * 3 + 3])
                    V.tensor_copy(xitmp[pi][:, 3:3 + QT], pin[pi][:, :])
                    gV()
                    V.tensor_copy(halo[:, mt * 3:mt * 3 + 3],
                                  xitmp[pi][:, QT:QT + 3])
                    V.tensor_scalar_mul(cacc[0][:, :], xitmp[pi][:, 0:QT],
                                        cw_all[:, mt * 4:mt * 4 + 1])
                    gV()
                    V.scalar_tensor_tensor(
                        out=cacc[1][:, :], in0=xitmp[pi][:, 1:1 + QT],
                        scalar=cw_all[:, mt * 4 + 1:mt * 4 + 2],
                        in1=cacc[0][:, :], op0=ALU.mult, op1=ALU.add)
                    gV()
                    V.scalar_tensor_tensor(
                        out=cacc[0][:, :], in0=xitmp[pi][:, 2:2 + QT],
                        scalar=cw_all[:, mt * 4 + 2:mt * 4 + 3],
                        in1=cacc[1][:, :], op0=ALU.mult, op1=ALU.add)
                    gV()
                    V.scalar_tensor_tensor(
                        out=cacc[1][:, :], in0=xitmp[pi][:, 3:3 + QT],
                        scalar=cw_all[:, mt * 4 + 3:mt * 4 + 4],
                        in1=cacc[0][:, :], op0=ALU.mult, op1=ALU.add)
                    incV()
                    pin_free[pi] = (sVV, c["vv"])
                    conv_vv = c["vv"]
                    if mt < 8:
                        udst = u[mt]
                        if ch > 0:
                            S.wait_ge(sVV, prev_gated_vv[mt])
                    else:
                        udst = utmp[mt % 2]
                        wtok(S, utmp_free[mt % 2])
                    S.wait_ge(sVV, conv_vv)
                    S.activation(out=udst[:, :], in_=cacc[1][:, :],
                                 func=AF.Silu, bias=cb_all[:, mt:mt + 1])
                    incS()
                    dS()
                    T.wait_ge(sSS, c["ss"])
                    T.matmul(px[:, :], xprojbf[:, mt * 96:(mt + 1) * 96],
                             udst[:, :], start=(mt == 0), stop=(mt == 15))
                    incM()
                    if mt >= 8:
                        utmp_free[mt % 2] = (sMM, c["mm"])
                else:
                    dt = mt - 16
                    if ch > 0:
                        S.wait_ge(sVV, prev_gated_vv[dt])
                    S.wait_ge(sMM, in_mm)
                    S.activation(out=sz[dt][:, :], in_=pin[pi][:, :],
                                 func=AF.Silu)
                    incS()
                    dS()
                    pin_free[pi] = (sSS, c["ss"])
            xproj_mm = c["mm"]

            V.wait_ge(sMM, xproj_mm)
            if ch > 0:
                V.wait_ge(sMM, prev_dtproj_mm)
            V.tensor_copy(dtr_bf[:, :], px[0:64, :])
            V.tensor_copy(bcr_bf[:, :], px[64:96, :])
            incV()
            dV()
            bc_vv = c["vv"]
            GP.wait_ge(sVV, bc_vv)
            if ch > 0:
                GP.wait_ge(sem["LB"], c["LB"])
                GP.wait_ge(sem["LC"], c["LC"])
            bct_tok = dma(GP, "STB", bct[:, :], bcr_bf[:, :])
            SY.wait_ge(*bct_tok)
            if prev_scan_all_vv is not None:
                SY.wait_ge(sVV, prev_scan_all_vv)
                GP.wait_ge(sVV, prev_scan_all_vv)
            for n in range(16):
                dma(SY, "LB", brep[:, n * QT:(n + 1) * QT],
                    bct[n:n + 1, :].broadcast_to([128, QT]))
                dma(GP, "LC", crep[:, n * QT:(n + 1) * QT],
                    bct[16 + n:17 + n, :].broadcast_to([128, QT]))
            rep_b = (sem["LB"], c["LB"])
            rep_c = (sem["LC"], c["LC"])

            if DEBUG_A and ch == 0:
                V.wait_ge(sSS, c["ss"])
                V.wait_ge(*rep_b)
                V.wait_ge(*rep_c)
                incV()
                dV()
                dS()
                GP.wait_ge(sVV, c["vv"])
                GP.wait_ge(sSS, c["ss"])
                dma(GP, "DBG", dbg["dbg_u0"], u[0][:, :])
                dma(GP, "DBG", dbg["dbg_sz0"], sz[0][:, :])
                dma(GP, "DBG", dbg["dbg_dtr"], dtr_bf[:, :])
                dma(GP, "DBG", dbg["dbg_bcr"], bcr_bf[:, :])
                dma(GP, "DBG", dbg["dbg_brep"], brep[:, 0:512])
                dbg_tok = (sem["DBG"], c["DBG"])
                V.wait_ge(*dbg_tok)
                S.wait_ge(*dbg_tok)
            for dt in range(8):
                wtok(T, pdt_free)
                if dt == 0:
                    T.wait_ge(sVV, bc_vv)
                T.matmul(pdt[:, :], dtwbf[:, dt * 128:(dt + 1) * 128],
                         dtr_bf[:, :], start=True, stop=True)
                incM()
                dtmm = c["mm"]
                S.wait_ge(sMM, dtmm)
                S.activation(out=spe[:, :], in_=pdt[:, :], func=AF.Exp,
                             bias=dtb_all[:, dt:dt + 1])
                gS()
                S.activation(out=delta[:, :], in_=spe[:, :], func=AF.Ln,
                             bias=1.0)
                if ch == 0 and dt == 0:
                    S.wait_ge(sVV, aneg_done)
                incS()
                dS()
                pdt_free = (sSS, c["ss"])
                sp_ss = c["ss"]
                V.wait_ge(sSS, sp_ss)
                V.tensor_mul(du_bf[:, :], delta[:, :], u[dt][:, :])
                gV()
                if DEBUG_A and ch == 0 and dt == 0:
                    dV()
                    dS()
                    GP.wait_ge(sVV, c["vv"])
                    GP.wait_ge(sSS, c["ss"])
                    dma(GP, "DBG", dbg["dbg_delta"], delta[:, :])
                    dma(GP, "DBG", dbg["dbg_du"], du_bf[:, :])
                    V.wait_ge(sem["DBG"], c["DBG"])
                    S.wait_ge(sem["DBG"], c["DBG"])
                if dt == 0:
                    V.wait_ge(*rep_b)
                    V.wait_ge(*rep_c)
                scan_vv = {}
                for n in range(16):
                    nb, half = n % 8, n // 8
                    nq4 = n % 4
                    if nq4 == 0:
                        for n2 in range(n, n + 4):
                            V.tensor_mul(p_bf[:, (n2 % 4) * QT:
                                               (n2 % 4 + 1) * QT],
                                         du_bf[:, :],
                                         brep[:, n2 * QT:(n2 + 1) * QT])
                        incV()
                    if n >= 4:
                        S.wait_ge(sVV, scan_vv[n - 4])
                    S.activation(out=dA[n % 4][:, :], in_=delta[:, :],
                                 func=AF.Exp,
                                 scale=aneg[:, dt * 16 + n:dt * 16 + n + 1])
                    incS()
                    dS()
                    V.wait_ge(sSS, c["ss"])
                    V.tensor_tensor_scan(
                        out=hb[:, nb * QT:(nb + 1) * QT],
                        data0=dA[n % 4][:, :],
                        data1=p_bf[:, nq4 * QT:(nq4 + 1) * QT],
                        initial=(0.0 if ch == 0 else
                                 hstate[:, dt * 16 + n:dt * 16 + n + 1]),
                        op0=ALU.mult, op1=ALU.add)
                    incV()
                    scan_vv[n] = c["vv"]
                    if nb == 7 and half == 0:
                        gV()
                        hb3 = hb[:, :].rearrange("p (n t) -> p n t", n=8)
                        V.tensor_copy(
                            hstate[:, dt * 16:dt * 16 + 8],
                            hb3[:, :, QT - 1])
                        V.tensor_mul(t2[:, :], hb[:, :],
                                     crep[:, 0:8 * QT])
                        incV()
                    if half == 1 and nb == 0:
                        V.tensor_add(t2[:, 0:4 * QT], t2[:, 0:4 * QT],
                                     t2[:, 4 * QT:8 * QT])
                    if half == 1 and nb == 3:
                        V.tensor_add(t2[:, 0:2 * QT], t2[:, 0:2 * QT],
                                     t2[:, 2 * QT:4 * QT])
                    if half == 1 and nb == 5:
                        V.tensor_add(y_f32[:, :], t2[:, 0:QT],
                                     t2[:, QT:2 * QT])
                    if nb == 7 and half == 1:
                        gV()
                        hb3 = hb[:, :].rearrange("p (n t) -> p n t", n=8)
                        V.tensor_copy(
                            hstate[:, dt * 16 + 8:dt * 16 + 16],
                            hb3[:, :, QT - 1])
                        V.tensor_mul(t2[:, :], hb[:, :],
                                     crep[:, 8 * QT:16 * QT])
                        gV()
                        V.tensor_add(t2[:, 0:4 * QT], t2[:, 0:4 * QT],
                                     t2[:, 4 * QT:8 * QT])
                        gV()
                        V.tensor_add(t2[:, 0:2 * QT], t2[:, 0:2 * QT],
                                     t2[:, 2 * QT:4 * QT])
                        gV()
                        V.tensor_add(ytmp[:, :], t2[:, 0:QT],
                                     t2[:, QT:2 * QT])
                        gV()
                        V.tensor_add(y_f32[:, :], y_f32[:, :], ytmp[:, :])
                        incV()
                prev_scan_all_vv = c["vv"]
                V.wait_ge(sVV, c["vv"])
                if DEBUG_A and ch == 0 and dt == 0:
                    dV()
                    GP.wait_ge(sVV, c["vv"])
                    dma(GP, "DBG", dbg["dbg_hb"], hb[:, :])
                    dma(GP, "DBG", dbg["dbg_y"], y_f32[:, :])
                    V.wait_ge(sem["DBG"], c["DBG"])
                V.scalar_tensor_tensor(
                    out=yg[:, :], in0=u[dt][:, :],
                    scalar=dh_all[:, dt:dt + 1], in1=y_f32[:, :],
                    op0=ALU.mult, op1=ALU.add)
                gV()
                V.tensor_mul(gated[dt][:, :], yg[:, :], sz[dt][:, :])
                incV()
                prev_gated_vv[dt] = c["vv"]
                if DEBUG_A and ch == 0 and dt == 0:
                    dV()
                    GP.wait_ge(sVV, c["vv"])
                    dma(GP, "DBG", dbg["dbg_gated"], gated[dt][:, :])
                    V.wait_ge(sem["DBG"], c["DBG"])
            prev_dtproj_mm = c["mm"]
            prev_inproj_mm = xproj_mm
            all_gated = c["vv"]
            for _ in range(6):
                if cast_jobs:
                    dst_, src_ = cast_jobs.pop(0)
                    shared["scr_tok"] = dma(GP, "STB", dst_, src_)

            for tt in range(4):
                for fh in range(2):
                    i = (tt * 2 + fh) % 2
                    wtok(T, pout_free[i])
                    if tt == 0 and fh == 0:
                        T.wait_ge(sVV, all_gated)
                    for k in range(8):
                        T.matmul(pout[i][:, :],
                                 gated[k][:, tt * 128:(tt + 1) * 128],
                                 outwbf[k][:, fh * QT:(fh + 1) * QT],
                                 start=(k == 0), stop=(k == 7))
                    incM()
                    omm = c["mm"]
                    wtok(S, oevac_free[i])
                    S.wait_ge(sMM, omm)
                    S.copy(oevac[i][:, :], pout[i][:, :])
                    incS()
                    pout_free[i] = (sSS, c["ss"])
                    dS()
                    GP.wait_ge(sSS, c["ss"])
                    oevac_free[i] = dma(
                        GP, "STP%d" % i,
                        partial[col0 + tt * 128:col0 + (tt + 1) * 128,
                                fh * QT:(fh + 1) * QT],
                        oevac[i][:, :])

        GP.wait_ge(sem["STP0"], c["STP0"])
        GP.wait_ge(sem["STP1"], c["STP1"])
        if shared is None:
            _emit_blocks(nc, prog)
    return nc



def build_B(shared=None):
    if shared is None:
        nc = bass.Bass("TRN2", target_bir_lowering=False,
                       detect_race_conditions=False)
        pa = nc.dram_tensor("pa", [2048, 1024], F32,
                            kind="ExternalInput").ap()
        pb = nc.dram_tensor("pb", [2048, 1024], F32,
                            kind="ExternalInput").ap()
    else:
        nc = shared["nc"]
        pa = shared["cc_out"]
        pb = None
    x_res = nc.dram_tensor("x_res", [2048, 1024], F32, kind="ExternalInput").ap()
    router_t = nc.dram_tensor("router_t", [128, 256], F32,
                              kind="ExternalInput").ap()
    rbias = nc.dram_tensor("rbias", [1, 32], F32, kind="ExternalInput").ap()
    if shared is not None:
        fd = shared["ffn_dram"]
        gate_t, up_t, down_f = fd["gate_t"], fd["up_t"], fd["down_f"]
    else:
        gate_t = nc.dram_tensor("gate_t", [1024, 4096], F32,
                                kind="ExternalInput").ap()
        up_t = nc.dram_tensor("up_t", [1024, 4096], F32,
                              kind="ExternalInput").ap()
        down_f = nc.dram_tensor("down_f", [4096, 1024], F32,
                                kind="ExternalInput").ap()
    sgate_t = nc.dram_tensor("sgate_t", [128, 4096], F32,
                             kind="ExternalInput").ap()
    sup_t = nc.dram_tensor("sup_t", [128, 4096], F32, kind="ExternalInput").ap()
    sdown_t = nc.dram_tensor("sdown_t", [128, 4096], F32,
                             kind="ExternalInput").ap()
    ident = nc.dram_tensor("ident", [128, 128], BF16, kind="ExternalInput").ap()

    out = nc.dram_tensor("out", [2048, 1024], F32, kind="ExternalOutput").ap()

    dbg = {}
    if DEBUG_B:
        for nm, shp in (("dbg_slw", [32, 128]), ("dbg_wbf", [128, 32]),
                        ("dbg_wt", [32, 512]), ("dbg_wr", [128, 512]),
                        ("dbg_ge", [128, 1024]), ("dbg_ub", [128, 512]),
                        ("dbg_xn", [512, 1024]), ("dbg_wsel", [512, 32]),
                        ("dbg_xnT", [128, 512]), ("dbg_h1", [128, 512]),
                        ("dbg_hs", [128, 512]), ("dbg_sg", [128, 512]),
                        ("dbg_res2", [128, 1024])):
            dbg[nm] = nc.dram_tensor(nm, shp, F32, kind="ExternalOutput").ap()
    if shared is not None:
        gbs, ubs, dbs = fd["gbs"], fd["ubs"], fd["dbs"]
    else:
        gbs = nc.dram_tensor("gbs", [1024, 4096], BF16).ap()
        ubs = nc.dram_tensor("ubs", [1024, 4096], BF16).ap()
        dbs = nc.dram_tensor("dbs", [4096, 1024], BF16).ap()
    wscr = nc.dram_tensor("wscr", [32, 512], BF16).ap()

    if shared is None:
        prog = {e: [] for e in ENGINES}
    else:
        prog = shared["prog"]
    SY, GP, T, V, S = (Rec(prog, e) for e in ENGINES)

    es = contextlib.ExitStack()
    sem_es = shared["es"] if shared is not None else es
    with es:
        def sb(name, shape, dt):
            return es.enter_context(nc.sbuf_tensor(name, shape, dt))

        identb = sb("identb", [128, 128], BF16)
        routerbf = sb("routerbf", [128, 256], BF16)
        rbrep = sb("rbrep", [128, 32], F32)
        sgatebf = sb("sgatebf", [128, 4096], BF16)
        supbf = sb("supbf", [128, 4096], BF16)
        sdownbf = sb("sdownbf", [128, 4096], BF16)

        s_pa = sb("s_pa", [128, 1024], F32)
        s_pb = sb("s_pb", [128, 1024], F32)
        s_xr = sb("s_xr", [128, 1024], F32)
        res = sb("res", [128, 1024], F32)
        xc = sb("xc", [128, 1024], F32)
        sqd = sb("sqd", [128, 1024], F32)
        st1 = sb("st1", [128, 1], F32)
        st2 = sb("st2", [128, 1], F32)
        epsb = sb("epsb", [128, 1], F32)
        halfneg = sb("halfneg", [128, 1], F32)
        xn_f32 = [sb(f"xn{i}", [128, 1024], F32) for i in range(4)]
        xn_bf = sb("xn_bf", [128, 1024], BF16)
        xnT = [sb(f"xnT{k}", [128, 512], BF16) for k in range(8)]
        sco = sb("sco", [128, 32], F32)
        scow = sb("scow", [128, 32], F32)
        msk = sb("msk", [128, 32], F32)
        ind = sb("ind", [128, 32], F32)
        wsel_bf = sb("wsel_bf", [128, 128], BF16)
        wselT = sb("wselT", [128, 512], BF16)
        ge = [sb(f"ge{i}", [128, 1024], BF16) for i in range(2)]
        ue = [sb(f"ue{i}", [128, 1024], BF16) for i in range(2)]
        sg2 = [sb(f"sg{i}", [128, 512], BF16) for i in range(2)]
        ub2 = [sb(f"ub{i}", [128, 512], BF16) for i in range(2)]
        wr = [sb(f"wr{i}", [128, 512], BF16) for i in range(2)]
        h1 = [sb(f"h1_{k}", [128, 512], BF16) for k in range(32)]
        dbf = [sb(f"dbf{i}", [128, 512], BF16) for i in range(2)]
        hs = [sb(f"hs{i}", [128, 512], BF16) for i in range(4)]
        hsg = sb("hsg", [128, 512], BF16)
        res2 = [sb(f"res2_{i}", [128, 1024], F32) for i in range(4)]
        if DEBUG_B:
            dbsb = sb("dbsb", [128, 1024], F32)
            dbw = sb("dbw", [128, 128], F32)
            dbw2 = sb("dbw2", [128, 512], F32)
            dbu = sb("dbu", [128, 512], F32)
        outb = [sb(f"outb{i}", [128, 1024], F32) for i in range(2)]

        ppAs = [es.enter_context(nc.psum_tensor(f"ppA{i}", [128, 512], F32))
                for i in range(2)]
        ppBs = [es.enter_context(nc.psum_tensor(f"ppB{i}", [128, 512], F32))
                for i in range(2)]
        ppA, ppB = ppAs[0], ppBs[0]
        ppd = [es.enter_context(nc.psum_tensor(f"ppd{i}", [128, 512], F32))
               for i in range(4)]

        sem = {}
        c = {}
        for name in ("LP", "LW", "LG0", "LG1", "LF0", "LF1",
                     "STW", "STS", "STO0", "STO1", "LT", "DBG"):
            sem[name] = sem_es.enter_context(nc.semaphore("sB" + name))
            c[name] = 0
        sMM = sem_es.enter_context(nc.semaphore("sBMM"))
        sVV = sem_es.enter_context(nc.semaphore("sBVV"))
        sSS = sem_es.enter_context(nc.semaphore("sBSS"))
        c.update(mm=0, vv=0, ss=0)

        def dma(eng, key, out_, in_):
            eng.dma_start(out=out_, in_=in_).then_inc(sem[key], 16)
            c[key] += 16
            return (sem[key], c[key])

        def dma_t(eng, key, out_, in_):
            eng.dma_start_transpose(out=out_, in_=in_).then_inc(sem[key], 16)
            c[key] += 16
            return (sem[key], c[key])

        def dma_t(eng, key, out_, in_):
            eng.dma_start_transpose(out=out_, in_=in_).then_inc(sem[key], 16)
            c[key] += 16
            return (sem[key], c[key])

        def incM():
            prog["tensor"][-1][3].append((sMM, 1))
            c["mm"] += 1

        def incV():
            prog["vector"][-1][3].append((sVV, 1))
            c["vv"] += 1

        def incS():
            prog["scalar"][-1][3].append((sSS, 1))
            c["ss"] += 1

        def gV():
            if not prog["vector"][-1][3]:
                incV()
            V.wait_ge(sVV, c["vv"])

        def dV():
            V.drain()
            V.sem_inc(sVV, 1)
            c["vv"] += 1

        def dS():
            S.drain()
            S.sem_inc(sSS, 1)
            c["ss"] += 1

        def gS():
            if not prog["scalar"][-1][3]:
                incS()
            S.wait_ge(sSS, c["ss"])

        def wtok(Rr, tok):
            if tok is not None:
                Rr.wait_ge(tok[0], tok[1])

        V.memset(epsb[:, :], 1e-5)
        V.memset(halfneg[:, :], -0.5)
        V.memset(wsel_bf[:, :], 0.0)
        incV()
        dma(SY, "LW", identb[:, :], ident)
        dma(SY, "LW", rbrep[:, :], rbias[0:1, :].broadcast_to([128, 32]))
        dma(GP, "STW", routerbf[:, :], router_t)
        dma(GP, "STW", sgatebf[:, :], sgate_t)
        dma(GP, "STW", supbf[:, :], sup_t)
        wcast_tok = dma(GP, "STW", sdownbf[:, :], sdown_t)
        if shared is not None:
            wcast_tok = shared["scr_tok"]
        else:
            for (src, dsc, ntile) in ((gate_t, gbs, 8), (up_t, ubs, 8),
                                      (down_f, dbs, 32)):
                for it in range(ntile):
                    wcast_tok = dma(GP, "STW",
                                    dsc[it * 128:(it + 1) * 128, :],
                                    src[it * 128:(it + 1) * 128, :])

        if shared is not None:
            ccs = shared["ccsem"]
            for Rr in (SY, GP, T, V, S):
                Rr.wait_ge(ccs, 1)
        ppA_frees = [None, None]
        ppB_frees = [None, None]
        xnt_read_tok = None
        ppd_free = [None] * 4
        ge_free = [None, None]
        ue_free = [None, None]
        wr_free = [None, None]
        dbf_free = [None, None]
        outb_free = [None, None]
        xnT_prev_mm = None
        prev_h1_vv = None

        spa_free = None
        wselT_read_tok = None
        for q in range(NQUARTER):
            row0 = q * 512
            wsel_tt_tok = None
            for tt in range(4):
                r0 = row0 + tt * 128
                wtok(SY, spa_free)
                dma(SY, "LP", s_pa[:, :], pa[r0:r0 + 128, :])
                if pb is not None:
                    dma(SY, "LP", s_pb[:, :], pb[r0:r0 + 128, :])
                tok = dma(SY, "LP", s_xr[:, :], x_res[r0:r0 + 128, :])
                V.wait_ge(*tok)
                if pb is not None:
                    V.tensor_add(res[:, :], s_pa[:, :], s_pb[:, :])
                    gV()
                    V.tensor_add(res[:, :], res[:, :], s_xr[:, :])
                else:
                    V.tensor_add(res[:, :], s_pa[:, :], s_xr[:, :])
                gV()
                V.reduce_sum(st1[:, :], res[:, :], axis=AX.X)
                gV()
                V.tensor_scalar_mul(st1[:, :], st1[:, :], 1.0 / 1024.0)
                gV()
                V.tensor_scalar(out=xc[:, :], in0=res[:, :],
                                scalar1=st1[:, 0:1], scalar2=None,
                                op0=ALU.subtract)
                incV()
                spa_free = (sVV, c["vv"])
                S.wait_ge(sVV, c["vv"])
                S.activation(out=sqd[:, :], in_=xc[:, :], func=AF.Square,
                             accum_out=st2[:, 0:1])
                gS()
                S.mul(st2[:, 0:1], st2[:, 0:1], 1.0 / 1024.0)
                gS()
                S.activation(out=st2[:, 0:1], in_=st2[:, 0:1], func=AF.Ln,
                             bias=epsb[:, 0:1])
                gS()
                S.activation(out=st2[:, 0:1], in_=st2[:, 0:1], func=AF.Exp,
                             scale=halfneg[:, 0:1])
                incS()
                V.wait_ge(sSS, c["ss"])
                if q > 0 and tt == 0:
                    V.wait_ge(sMM, xnT_prev_mm)
                V.tensor_scalar_mul(xn_f32[tt][:, :], xc[:, :], st2[:, 0:1])
                gV()
                V.tensor_copy(xn_bf[:, :], xn_f32[tt][:, :])
                incV()
                xn_vv = c["vv"]
                dV()
                SY.wait_ge(sVV, c["vv"])
                if xnt_read_tok is not None:
                    SY.wait_ge(*xnt_read_tok)
                for k in range(8):
                    tok = dma_t(SY, "LT",
                                xnT[k][:, tt * 128:(tt + 1) * 128],
                                xn_bf[:, k * 128:(k + 1) * 128])
                xnT_tt_tok = tok
                if SKIP_ROUTER:
                    V.memset(wselT[0:32, tt * 128:(tt + 1) * 128], 0.125)
                    incV()
                    wsel_tt_tok = (sVV, c["vv"])
                    continue
                wtok(T, ppA_frees[0])
                if q == 0 and tt == 0:
                    T.wait_ge(sem["STW"], c["STW"])
                T.wait_ge(*xnT_tt_tok)
                for k in range(8):
                    T.matmul(ppA[:, 0:32],
                             xnT[k][:, tt * 128:(tt + 1) * 128],
                             routerbf[:, k * 32:(k + 1) * 32],
                             start=(k == 0), stop=(k == 7))
                incM()
                V.wait_ge(sMM, c["mm"])
                V.tensor_add(sco[:, :], ppA[:, 0:32], rbrep[:, :])
                gV()
                ppA_frees[0] = (sVV, c["vv"])
                V.reduce_max(st1[:, :], sco[:, :], axis=AX.X)
                gV()
                V.tensor_scalar_mul(st1[:, :], st1[:, :], -1.0)
                incV()
                S.wait_ge(sVV, c["vv"])
                S.activation(out=sco[:, :], in_=sco[:, :], func=AF.Exp,
                             bias=st1[:, 0:1])
                incS()
                V.wait_ge(sSS, c["ss"])
                V.reduce_sum(st2[:, :], sco[:, :], axis=AX.X)
                gV()
                V.reciprocal(st1[:, :], st2[:, :])
                gV()
                V.tensor_scalar_mul(sco[:, :], sco[:, :], st1[:, 0:1])
                V.tensor_copy(scow[:, :], sco[:, :])
                V.memset(msk[:, :], 0.0)
                for _ in range(4):
                    gV()
                    V.reduce_max(st1[:, :], scow[:, :], axis=AX.X)
                    gV()
                    V.tensor_single_scalar(out=ind[:, :], in_=scow[:, :],
                                           scalar=st1[:, 0:1], op=ALU.is_ge)
                    gV()
                    V.tensor_add(msk[:, :], msk[:, :], ind[:, :])
                    V.scalar_tensor_tensor(
                        out=scow[:, :], in0=ind[:, :], scalar=-1e30,
                        in1=scow[:, :], op0=ALU.mult, op1=ALU.add)
                V.tensor_mul(sco[:, :], sco[:, :], msk[:, :])
                gV()
                V.reduce_sum(st1[:, :], sco[:, :], axis=AX.X)
                gV()
                V.tensor_scalar_max(st1[:, :], st1[:, :], 1e-12)
                gV()
                V.reciprocal(st2[:, :], st1[:, :])
                gV()
                V.tensor_scalar_mul(sco[:, :], sco[:, :], st2[:, 0:1])
                gV()
                V.tensor_copy(wsel_bf[:, 0:32], sco[:, :])
                incV()
                if DEBUG_B and q == 0:
                    V.tensor_copy(dbw[:, tt * 32:(tt + 1) * 32], sco[:, :])
                    incV()
                    GP.wait_ge(sVV, c["vv"])
                    dma(GP, "DBG", dbg["dbg_xn"][tt * 128:(tt + 1) * 128, :],
                        xn_f32[tt][:, :])
                    if tt == 3:
                        dma(GP, "DBG",
                            dbg["dbg_wsel"].rearrange("(a p) e -> p a e", a=4),
                            dbw[:, :].rearrange("p (a e) -> p a e", a=4))
                dV()
                SY.wait_ge(sVV, c["vv"])
                wtok(SY, wselT_read_tok)
                wsv_tok = dma_t(SY, "LT",
                                wselT[:, tt * 128:(tt + 1) * 128],
                                wsel_bf[:, :])
                if DEBUG_B and q == 0 and tt == 0:
                    V.tensor_copy(dbu[0:32, 0:128], sl)
                    V.tensor_copy(dbu[0:128, 128:160], wsel_bf[:, :])
                    incV()
                    GP.wait_ge(sVV, c["vv"])
                    dma(GP, "DBG", dbg["dbg_slw"], dbu[0:32, 0:128])
                    dma(GP, "DBG", dbg["dbg_wbf"], dbu[0:128, 128:160])
                wsel_tt_tok = wsv_tok
            all_xnT_tok = (sem["LT"], c["LT"])

            if DEBUG_B and q == 0:
                V.wait_ge(*wsel_tt_tok)
                V.tensor_copy(dbw2[0:32, :], wselT[0:32, :])
                incV()
                GP.wait_ge(sVV, c["vv"])
                dma(GP, "DBG", dbg["dbg_wt"], dbw2[0:32, :])
            GP.wait_ge(*wsel_tt_tok)
            if prev_h1_vv is not None:
                GP.wait_ge(sVV, prev_h1_vv)
            if not SKIP_WSCR:
                wscr_tok = dma(GP, "STS", wscr[:, :], wselT[0:32, :])
                SY.wait_ge(*wscr_tok)

            h1_vv = [None] * 32
            for e in range(0 if SKIP_MOE else NEXPERT):
                i = e % 2
                key = "LG%d" % i
                if e == 0 and q == 0:
                    SY.wait_ge(*wcast_tok)
                wtok(SY, ge_free[i])
                wtok(SY, ue_free[i])
                wtok(SY, wr_free[i])
                dma(SY, key, ge[i][:, :].rearrange("p (k j) -> p k j", k=8),
                    gbs[:, e * 128:(e + 1) * 128].rearrange(
                        "(k p) j -> p k j", k=8))
                dma(SY, key, ue[i][:, :].rearrange("p (k j) -> p k j", k=8),
                    ubs[:, e * 128:(e + 1) * 128].rearrange(
                        "(k p) j -> p k j", k=8))
                if SKIP_WR:
                    V.memset(wr[i][:, :], 0.5)
                    incV()
                    grp_tok = (sem[key], c[key])
                else:
                    grp_tok = dma(SY, key, wr[i][:, :],
                                  wscr[e:e + 1, :].broadcast_to([128, 512]))
                if e == 0 and q == 0:
                    T.wait_ge(*wcast_tok)
                pa_ = ppAs[i]
                pb_ = ppBs[i]
                wtok(T, ppA_frees[i])
                T.wait_ge(*grp_tok)
                if e == 0:
                    T.wait_ge(*all_xnT_tok)
                for k in range(8):
                    T.matmul(pa_[:, :], ge[i][:, k * 128:(k + 1) * 128],
                             xnT[k][:, :], start=(k == 0), stop=(k == 7))
                incM()
                ge_free[i] = (sMM, c["mm"])
                wtok(T, ppB_frees[i])
                for k in range(8):
                    T.matmul(pb_[:, :], ue[i][:, k * 128:(k + 1) * 128],
                             xnT[k][:, :], start=(k == 0), stop=(k == 7))
                incM()
                ue_free[i] = (sMM, c["mm"])
                S.wait_ge(sMM, c["mm"] - 1)
                S.activation(out=sg2[i][:, :], in_=pa_[:, :], func=AF.Silu)
                incS()
                ppA_frees[i] = (sSS, c["ss"])
                S.wait_ge(sMM, c["mm"])
                S.copy(ub2[i][:, :], pb_[:, :])
                incS()
                dS()
                ppB_frees[i] = (sSS, c["ss"])
                ub_ss = c["ss"]
                V.wait_ge(sSS, ub_ss)
                V.wait_ge(*grp_tok)
                V.tensor_mul(h1[e][:, :], sg2[i][:, :], ub2[i][:, :])
                gV()
                V.tensor_mul(h1[e][:, :], h1[e][:, :], wr[i][:, :])
                incV()
                h1_vv[e] = c["vv"]
                wr_free[i] = (sVV, c["vv"])
                if DEBUG_B and q == 0 and e == 0:
                    V.tensor_copy(dbsb[:, :], ge[0][:, :])
                    V.tensor_copy(dbw2[:, :], wr[0][:, :])
                    V.tensor_copy(dbu[:, :], ub[:, :])
                    incV()
                    GP.wait_ge(sVV, c["vv"])
                    dma(GP, "DBG", dbg["dbg_h1"], h1[0][:, :])
                    dma(GP, "DBG", dbg["dbg_sg"], sg[:, :])
                    dma(GP, "DBG", dbg["dbg_xnT"], xnT[0][:, :])
                    dma(GP, "DBG", dbg["dbg_ge"], dbsb[:, :])
                    dma(GP, "DBG", dbg["dbg_wr"], dbw2[:, :])
                    dma(GP, "DBG", dbg["dbg_ub"], dbu[:, :])
                    dbg_tok = (sem["DBG"], c["DBG"])
                else:
                    dbg_tok = None
            prev_h1_vv = c["vv"] if not SKIP_MOE else None

            for st_ in ([] if SKIP_SHARED else range(4)):
                wtok(T, ppA_frees[0])
                if q == 0 and st_ == 0:
                    T.wait_ge(sem["STW"], c["STW"])
                for k in range(8):
                    T.matmul(ppA[:, :],
                             sgatebf[:, k * 512 + st_ * 128:
                                     k * 512 + (st_ + 1) * 128],
                             xnT[k][:, :], start=(k == 0), stop=(k == 7))
                incM()
                S.wait_ge(sMM, c["mm"])
                S.activation(out=hsg[:, :], in_=ppA[:, :], func=AF.Silu)
                incS()
                ppA_frees[0] = (sSS, c["ss"])
                wtok(T, ppB_frees[0])
                for k in range(8):
                    T.matmul(ppB[:, :],
                             supbf[:, k * 512 + st_ * 128:
                                   k * 512 + (st_ + 1) * 128],
                             xnT[k][:, :], start=(k == 0), stop=(k == 7))
                incM()
                S.wait_ge(sMM, c["mm"])
                S.copy(ub2[0][:, :], ppB[:, :])
                incS()
                dS()
                ppB_frees[0] = (sSS, c["ss"])
                V.wait_ge(sSS, c["ss"])
                V.tensor_mul(hs[st_][:, :], hsg[:, :], ub2[0][:, :])
                incV()
            hs_vv = c["vv"]
            if SKIP_SHARED:
                for st_ in range(4):
                    V.memset(hs[st_][:, :], 0.01)
                incV()
                hs_vv = c["vv"]
            if DEBUG_B and q == 0:
                GP.wait_ge(sVV, hs_vv)
                dma(GP, "DBG", dbg["dbg_hs"], hs[0][:, :])

            if SKIP_DOWN:
                for tt in range(4):
                    V.memset(res2[tt][:, :], 0.1)
                    incV()
                    ppd_free[tt] = (sVV, c["vv"])
                down_mm = c["mm"]
            for fh in ([] if SKIP_DOWN else range(2)):
                for k in list(range(NEXPERT)) + list(range(32, 36)):
                    if k < 32:
                        i = k % 2
                        if q == 0 and fh == 0 and k == 0:
                            SY.wait_ge(*wcast_tok)
                        wtok(SY, dbf_free[i])
                        ftok = dma(SY, "LF%d" % i, dbf[i][:, :],
                                   dbs[k * 128:(k + 1) * 128,
                                       fh * 512:(fh + 1) * 512])
                        T.wait_ge(*ftok)
                        if fh == 0 and k == 0:
                            T.wait_ge(sVV, hs_vv)
                        lhs_full = h1[k]
                        rhs = dbf[i][:, :]
                    else:
                        lhs_full = hs[k - 32]
                        rhs = sdownbf[:, (k - 32) * 1024 + fh * 512:
                                      (k - 32) * 1024 + (fh + 1) * 512]
                    for tt in range(4):
                        if k == 0:
                            wtok(T, ppd_free[tt])
                        T.matmul(ppd[tt][:, :],
                                 lhs_full[:, tt * 128:(tt + 1) * 128],
                                 rhs, start=(k == 0), stop=(k == 35))
                    if k < 32:
                        incM()
                        dbf_free[i] = (sMM, c["mm"])
                incM()
                down_mm = c["mm"]
                for tt in range(4):
                    V.wait_ge(sMM, down_mm)
                    V.scalar_tensor_tensor(
                        out=res2[tt][:, fh * 512:(fh + 1) * 512],
                        in0=ppd[tt][:, :], scalar=1.0,
                        in1=xn_f32[tt][:, fh * 512:(fh + 1) * 512],
                        op0=ALU.mult, op1=ALU.add)
                    incV()
                    ppd_free[tt] = (sVV, c["vv"])
            xnT_prev_mm = down_mm
            xnt_read_tok = (sMM, down_mm)
            wselT_read_tok = wscr_tok if not SKIP_WSCR else wsel_tt_tok
            if DEBUG_B and q == 0:
                GP.wait_ge(sVV, c["vv"])
                dma(GP, "DBG", dbg["dbg_res2"], res2[0][:, :])

            for tt in range(4):
                o = tt % 2
                V.reduce_sum(st1[:, :], res2[tt][:, :], axis=AX.X)
                gV()
                V.tensor_scalar_mul(st1[:, :], st1[:, :], 1.0 / 1024.0)
                gV()
                V.tensor_scalar(out=xc[:, :], in0=res2[tt][:, :],
                                scalar1=st1[:, 0:1], scalar2=None,
                                op0=ALU.subtract)
                incV()
                S.wait_ge(sVV, c["vv"])
                S.activation(out=sqd[:, :], in_=xc[:, :], func=AF.Square,
                             accum_out=st2[:, 0:1])
                gS()
                S.mul(st2[:, 0:1], st2[:, 0:1], 1.0 / 1024.0)
                gS()
                S.activation(out=st2[:, 0:1], in_=st2[:, 0:1], func=AF.Ln,
                             bias=epsb[:, 0:1])
                gS()
                S.activation(out=st2[:, 0:1], in_=st2[:, 0:1], func=AF.Exp,
                             scale=halfneg[:, 0:1])
                incS()
                V.wait_ge(sSS, c["ss"])
                wtok(V, outb_free[o])
                V.tensor_scalar_mul(outb[o][:, :], xc[:, :], st2[:, 0:1])
                incV()
                dV()
                GP.wait_ge(sVV, c["vv"])
                outb_free[o] = dma(
                    GP, "STO%d" % o,
                    out[row0 + tt * 128:row0 + (tt + 1) * 128, :],
                    outb[o][:, :])

        GP.wait_ge(sem["STO0"], c["STO0"])
        GP.wait_ge(sem["STO1"], c["STO1"])
        if shared is None:
            _emit_blocks(nc, prog)
    return nc


def build_AB():
    nc = bass.Bass("TRN2", target_bir_lowering=False,
                   detect_race_conditions=False)
    cc_in = nc.dram_tensor("cc_in", [4096, 1024], F32).ap()
    cc_out = nc.dram_tensor("cc_out", [2048, 1024], F32).ap()
    gate_t = nc.dram_tensor("gate_t", [1024, 4096], F32,
                            kind="ExternalInput").ap()
    up_t = nc.dram_tensor("up_t", [1024, 4096], F32,
                          kind="ExternalInput").ap()
    down_f = nc.dram_tensor("down_f", [4096, 1024], F32,
                            kind="ExternalInput").ap()
    gbs = nc.dram_tensor("gbs", [1024, 4096], BF16).ap()
    ubs = nc.dram_tensor("ubs", [1024, 4096], BF16).ap()
    dbs = nc.dram_tensor("dbs", [4096, 1024], BF16).ap()
    cast_jobs = []
    for (src, dsc, ntile) in ((gate_t, gbs, 8), (up_t, ubs, 8),
                              (down_f, dbs, 32)):
        for it in range(ntile):
            cast_jobs.append((dsc[it * 128:(it + 1) * 128, :],
                              src[it * 128:(it + 1) * 128, :]))
    prog = {e: [] for e in ENGINES}
    es = contextlib.ExitStack()
    with es:
        ccsem = es.enter_context(nc.semaphore("sCC"))
        shared = dict(nc=nc, prog=prog, es=es, cc_in=cc_in, cc_out=cc_out,
                      ccsem=ccsem, cast_jobs=cast_jobs, scr_tok=None,
                      ffn_dram=dict(gate_t=gate_t, up_t=up_t, down_f=down_f,
                                    gbs=gbs, ubs=ubs, dbs=dbs))
        build_A(shared)
        assert not cast_jobs
        GP = Rec(prog, "gpsimd")
        GP.collective_compute(
            "ReduceScatter", ALU.add,
            replica_groups=[[0, 1], [2, 3], [4, 5], [6, 7]],
            ins=[cc_in.opt()], outs=[cc_out.opt()]).then_inc(ccsem, 1)
        build_B(shared)
        _emit_blocks(nc, prog)
    return nc



def _tile_rows(a, k):
    a = np.asarray(a)
    return np.ascontiguousarray(
        a.reshape(k, 128, -1).transpose(1, 0, 2).reshape(128, -1))


def kernel(L0_attn_x, L0_attn_in_proj_w, L0_attn_conv_w, L0_attn_conv_b,
           L0_attn_x_proj_w, L0_attn_dt_proj_w, L0_attn_dt_proj_b,
           L0_attn_A_log, L0_attn_D, L0_attn_out_proj_w, L0_ffn_router_w,
           L0_ffn_router_bias, L0_ffn_gate_w, L0_ffn_up_w, L0_ffn_down_w,
           L0_ffn_sgate_w, L0_ffn_sup_w, L0_ffn_sdown_w):
    f = lambda a: np.ascontiguousarray(np.asarray(a, dtype=np.float32))
    x = f(L0_attn_x)
    w_in = f(L0_attn_in_proj_w)
    conv_w = f(L0_attn_conv_w)
    conv_b = f(L0_attn_conv_b)
    x_proj = f(L0_attn_x_proj_w)
    dt_w = f(L0_attn_dt_proj_w)
    dt_b = f(L0_attn_dt_proj_b)
    a_log = f(L0_attn_A_log)
    d_vec = f(L0_attn_D)
    out_w = f(L0_attn_out_proj_w)

    fused = True
    ncA = build_AB() if fused else build_A()
    mapsA = []
    for core in range(8):
        b, h = divmod(core, 2)
        perm = np.concatenate([np.arange(h * 1024, (h + 1) * 1024),
                               np.arange((1 - h) * 1024, (1 - h) * 1024 + 1024)])
        m = {
            "x_fm": np.ascontiguousarray(x[b].T),
            "w_in_t": np.ascontiguousarray(
                np.concatenate([w_in[0:2048][perm],
                                w_in[2048 + h * 1024:2048 + (h + 1) * 1024]],
                               axis=0).T),
            "conv_w": _tile_rows(conv_w[perm], 16),
            "conv_b": _tile_rows(conv_b[perm][:, None], 16),
            "xproj_t": _tile_rows(x_proj.T[perm], 16),
            "dtw_t": np.ascontiguousarray(
                dt_w[h * 1024:(h + 1) * 1024].T),
            "dt_b": _tile_rows(dt_b[h * 1024:(h + 1) * 1024][:, None], 8),
            "a_log": _tile_rows(a_log[h * 1024:(h + 1) * 1024], 8),
            "d_h": _tile_rows(d_vec[h * 1024:(h + 1) * 1024][:, None], 8),
            "outw_t": np.ascontiguousarray(
                out_w[:, h * 1024:(h + 1) * 1024].T),
        }
        mapsA.append(m)

    router_w = f(L0_ffn_router_w)
    router_b = f(L0_ffn_router_bias)
    gate_w = f(L0_ffn_gate_w)
    up_w = f(L0_ffn_up_w)
    down_w = f(L0_ffn_down_w)
    sgate_w = f(L0_ffn_sgate_w)
    sup_w = f(L0_ffn_sup_w)
    sdown_w = f(L0_ffn_sdown_w)

    gate_t = np.ascontiguousarray(gate_w.reshape(4096, 1024).T)
    up_t = np.ascontiguousarray(up_w.reshape(4096, 1024).T)
    down_flat = np.ascontiguousarray(
        down_w.transpose(0, 2, 1).reshape(4096, 1024))
    shared_w = {
        "router_t": _tile_rows(router_w.T, 8),
        "rbias": np.ascontiguousarray(router_b[None, :]),
        "gate_t": gate_t,
        "up_t": up_t,
        "down_f": down_flat,
        "sgate_t": _tile_rows(sgate_w.T, 8),
        "sup_t": _tile_rows(sup_w.T, 8),
        "sdown_t": _tile_rows(sdown_w.T, 4),
        "ident": np.eye(128, dtype=np.float32).astype(ml_dtypes.bfloat16),
    }
    maps = []
    for core in range(8):
        b, h = divmod(core, 2)
        m = dict(shared_w)
        m.update(mapsA[core])
        m["x_res"] = np.ascontiguousarray(x[b, h * 2048:(h + 1) * 2048])
        maps.append(m)
    res = run_bass_kernel_spmd(ncA, maps, core_ids=list(range(8)))

    out = np.empty((4, 4096, 1024), dtype=np.float32)
    for core in range(8):
        b, h = divmod(core, 2)
        out[b, h * 2048:(h + 1) * 2048] = np.asarray(res.results[core]["out"])
    return out
